# version 24
# baseline (speedup 1.0000x reference)
import contextlib
import numpy as np
import ml_dtypes
import concourse.bass as bass
import concourse.mybir as mybir
from concourse.bass_utils import run_bass_kernel_spmd

F32 = mybir.dt.float32
BF16 = mybir.dt.bfloat16
AF = mybir.ActivationFunctionType
ALU = mybir.AluOpType
AX = mybir.AxisListType.X

D = 2048
HEADS = 16
S_CORE = 2048
HALO = 512
SK = S_CORE + HALO
NCH_K = 5
QC = 256
NQC = S_CORE // QC
HID = 8192
NHT = 64
MAGIC = 12582912.0
LN_EPS = 1e-5
RMS_EPS = float(np.finfo(np.float32).eps)
SCALE_QK = 1.0 / float(np.sqrt(128.0))
NEG = -1e30
NCORES = 8


def _perm_eo():
    perm = np.zeros(D, dtype=np.int64)
    for g in range(8):
        b = 256 * g
        perm[b:b + 64] = (2 * g) * 128 + 2 * np.arange(64)
        perm[b + 64:b + 128] = (2 * g + 1) * 128 + 2 * np.arange(64)
        perm[b + 128:b + 192] = (2 * g) * 128 + 2 * np.arange(64) + 1
        perm[b + 192:b + 256] = (2 * g + 1) * 128 + 2 * np.arange(64) + 1
    return perm


def _slabs_from_T(WT):
    return np.ascontiguousarray(
        WT.reshape(16, 128, 16, 128).transpose(2, 1, 0, 3).reshape(16, 128, 2048))


def _host_prep(L0_attn_x, L0_attn_Wq, L0_attn_Wk, L0_attn_Wv, L0_attn_Wo,
               L0_ffn_W_fc, L0_ffn_W_proj):
    X = np.asarray(L0_attn_x, dtype=np.float32)
    perm = _perm_eo()
    slab_q = _slabs_from_T(np.ascontiguousarray(np.asarray(L0_attn_Wq, np.float32).T)[:, perm])
    slab_k = _slabs_from_T(np.ascontiguousarray(np.asarray(L0_attn_Wk, np.float32).T)[:, perm])
    slab_v = np.ascontiguousarray(np.asarray(L0_attn_Wv, np.float32).T.reshape(16, 128, 2048))
    slab_o = _slabs_from_T(np.ascontiguousarray(np.asarray(L0_attn_Wo, np.float32).T))
    parts = []
    for c in range(8):
        parts += [slab_k[2 * c:2 * c + 2].reshape(-1, 2048),
                  slab_v[2 * c:2 * c + 2].reshape(-1, 2048),
                  slab_q[2 * c:2 * c + 2].reshape(-1, 2048),
                  slab_o[2 * c:2 * c + 2].reshape(-1, 2048)]
    attn_flat = np.concatenate(parts, axis=0)
    Wfc = np.asarray(L0_ffn_W_fc, np.float32)
    Wpr = np.asarray(L0_ffn_W_proj, np.float32)
    inv = 10000.0 ** (-2.0 * np.arange(64) / 128.0)

    in_maps = []
    for c in range(NCORES):
        b, h = c // 2, c % 2
        start = h * S_CORE
        xm = np.ascontiguousarray(X[b, start:start + S_CORE])
        xhalo = np.ascontiguousarray(X[b, start - HALO:start] if h == 1 else X[b, 0:HALO])
        pos = (start - HALO) + np.arange(SK)
        th = pos[None, :] * inv[:, None]
        c2 = np.concatenate([np.cos(th), np.cos(th)], axis=0).astype(np.float32)
        s2 = np.concatenate([np.sin(th), np.sin(th)], axis=0).astype(np.float32)
        maskcol = np.full((1, 128), 0.0 if h == 1 else NEG, dtype=np.float32)
        in_maps.append({
            "identin": np.eye(128, dtype=np.float32),
            "xm": xm, "xh": xhalo,
            "attnw": np.ascontiguousarray(attn_flat[c * 1024:(c + 1) * 1024]),
            "fcw": np.ascontiguousarray(Wfc[c * 1024:(c + 1) * 1024]),
            "prw": np.ascontiguousarray(Wpr[c * 256:(c + 1) * 256]),
            "c2": c2, "s2": s2, "maskcol": maskcol,
        })
    return in_maps


class Ctr:
    def __init__(self, stack, nc, name):
        self.h = stack.enter_context(nc.semaphore(name))
        self.n = 0

    def inc(self, inst, k=1):
        inst.then_inc(self.h, k)
        self.n += k
        return self.n


def _wallA1_k_rows(i):
    return (i // 2) * 512 + (i % 2) * 128


def _wallA1_v_rows(i):
    return (i // 2) * 512 + 256 + (i % 2) * 128


def _wallA2_q_rows(i):
    return (i // 2) * 512 + (i % 2) * 128


def _wallA2_o_rows(i):
    return (i // 2) * 512 + 256 + (i % 2) * 128


def _wallB_fc_rows(ht):
    return (ht // 8) * 2048 + (ht % 8) * 128


def _wallB_pr_rows(do):
    return (do // 2) * 2048 + 1024 + (do % 2) * 512


def build_nc(debug=()):
    nc = bass.Bass("TRN2", target_bir_lowering=False)

    xm = nc.dram_tensor("xm", [S_CORE, D], F32, kind="ExternalInput").ap()
    xhalo = nc.dram_tensor("xh", [HALO, D], F32, kind="ExternalInput").ap()
    attnw = nc.dram_tensor("attnw", [1024, D], F32, kind="ExternalInput").ap()
    fcw_in = nc.dram_tensor("fcw", [1024, D], F32, kind="ExternalInput").ap()
    prw_in = nc.dram_tensor("prw", [256, HID], F32, kind="ExternalInput").ap()
    c2_in = nc.dram_tensor("c2", [128, SK], F32, kind="ExternalInput").ap()
    s2_in = nc.dram_tensor("s2", [128, SK], F32, kind="ExternalInput").ap()
    maskcol_in = nc.dram_tensor("maskcol", [1, 128], F32, kind="ExternalInput").ap()
    ident_in = nc.dram_tensor("identin", [128, 128], F32, kind="ExternalInput").ap()
    y = nc.dram_tensor("y", [S_CORE, D], F32, kind="ExternalOutput").ap()

    def internal(name, shape, dtype):
        kind = "ExternalOutput" if name in debug else "Internal"
        return nc.dram_tensor(name, shape, dtype, kind=kind).ap()

    wsh = internal("wsh", [3072, D], BF16)
    wallA1 = internal("wallA1", [4096, D], BF16)
    wallA2 = internal("wallA2", [4096, D], BF16)
    wallB = internal("wallB", [16384, D], BF16)
    qT_d = internal("qT_d", [16, 128, S_CORE], BF16)
    kT_d = internal("kT_d", [16, 128, SK], BF16)
    v_d = internal("v_d", [SK, D], BF16)
    x1_d = internal("x1_d", [S_CORE, D], F32)

    st = contextlib.ExitStack()
    with st:
        ident = st.enter_context(nc.sbuf_tensor("ident", [128, 128], BF16))
        ones128 = st.enter_context(nc.sbuf_tensor("ones128", [128, 128], BF16))
        onesrow = st.enter_context(nc.sbuf_tensor("onesrow", [1, QC], BF16))
        maskc_sb = st.enter_context(nc.sbuf_tensor("maskc_sb", [1, 128], BF16))
        eps_ln = st.enter_context(nc.sbuf_tensor("eps_ln", [128, 1], F32))
        eps_rms = st.enter_context(nc.sbuf_tensor("eps_rms", [128, 1], F32))
        sv = st.enter_context(nc.sbuf_tensor("sv", [128, 4], F32))
        ssq = st.enter_context(nc.sbuf_tensor("ssq", [128, 4], F32))
        sv3 = st.enter_context(nc.sbuf_tensor("sv3", [128, 16], F32))
        ssq3 = st.enter_context(nc.sbuf_tensor("ssq3", [128, 16], F32))
        m3 = st.enter_context(nc.sbuf_tensor("m3", [128, 16], F32))
        r3 = st.enter_context(nc.sbuf_tensor("r3", [128, 16], F32))
        mvec = st.enter_context(nc.sbuf_tensor("mvec", [128, 4], F32))
        vvec = st.enter_context(nc.sbuf_tensor("vvec", [128, 4], F32))
        rstd = st.enter_context(nc.sbuf_tensor("rstd", [128, 4], F32))
        sqscr = st.enter_context(nc.sbuf_tensor("sqscr", [128, D], BF16))

        PEc = Ctr(st, nc, "sPE")
        DV = Ctr(st, nc, "sDV")
        AC = Ctr(st, nc, "sAC")
        PO = Ctr(st, nc, "sPO")
        CC = Ctr(st, nc, "sCC")
        dsem = {n: Ctr(st, nc, "d_" + n) for n in (
            "xa", "xb", "wv", "tab", "wq0", "wq1",
            "q0", "q1", "q2", "q3", "vs0", "vs1",
            "cast", "wb", "tev",
            "p2", "wo", "xr", "x1s",
            "xc", "f0", "f1", "pr0", "pr1", "y0", "y1", "p2b", "p2c", "wb2",
            "im", "p2d", "p2e", "p2f")}

        block = st.enter_context(nc.Block())

        ln_state = {"slot_busy": [(0, 0)] * 4, "next": 0}

        def emit_ln_stats(xs, pre_waits):
            k = len(xs)
            slots = [(ln_state["next"] + i) % 4 for i in range(k)]
            ln_state["next"] = (ln_state["next"] + k) % 4
            busys = [ln_state["slot_busy"][s] for s in slots]

            @block.vector
            def _(v):
                for (ct, tok) in pre_waits:
                    v.wait_ge(ct.h, tok)
                for (b_dv, b_ac) in busys:
                    v.wait_ge(DV.h, b_dv)
                    v.wait_ge(AC.h, b_ac)
                for x_ap, s_ in zip(xs, slots):
                    DV.inc(v.tensor_reduce(out=sv[:, s_:s_ + 1], in_=x_ap,
                                           axis=AX, op=ALU.add))
            rtok = DV.n

            @block.scalar
            def _(s):
                for (ct, tok) in pre_waits:
                    s.wait_ge(ct.h, tok)
                for (b_dv, b_ac) in busys:
                    s.wait_ge(DV.h, b_dv)
                for x_ap, s_ in zip(xs, slots):
                    s.wait_ge(AC.h, AC.n)
                    AC.inc(s.activation(out=sqscr[:], in_=x_ap, func=AF.Square,
                                        accum_out=ssq[:, s_:s_ + 1]))
            sqtok = AC.n

            @block.vector
            def _(v):
                v.wait_ge(DV.h, rtok)
                v.wait_ge(AC.h, sqtok)
                for s_ in slots:
                    DV.inc(v.tensor_scalar(out=mvec[:, s_:s_ + 1], in0=sv[:, s_:s_ + 1],
                                           scalar1=1.0 / D, scalar2=None, op0=ALU.mult))
                    DV.inc(v.tensor_scalar(out=vvec[:, s_:s_ + 1], in0=ssq[:, s_:s_ + 1],
                                           scalar1=1.0 / D, scalar2=None, op0=ALU.mult))
                v.wait_ge(DV.h, DV.n)
                for s_ in slots:
                    DV.inc(v.tensor_tensor(out=rstd[:, s_:s_ + 1], in0=mvec[:, s_:s_ + 1],
                                           in1=mvec[:, s_:s_ + 1], op=ALU.mult))
                v.wait_ge(DV.h, DV.n)
                for s_ in slots:
                    DV.inc(v.tensor_tensor(out=vvec[:, s_:s_ + 1], in0=vvec[:, s_:s_ + 1],
                                           in1=rstd[:, s_:s_ + 1], op=ALU.subtract))
            vartok = DV.n

            @block.scalar
            def _(s):
                s.wait_ge(DV.h, vartok)
                for s_ in slots:
                    AC.inc(s.activation(out=rstd[:, s_:s_ + 1], in_=vvec[:, s_:s_ + 1],
                                        func=AF.Ln, bias=eps_ln[:]))
                s.wait_ge(AC.h, AC.n)
                for s_ in slots:
                    AC.inc(s.activation(out=rstd[:, s_:s_ + 1], in_=rstd[:, s_:s_ + 1],
                                        func=AF.Exp, scale=-0.5))
            rstok = AC.n
            return [(s_, rstok) for s_ in slots]

        def emit_ln_apply(x_ap, slot_tok, out_ap, apply_waits=()):
            slot, rstok = slot_tok

            @block.vector
            def _(v):
                v.wait_ge(AC.h, rstok)
                for (ct, tok) in apply_waits:
                    v.wait_ge(ct.h, tok)
                DV.inc(v.tensor_scalar(out=out_ap, in0=x_ap,
                                       scalar1=mvec[:, slot:slot + 1],
                                       scalar2=rstd[:, slot:slot + 1],
                                       op0=ALU.subtract, op1=ALU.mult))
            ln_state["slot_busy"][slot] = (DV.n, rstok)
            return DV.n

        def emit_ln(x_ap, pre_waits, out_ap, apply_waits=()):
            st_ = emit_ln_stats([x_ap], pre_waits)[0]
            return emit_ln_apply(x_ap, st_, out_ap, apply_waits)

        @block.gpsimd
        def _(g):
            g.dma_start(out=ident[:], in_=ident_in).then_inc(dsem["im"].h, 16)
            dsem["im"].n += 16
            PO.inc(g.memset(ones128[:], 1.0))
            PO.inc(g.memset(onesrow[:], 1.0))
            PO.inc(g.memset(eps_ln[:], LN_EPS))
            PO.inc(g.memset(eps_rms[:], RMS_EPS))
            g.dma_start(out=wsh[0:1024, :], in_=attnw).then_inc(dsem["cast"].h, 16)
            dsem["cast"].n += 16
            g.dma_start(out=maskc_sb[:], in_=maskcol_in).then_inc(dsem["im"].h, 16)
            dsem["im"].n += 16
        consts_tok = PO.n
        ident_tok = 32
        cast_tok = dsem["cast"].n

        @block.gpsimd
        def _(g):
            g.wait_ge(dsem["cast"].h, cast_tok)
            CC.inc(g.collective_compute(
                "AllGather", ALU.bypass, replica_groups=[list(range(NCORES))],
                ins=[wsh[0:512, :].opt()], outs=[wallA1.opt()]))
        ccA = CC.n

        @block.gpsimd
        def _(g):
            g.wait_ge(dsem["cast"].h, cast_tok)
            CC.inc(g.collective_compute(
                "AllGather", ALU.bypass, replica_groups=[list(range(NCORES))],
                ins=[wsh[512:1024, :].opt()], outs=[wallA2.opt()]))
        ccA2 = CC.n

        with contextlib.ExitStack() as p0:
            wbb = [p0.enter_context(nc.sbuf_tensor(f"q_wb{i}", [128, HID], BF16))
                   for i in range(2)]
            gsum = p0.enter_context(nc.sbuf_tensor("q_gs", [128, 128], F32))
            scb = p0.enter_context(nc.sbuf_tensor("q_sc", [128, 128], BF16))
            rr = p0.enter_context(nc.sbuf_tensor("q_rr", [128, 128], F32))
            qs_sb = p0.enter_context(nc.sbuf_tensor("q_qs", [128, HID], BF16))
            tev = p0.enter_context(nc.sbuf_tensor("q_tev", [128, HID], BF16))
            qp = p0.enter_context(nc.psum_tensor("q_qp", [128, 2048], BF16))

            q_pe_busy = [0]
            tev_busy = [0]
            q_toks = []

            def quant_tile(src_ap, ncols, dst_row0, is_proj):
                ng = ncols // 64
                ti = len(q_toks)
                wb = wbb[ti % 2]
                wsem = dsem["wb"] if ti % 2 == 0 else dsem["wb2"]
                wb_guard = q_toks[ti - 2] if ti >= 2 else 0

                @block.gpsimd
                def _(g):
                    g.wait_ge(DV.h, wb_guard)
                    g.dma_start(out=wb[:, 0:ncols], in_=src_ap).then_inc(wsem.h, 16)
                    wsem.n += 16
                ld = wsem.n

                @block.vector
                def _(v):
                    v.wait_ge(wsem.h, ld)
                    v.wait_ge(PEc.h, q_pe_busy[0])
                    DV.inc(v.tensor_reduce(
                        out=gsum[:, 0:ng],
                        in_=wb[:, 0:ncols].rearrange("p (g k) -> p g k", k=64),
                        axis=AX, op=ALU.add, apply_absolute_value=True))
                    v.wait_ge(DV.h, DV.n)
                    DV.inc(v.tensor_scalar(out=scb[:, 0:ng], in0=gsum[:, 0:ng],
                                           scalar1=1.0 / 64.0, scalar2=1e-8,
                                           op0=ALU.mult, op1=ALU.max))
                    v.wait_ge(DV.h, DV.n)
                    DV.inc(v.reciprocal(out=rr[:, 0:ng], in_=scb[:, 0:ng]))
                    v.wait_ge(DV.h, DV.n)
                    DV.inc(v.tensor_tensor(
                        out=wb[:, 0:ncols].rearrange("p (g k) -> p g k", k=64),
                        in0=wb[:, 0:ncols].rearrange("p (g k) -> p g k", k=64),
                        in1=rr[:, 0:ng].unsqueeze(2).broadcast_to([128, ng, 64]),
                        op=ALU.mult))
                    v.wait_ge(DV.h, DV.n)
                    DV.inc(v.tensor_scalar(out=wb[:, 0:ncols], in0=wb[:, 0:ncols],
                                           scalar1=MAGIC, scalar2=MAGIC,
                                           op0=ALU.add, op1=ALU.subtract))
                    v.wait_ge(DV.h, DV.n)
                    DV.inc(v.tensor_scalar(out=wb[:, 0:ncols], in0=wb[:, 0:ncols],
                                           scalar1=1.0, scalar2=-1.0,
                                           op0=ALU.min, op1=ALU.max))
                    v.wait_ge(DV.h, DV.n)
                    DV.inc(v.tensor_tensor(
                        out=qs_sb[:, 0:ncols].rearrange("p (g k) -> p g k", k=64),
                        in0=wb[:, 0:ncols].rearrange("p (g k) -> p g k", k=64),
                        in1=scb[:, 0:ng].unsqueeze(2).broadcast_to([128, ng, 64]),
                        op=ALU.mult))
                qtok = DV.n
                q_toks.append(qtok)

                for grp in range(ncols // 2048):
                    ac_before = AC.n

                    @block.tensor
                    def _(t, grp=grp, ac_before=ac_before):
                        t.wait_ge(DV.h, qtok)
                        t.wait_ge(AC.h, ac_before)
                        t.wait_ge(PO.h, consts_tok)
                        t.wait_ge(dsem["im"].h, ident_tok)
                        inst = None
                        for kk in range(16):
                            k = grp * 16 + kk
                            inst = t.transpose(qp[:, kk * 128:(kk + 1) * 128],
                                               qs_sb[:, k * 128:(k + 1) * 128], ident[:])
                        PEc.inc(inst)
                    ttok = PEc.n
                    q_pe_busy[0] = ttok

                    @block.scalar
                    def _(s, grp=grp, ttok=ttok):
                        s.wait_ge(PEc.h, ttok)
                        s.wait_ge(dsem["tev"].h, tev_busy[0])
                        AC.inc(s.activation(out=tev[:, grp * 2048:(grp + 1) * 2048],
                                            in_=qp[:], func=AF.Copy))
                evtok = AC.n

                @block.scalar
                def _(s):
                    s.wait_ge(AC.h, evtok)
                    if is_proj:
                        out_ap = wsh[dst_row0:dst_row0 + 512, :].rearrange(
                            "(p a) c -> p (a c)", p=128)
                    else:
                        out_ap = wsh[dst_row0:dst_row0 + 128, :]
                    s.dma_start(out=out_ap, in_=tev[:, 0:ncols]).then_inc(dsem["tev"].h, 16)
                    dsem["tev"].n += 16
                tev_busy[0] = dsem["tev"].n

            for t8 in range(8):
                quant_tile(fcw_in[t8 * 128:(t8 + 1) * 128, :], 2048, 1024 + t8 * 128, False)
            for t2 in range(2):
                quant_tile(prw_in[t2 * 128:(t2 + 1) * 128, :], HID, 2048 + t2 * 512, True)
            spill_tok = dsem["tev"].n

            @block.gpsimd
            def _(g):
                g.wait_ge(dsem["tev"].h, spill_tok)
                CC.inc(g.collective_compute(
                    "AllGather", ALU.bypass, replica_groups=[list(range(NCORES))],
                    ins=[wsh[1024:3072, :].opt()], outs=[wallB.opt()]))
        ccB = CC.n
        p0_ac = [AC.n]
        p0_dv = DV.n
        p0_pe = PEc.n
        p0_tev = dsem["tev"].n
        p0_wb = dsem["wb"].n
        p0_wb2 = dsem["wb2"].n

        if "xh_dump" in debug:
            xh_dump = internal("xh_dump", [HALO, D], F32)

            @block.gpsimd
            def _(g):
                g.dma_start(out=xh_dump, in_=xhalo).then_inc(dsem["cast"].h, 16)
                dsem["cast"].n += 16

        with contextlib.ExitStack() as p1:
            xsb = [p1.enter_context(nc.sbuf_tensor(f"xsb{i}", [128, D], F32)) for i in range(2)]
            xnat = p1.enter_context(nc.sbuf_tensor("xnat", [128, D], BF16))
            xnT = [p1.enter_context(nc.sbuf_tensor(f"xnT{j}", [128, 16 * 512], BF16))
                   for j in range(NCH_K)]
            wvres = p1.enter_context(nc.sbuf_tensor("wvres", [128, 16 * 2048], BF16))
            wqk = [p1.enter_context(nc.sbuf_tensor(f"wqk{i}", [128, 2048], BF16))
                   for i in range(2)]
            c2c = p1.enter_context(nc.sbuf_tensor("c2c", [128, 512], F32))
            s2c = p1.enter_context(nc.sbuf_tensor("s2c", [128, 512], F32))
            rt1 = p1.enter_context(nc.sbuf_tensor("rt1", [128, 512], F32))
            rt2 = p1.enter_context(nc.sbuf_tensor("rt2", [128, 512], F32))
            qkst = [p1.enter_context(nc.sbuf_tensor(f"qkst{i}", [128, 512], BF16))
                    for i in range(4)]
            vst = [p1.enter_context(nc.sbuf_tensor(f"vst{i}", [128, D], BF16)) for i in range(2)]
            tp_stack = contextlib.ExitStack()
            tp = tp_stack.enter_context(nc.psum_tensor("tp", [128, 2048], BF16))

            xsb_busy = [(p0_dv, p0_ac[0]), (p0_dv, p0_ac[0])]
            tp_free = 0
            xnat_pe_busy = 0
            xnT_ready = [0] * NCH_K
            for t20 in range(20):
                j, twi = t20 // 4, t20 % 4
                xb = xsb[t20 % 2]
                dx = dsem["xa"] if t20 % 2 == 0 else dsem["xb"]
                src = xhalo[t20 * 128:(t20 + 1) * 128, :] if t20 < 4 else \
                    xm[(t20 - 4) * 128:(t20 - 3) * 128, :]
                b_dv, b_ac = xsb_busy[t20 % 2]

                @block.sync
                def _(s, b_dv=b_dv, b_ac=b_ac, dx=dx):
                    s.wait_ge(DV.h, b_dv)
                    s.wait_ge(AC.h, b_ac)
                    s.wait_ge(PEc.h, p0_pe)
                    s.wait_ge(dsem["tev"].h, p0_tev)
                    s.wait_ge(dsem["wb"].h, p0_wb)
                    s.wait_ge(dsem["wb2"].h, p0_wb2)
                    s.dma_start(out=xb[:], in_=src).then_inc(dx.h, 16)
                    dx.n += 16
                xtok = dx.n

                atok = emit_ln(xb[:], [(dx, xtok), (PO, consts_tok)], xnat[:],
                               apply_waits=[(PEc, xnat_pe_busy)])
                xsb_busy[t20 % 2] = (atok, AC.n)

                if t20 < 2 and "xnat_d" in debug:
                    if t20 == 0:
                        xnat_d = internal("xnat_d", [256, D], BF16)
                        xsb_d = internal("xsb_d", [256, D], F32)
                        st_d = internal("st_d", [256, 8], F32)

                    @block.scalar
                    def _(s, t20=t20, atok=atok):
                        s.wait_ge(DV.h, atok)
                        stx = nc.allow_non_contiguous_dma(reason="debug")
                        stx.__enter__()
                        s.dma_start(out=xnat_d[t20 * 128:(t20 + 1) * 128, :],
                                    in_=xnat[:]).then_inc(dsem["tev"].h, 16)
                        dsem["tev"].n += 16
                        s.dma_start(out=xsb_d[t20 * 128:(t20 + 1) * 128, :],
                                    in_=xb[:]).then_inc(dsem["tev"].h, 16)
                        dsem["tev"].n += 16
                        slot = t20 % 2
                        s.dma_start(out=st_d[t20 * 128:(t20 + 1) * 128, 0:1],
                                    in_=mvec[:, slot:slot + 1]).then_inc(dsem["tev"].h, 16)
                        dsem["tev"].n += 16
                        s.dma_start(out=st_d[t20 * 128:(t20 + 1) * 128, 1:2],
                                    in_=rstd[:, slot:slot + 1]).then_inc(dsem["tev"].h, 16)
                        dsem["tev"].n += 16
                        s.dma_start(out=st_d[t20 * 128:(t20 + 1) * 128, 2:3],
                                    in_=sv[:, slot:slot + 1]).then_inc(dsem["tev"].h, 16)
                        dsem["tev"].n += 16
                        s.dma_start(out=st_d[t20 * 128:(t20 + 1) * 128, 3:4],
                                    in_=ssq[:, slot:slot + 1]).then_inc(dsem["tev"].h, 16)
                        dsem["tev"].n += 16
                        stx.__exit__(None, None, None)
                    spt = dsem["tev"].n

                    @block.tensor
                    def _(t, spt=spt):
                        t.wait_ge(dsem["tev"].h, spt)

                @block.tensor
                def _(t, atok=atok, tpf=tp_free):
                    t.wait_ge(DV.h, atok)
                    t.wait_ge(DV.h, tpf)
                    t.wait_ge(PO.h, consts_tok)
                    t.wait_ge(dsem["im"].h, ident_tok)
                    t.wait_ge(AC.h, p0_ac[0])
                    inst = None
                    for k in range(16):
                        inst = t.transpose(tp[:, k * 128:(k + 1) * 128],
                                           xnat[:, k * 128:(k + 1) * 128], ident[:])
                    PEc.inc(inst)
                ttok = PEc.n
                xnat_pe_busy = ttok

                @block.vector
                def _(v, j=j, twi=twi, ttok=ttok):
                    v.wait_ge(PEc.h, ttok)
                    inst = None
                    for k in range(16):
                        inst = v.tensor_copy(
                            out=xnT[j][:, k * 512 + twi * 128:k * 512 + (twi + 1) * 128],
                            in_=tp[:, k * 128:(k + 1) * 128])
                    DV.inc(inst)
                tp_free = DV.n
                xnT_ready[j] = DV.n
            passA_done = DV.n
            tp_stack.close()
            qe_ps = [p1.enter_context(nc.psum_tensor(f"qe_ps{i}", [128, 512], F32))
                     for i in range(2)]
            qo_ps = [p1.enter_context(nc.psum_tensor(f"qo_ps{i}", [128, 512], F32))
                     for i in range(2)]
            vp_ps = p1.enter_context(nc.psum_tensor("vp_ps", [128, 2048], F32))

            if "xnT0_d" in debug:
                xnT0_d = internal("xnT0_d", [128, 16 * 512], BF16)

                @block.scalar
                def _(s):
                    s.wait_ge(DV.h, xnT_ready[0])
                    s.dma_start(out=xnT0_d, in_=xnT[0][:]).then_inc(dsem["tev"].h, 16)
                    dsem["tev"].n += 16

            @block.sync
            def _(s):
                s.wait_ge(CC.h, ccA)
                s.wait_ge(DV.h, p0_dv)
                s.wait_ge(AC.h, p0_ac[0])
                s.wait_ge(PEc.h, p0_pe)
                s.wait_ge(dsem["tev"].h, p0_tev)
                s.wait_ge(dsem["wb"].h, p0_wb)
                s.wait_ge(dsem["wb2"].h, p0_wb2)
                for k in range(16):
                    r0 = _wallA1_v_rows(k)
                    s.dma_start(out=wvres[:, k * 2048:(k + 1) * 2048],
                                in_=wallA1[r0:r0 + 128, :]).then_inc(dsem["wv"].h, 16)
                    dsem["wv"].n += 16
            wv_tok = dsem["wv"].n

            wqk_use = [p0_pe, p0_pe]
            qkst_busy = [0, 0, 0, 0]
            vst_busy = [0, 0]
            tab_busy = [p0_dv]
            rope_last = [passA_done]

            for j in range(NCH_K):
                for tt in range(4):
                    ac_before = AC.n

                    @block.tensor
                    def _(t, j=j, tt=tt, ac_before=ac_before):
                        t.wait_ge(dsem["wv"].h, wv_tok)
                        t.wait_ge(DV.h, passA_done)
                        t.wait_ge(AC.h, ac_before)
                        inst = None
                        for k in range(16):
                            for dhc in range(4):
                                inst = t.matmul(
                                    vp_ps[:, dhc * 512:(dhc + 1) * 512],
                                    xnT[j][:, k * 512 + tt * 128:k * 512 + (tt + 1) * 128],
                                    wvres[:, k * 2048 + dhc * 512:k * 2048 + (dhc + 1) * 512],
                                    start=(k == 0), stop=(k == 15))
                        PEc.inc(inst)
                    vtok = PEc.n

                    @block.scalar
                    def _(s, tt=tt, vtok=vtok):
                        s.wait_ge(PEc.h, vtok)
                        s.wait_ge(dsem[f"vs{tt % 2}"].h, vst_busy[tt % 2])
                        AC.inc(s.activation(out=vst[tt % 2][:], in_=vp_ps[:], func=AF.Copy))
                    vev = AC.n

                    @block.scalar
                    def _(s, j=j, tt=tt, vev=vev):
                        s.wait_ge(AC.h, vev)
                        r0 = j * 512 + tt * 128
                        s.dma_start(out=v_d[r0:r0 + 128, :],
                                    in_=vst[tt % 2][:]).then_inc(dsem[f"vs{tt % 2}"].h, 16)
                        dsem[f"vs{tt % 2}"].n += 16
                    vst_busy[tt % 2] = dsem[f"vs{tt % 2}"].n

                @block.sync
                def _(s, j=j, tb=tab_busy[0]):
                    s.wait_ge(DV.h, tb)
                    s.wait_ge(AC.h, p0_ac[0])
                    s.wait_ge(PEc.h, p0_pe)
                    s.wait_ge(dsem["tev"].h, p0_tev)
                    s.dma_start(out=c2c[:], in_=c2_in[:, j * 512:(j + 1) * 512]).then_inc(dsem["tab"].h, 16)
                    dsem["tab"].n += 16
                    s.dma_start(out=s2c[:], in_=s2_in[:, j * 512:(j + 1) * 512]).then_inc(dsem["tab"].h, 16)
                    dsem["tab"].n += 16
                tab_tok = dsem["tab"].n

                mats = [(1, kT_d, j)]
                if j >= 1:
                    mats.append((0, qT_d, j - 1))
                for m, dst, jc in mats:
                    rope_done = [rope_last[0], rope_last[0]]
                    for g8 in range(8):
                        pp = g8 % 2
                        for eo in range(2):
                            ob = 2 * g8 + eo
                            buf = ob % 2
                            ps = qo_ps[pp] if eo else qe_ps[pp]

                            @block.sync
                            def _(s, m=m, ob=ob, buf=buf):
                                s.wait_ge(CC.h, ccA if m == 1 else ccA2)
                                s.wait_ge(PEc.h, wqk_use[buf])
                                if m == 1:
                                    wsrc, r0 = wallA1, _wallA1_k_rows(ob)
                                else:
                                    wsrc, r0 = wallA2, _wallA2_q_rows(ob)
                                s.dma_start(out=wqk[buf][:],
                                            in_=wsrc[r0:r0 + 128, :]).then_inc(dsem[f"wq{buf}"].h, 16)
                                dsem[f"wq{buf}"].n += 16
                            wtok = dsem[f"wq{buf}"].n

                            @block.tensor
                            def _(t, j=j, buf=buf, ps=ps, wtok=wtok, eo=eo,
                                  rb=rope_done[pp]):
                                t.wait_ge(dsem[f"wq{buf}"].h, wtok)
                                t.wait_ge(DV.h, passA_done)
                                if eo == 0:
                                    t.wait_ge(DV.h, rb)
                                inst = None
                                for k in range(16):
                                    inst = t.matmul(ps[:],
                                                    wqk[buf][:, k * 128:(k + 1) * 128],
                                                    xnT[j][:, k * 512:(k + 1) * 512],
                                                    start=(k == 0), stop=(k == 15))
                                PEc.inc(inst)
                            wqk_use[buf] = PEc.n
                        prtok = PEc.n

                        sE, sO = 2 * (g8 % 2), 2 * (g8 % 2) + 1
                        stE, stO = qkst[sE], qkst[sO]

                        @block.vector
                        def _(v, sE=sE, sO=sO, stE=stE, stO=stO, prtok=prtok, tab_tok=tab_tok,
                              bE=qkst_busy[sE], bO=qkst_busy[sO], dv_rope_prev=DV.n):
                            v.wait_ge(PEc.h, prtok)
                            v.wait_ge(dsem["tab"].h, tab_tok)
                            v.wait_ge(dsem[f"q{sE}"].h, bE)
                            v.wait_ge(dsem[f"q{sO}"].h, bO)
                            v.wait_ge(DV.h, dv_rope_prev)
                            DV.inc(v.tensor_tensor(out=rt1[:], in0=qe_ps[pp][:], in1=c2c[:], op=ALU.mult))
                            DV.inc(v.tensor_tensor(out=rt2[:], in0=qo_ps[pp][:], in1=s2c[:], op=ALU.mult))
                            v.wait_ge(DV.h, DV.n)
                            DV.inc(v.tensor_tensor(out=stE[:], in0=rt1[:], in1=rt2[:], op=ALU.subtract))
                            v.wait_ge(DV.h, DV.n)
                            DV.inc(v.tensor_tensor(out=rt1[:], in0=qe_ps[pp][:], in1=s2c[:], op=ALU.mult))
                            DV.inc(v.tensor_tensor(out=rt2[:], in0=qo_ps[pp][:], in1=c2c[:], op=ALU.mult))
                            v.wait_ge(DV.h, DV.n)
                            DV.inc(v.tensor_tensor(out=stO[:], in0=rt1[:], in1=rt2[:], op=ALU.add))
                        rptok = DV.n
                        rope_done[pp] = DV.n
                        rope_last[0] = DV.n
                        tab_busy[0] = DV.n

                        @block.scalar
                        def _(s, g8=g8, dst=dst, jc=jc, sE=sE, sO=sO, stE=stE, stO=stO, rptok=rptok):
                            s.wait_ge(DV.h, rptok)
                            s.dma_start(out=dst[2 * g8, :, jc * 512:(jc + 1) * 512],
                                        in_=stE[:]).then_inc(dsem[f"q{sE}"].h, 16)
                            dsem[f"q{sE}"].n += 16
                            s.dma_start(out=dst[2 * g8 + 1, :, jc * 512:(jc + 1) * 512],
                                        in_=stO[:]).then_inc(dsem[f"q{sO}"].h, 16)
                            dsem[f"q{sO}"].n += 16
                        qkst_busy[sE] = dsem[f"q{sE}"].n
                        qkst_busy[sO] = dsem[f"q{sO}"].n

        p1_spill_toks = [(dsem[f"q{i}"], dsem[f"q{i}"].n) for i in range(4)] + \
            [(dsem[f"vs{i}"], dsem[f"vs{i}"].n) for i in range(2)]

        KSL = 4
        VSL = 9
        with contextlib.ExitStack() as p2:
            wo_sb = [p2.enter_context(nc.sbuf_tensor(f"wo_sb{i}", [128, 2048], BF16))
                     for i in range(2)]
            qq = [p2.enter_context(nc.sbuf_tensor(f"qq{i}", [128, 16 * QC], BF16))
                  for i in range(2)]
            kq = p2.enter_context(nc.sbuf_tensor("kq", [128, 16 * KSL * 256], BF16))
            vq = p2.enter_context(nc.sbuf_tensor("vq", [128, VSL * 2048], BF16))
            asb = [p2.enter_context(nc.sbuf_tensor(f"asb{i}", [128, 6 * QC], BF16))
                   for i in range(2)]
            sums = p2.enter_context(nc.sbuf_tensor("sums", [128, 16 * QC], F32))
            otn = p2.enter_context(nc.sbuf_tensor("otn", [128, 16 * QC], BF16))
            att_ev = [p2.enter_context(nc.sbuf_tensor(f"att_ev{i}", [128, QC], BF16))
                      for i in range(2)]
            att_nat = p2.enter_context(nc.sbuf_tensor("att_nat", [128, 2 * D], BF16))
            x_res = p2.enter_context(nc.sbuf_tensor("x_res", [128, 2 * D], F32))
            lntmp = p2.enter_context(nc.sbuf_tensor("lntmp", [128, D], BF16))
            x1sb = [p2.enter_context(nc.sbuf_tensor(f"x1sb{i}", [128, D], F32))
                    for i in range(2)]
            sc_ps = [p2.enter_context(nc.psum_tensor(f"sc_ps{i}", [128, 6 * QC], F32))
                     for i in range(2)]
            su_ps = p2.enter_context(nc.psum_tensor("su_ps", [128, 512], F32))
            ot_ps = p2.enter_context(nc.psum_tensor("ot_ps", [128, 512], F32))
            wo_ps = [sc_ps[0][:, 0:QC], sc_ps[0][:, 512:512 + QC]]
            nt_ps = sc_ps[1][:, 0:512].bitcast(BF16)

            wo_busy = [0, 0]

            x1sb_busy = [0, 0]
            x1sb_ac = [0, 0]
            scores_done = [0] * NQC

            def emit_loads(qc, pe_guard, first=False):
                sq = dsem["p2"] if qc % 2 == 0 else dsem["p2d"]
                sk = dsem["p2b"] if qc % 2 == 0 else dsem["p2e"]
                sv_ = dsem["p2c"] if qc % 2 == 0 else dsem["p2f"]

                @block.sync
                def _(s, qc=qc, pe_guard=pe_guard, first=first, sq=sq, sk=sk):
                    if first:
                        for ds_, tok_ in p1_spill_toks:
                            s.wait_ge(ds_.h, tok_)
                    s.wait_ge(PEc.h, pe_guard)
                    for half in range(2):
                        for e in range(2):
                            src = qT_d[half::2, 64 * e:64 * e + 64,
                                       qc * QC:(qc + 1) * QC] \
                                .rearrange("g p c -> p g c")
                            dst = qq[qc % 2][64 * half:64 * half + 64, :] \
                                .rearrange("p (g e c) -> p g e c", g=8, e=2)[:, :, e, :]
                            s.dma_start(out=dst, in_=src).then_inc(sq.h, 16)
                            sq.n += 16
                    newc0 = [0, 256, 512] if qc == 0 else [qc * 256 + 512]
                    for c0 in newc0:
                        rc = (c0 // 256) % KSL
                        for half in range(2):
                            for e in range(2):
                                src = kT_d[half::2, 64 * e:64 * e + 64, c0:c0 + 256] \
                                    .rearrange("g p c -> p g c")
                                dst = kq[64 * half:64 * half + 64, :] \
                                    .rearrange("p (g e s c) -> p g e s c",
                                               e=2, s=KSL, c=256)[:, :, e, rc, :]
                                s.dma_start(out=dst, in_=src).then_inc(sk.h, 16)
                                sk.n += 16

                @block.gpsimd
                def _(g, qc=qc, pe_guard=pe_guard, first=first, sv_=sv_):
                    if first:
                        for ds_, tok_ in p1_spill_toks:
                            g.wait_ge(ds_.h, tok_)
                    g.wait_ge(PEc.h, pe_guard)
                    vr0 = [i * 128 for i in range(6)] if qc == 0 else \
                        [qc * 256 + 512, qc * 256 + 640]
                    for r0 in vr0:
                        sl = (r0 // 128) % VSL
                        g.dma_start(out=vq[:, sl * 2048:(sl + 1) * 2048],
                                    in_=v_d[r0:r0 + 128, :]).then_inc(sv_.h, 16)
                        sv_.n += 16
                return ((sq, sq.n), (sk, sk.n), (sv_, sv_.n))

            ld_toks = {0: emit_loads(0, 0, first=True),
                       1: emit_loads(1, 0, first=True)}
            prev_att_done = DV.n
            for qc in range(NQC):
                dv_pre = prev_att_done
                ld_tok = ld_toks[qc]

                AC_base = AC.n
                av_pe = [PEc.n] * HEADS
                av_dv = [0] * HEADS
                sctoks = [0] * HEADS
                aftoks = [0] * HEADS
                pend = None

                def emit_av(ph, paftok):
                    php = ph % 2
                    dv_before = DV.n

                    @block.tensor
                    def _(t, h=ph, hp=php, aftok=paftok, dv_before=dv_before, qc=qc):
                        t.wait_ge(PO.h, aftok)
                        t.wait_ge(PO.h, consts_tok)
                        t.wait_ge(DV.h, dv_before)
                        inst = None
                        for u in range(6):
                            inst = t.matmul(su_ps[:, 0:QC], ones128[:],
                                            asb[hp][:, u * QC:(u + 1) * QC],
                                            start=(u == 0), stop=(u == 5))
                        PEc.inc(inst)
                        for u in range(6):
                            vsl = (qc * 2 + u) % VSL
                            inst = t.matmul(ot_ps[:, 0:QC],
                                            vq[:, vsl * 2048 + h * 128:vsl * 2048 + (h + 1) * 128],
                                            asb[hp][:, u * QC:(u + 1) * QC],
                                            start=(u == 0), stop=(u == 5))
                        PEc.inc(inst)
                    avtok = PEc.n
                    av_pe[ph] = PEc.n

                    @block.vector
                    def _(v, h=ph, avtok=avtok):
                        v.wait_ge(PEc.h, avtok)
                        DV.inc(v.tensor_copy(out=sums[:, h * QC:(h + 1) * QC],
                                             in_=su_ps[:, 0:QC]))
                        DV.inc(v.tensor_copy(out=otn[:, h * QC:(h + 1) * QC],
                                             in_=ot_ps[:, 0:QC]))
                    av_dv[ph] = DV.n

                for h in range(HEADS):
                    g8, eo = h // 2, h % 2
                    hp = h % 2
                    pb = 64 * eo
                    scp = sc_ps[hp]
                    eb_pre = AC_base + h - 1 if h >= 2 else AC_base

                    @block.tensor
                    def _(t, qc=qc, h=h, ld_tok=ld_tok, scp=scp, eb=eb_pre,
                          dv_pre=dv_pre):
                        if h == 0:
                            t.wait_ge(dsem["im"].h, ident_tok)
                            t.wait_ge(ld_tok[0][0].h, ld_tok[0][1])
                            t.wait_ge(ld_tok[1][0].h, ld_tok[1][1])
                            t.wait_ge(ld_tok[2][0].h, ld_tok[2][1])
                        t.wait_ge(AC.h, eb)
                        if h < 2:
                            t.wait_ge(DV.h, dv_pre)
                        inst = None
                        for u in range(6):
                            sl = scp[:, u * QC:(u + 1) * QC]
                            c0 = qc * QC + u * 128
                            kc0 = ((c0 // 256) % KSL) * 256 + (c0 % 256)
                            ops = [(kq[:, h * KSL * 256 + kc0:h * KSL * 256 + kc0 + 128],
                                    qq[qc % 2][:, h * QC:(h + 1) * QC])]
                            if qc + (u // 2) < 2:
                                ops.append((maskc_sb[:], onesrow[:]))
                            for oi, (lhsT, rhs) in enumerate(ops):
                                inst = t.matmul(sl, lhsT, rhs,
                                                start=(u % 2 == 0 and oi == 0),
                                                stop=(u % 2 == 1 and oi == len(ops) - 1))
                        PEc.inc(inst)
                    sctoks[h] = PEc.n
                    extok = AC_base + h + 1
                    if pend is not None:
                        emit_av(*pend)

                    @block.gpsimd
                    def _(g, extok=extok, hp=hp):
                        g.wait_ge(AC.h, extok)
                        for u in (0, 1):
                            PO.inc(g.affine_select(out=asb[hp][:, u * QC:(u + 1) * QC],
                                                   in_=asb[hp][:, u * QC:(u + 1) * QC],
                                                   compare_op=ALU.is_gt, fill=0.0,
                                                   base=128 * u, pattern=[[-1, QC]],
                                                   channel_multiplier=1))
                        for u in (4, 5):
                            PO.inc(g.affine_select(out=asb[hp][:, u * QC:(u + 1) * QC],
                                                   in_=asb[hp][:, u * QC:(u + 1) * QC],
                                                   compare_op=ALU.is_ge, fill=0.0,
                                                   base=-128 * (u - 4), pattern=[[1, QC]],
                                                   channel_multiplier=-1))
                    aftoks[h] = PO.n
                    pend = (h, aftoks[h])

                emit_av(*pend)
                pend = None
                scores_done[qc] = PEc.n

                @block.scalar
                def _(s, AC_base=AC_base):
                    assert AC.n == AC_base
                    for h in range(HEADS):
                        hp = h % 2
                        s.wait_ge(PEc.h, sctoks[h])
                        if h >= 2:
                            s.wait_ge(PEc.h, av_pe[h - 2])
                        AC.inc(s.activation(out=asb[hp][:], in_=sc_ps[hp][:],
                                            func=AF.Exp, scale=SCALE_QK))

                half_toks = []
                for hf in range(2):
                    @block.scalar
                    def _(s, hf=hf, dvtok=av_dv[8 * hf + 7]):
                        s.wait_ge(DV.h, dvtok)
                        sl = sums[:, hf * 8 * QC:(hf + 1) * 8 * QC]
                        AC.inc(s.activation(out=sl, in_=sl, func=AF.Ln))
                        s.wait_ge(AC.h, AC.n)
                        AC.inc(s.activation(out=sl, in_=sl, func=AF.Exp, scale=-1.0))
                    half_toks.append(AC.n)

                for hf in range(2):
                    @block.vector
                    def _(v, hf=hf, actok=half_toks[hf]):
                        v.wait_ge(AC.h, actok)
                        inst = None
                        for h in range(8 * hf, 8 * hf + 8):
                            inst = v.tensor_tensor(out=otn[:, h * QC:(h + 1) * QC],
                                                   in0=otn[:, h * QC:(h + 1) * QC],
                                                   in1=sums[:, h * QC:(h + 1) * QC],
                                                   op=ALU.mult)
                        DV.inc(inst)
                norm_tok = DV.n


                dv_before_x = DV.n

                @block.sync
                def _(s, qc=qc, dv_before_x=dv_before_x):
                    s.wait_ge(DV.h, dv_before_x)
                    s.dma_start(out=x_res[:].rearrange("p (a d) -> p a d", a=2),
                                in_=xm[qc * QC:(qc + 1) * QC, :]
                                .rearrange("(a p) d -> p a d", p=128)).then_inc(dsem["xr"].h, 16)
                    dsem["xr"].n += 16
                xr_tok = dsem["xr"].n

                ev_toks = [0] * 16

                def emit_tr(pdo):
                    dv_b2 = DV.n

                    @block.tensor
                    def _(t, pdo=pdo, dv_b2=dv_b2):
                        t.wait_ge(DV.h, ev_toks[pdo])
                        t.wait_ge(DV.h, dv_b2)
                        PEc.inc(t.transpose(nt_ps[:, 0:128],
                                            att_ev[pdo % 2][:, 0:128], ident[:]))
                        PEc.inc(t.transpose(nt_ps[:, 128:256],
                                            att_ev[pdo % 2][:, 128:256], ident[:]))
                    trtok = PEc.n

                    @block.vector
                    def _(v, pdo=pdo, trtok=trtok):
                        v.wait_ge(PEc.h, trtok)
                        DV.inc(v.tensor_copy(
                            out=att_nat[:, pdo * 128:pdo * 128 + 128],
                            in_=nt_ps[:, 0:128]))
                        DV.inc(v.tensor_copy(
                            out=att_nat[:, D + pdo * 128:D + pdo * 128 + 128],
                            in_=nt_ps[:, 128:256]))

                for do in range(16):
                    wsem = dsem["xa"] if do % 2 == 0 else dsem["xb"]

                    @block.sync
                    def _(s, do=do, wb=wo_busy[do % 2], wsem=wsem):
                        s.wait_ge(CC.h, ccA2)
                        s.wait_ge(PEc.h, wb)
                        r0 = _wallA2_o_rows(do)
                        s.dma_start(out=wo_sb[do % 2][:],
                                    in_=wallA2[r0:r0 + 128, :]).then_inc(wsem.h, 16)
                        wsem.n += 16
                    wo_tok = wsem.n

                    @block.tensor
                    def _(t, do=do, norm_tok=norm_tok, wo_tok=wo_tok, wsem=wsem,
                          evb=ev_toks[do - 2] if do >= 2 else 0):
                        t.wait_ge(DV.h, norm_tok)
                        t.wait_ge(wsem.h, wo_tok)
                        t.wait_ge(DV.h, evb)
                        inst = None
                        for k in range(16):
                            inst = t.matmul(wo_ps[do % 2],
                                            wo_sb[do % 2][:, k * 128:(k + 1) * 128],
                                            otn[:, k * QC:(k + 1) * QC],
                                            start=(k == 0), stop=(k == 15))
                        PEc.inc(inst)
                    wotok = PEc.n
                    wo_busy[do % 2] = PEc.n

                    @block.vector
                    def _(v, do=do, wotok=wotok):
                        v.wait_ge(PEc.h, wotok)
                        DV.inc(v.tensor_copy(out=att_ev[do % 2][:], in_=wo_ps[do % 2]))
                    ev_toks[do] = DV.n

                    if do >= 1:
                        emit_tr(do - 1)
                emit_tr(15)
                att_done = DV.n
                prev_att_done = att_done

                if qc + 2 < NQC:
                    ld_toks[qc + 2] = emit_loads(qc + 2, scores_done[qc])

                lnst = emit_ln_stats([att_nat[:, 0:D], att_nat[:, D:2 * D]],
                                     [(DV, att_done), (dsem["xr"], xr_tok)])
                for qsub in range(2):
                    a_ap = att_nat[:, qsub * D:(qsub + 1) * D]
                    atok = emit_ln_apply(a_ap, lnst[qsub], lntmp[:])
                    idx = qc * 2 + qsub

                    @block.vector
                    def _(v, qsub=qsub, idx=idx, atok=atok,
                          xb=x1sb_busy[qsub % 2], xa=x1sb_ac[qsub % 2]):
                        v.wait_ge(DV.h, atok)
                        v.wait_ge(dsem["x1s"].h, xb)
                        v.wait_ge(AC.h, xa)
                        DV.inc(v.tensor_tensor(out=x1sb[qsub % 2][:],
                                               in0=x_res[:, qsub * D:(qsub + 1) * D],
                                               in1=lntmp[:], op=ALU.add))
                        v.wait_ge(DV.h, DV.n)
                        DV.inc(v.tensor_reduce(out=sv3[:, idx:idx + 1],
                                               in_=x1sb[qsub % 2][:],
                                               axis=AX, op=ALU.add))
                    x1tok = DV.n - 1

                    @block.scalar
                    def _(s, qc=qc, qsub=qsub, idx=idx, x1tok=x1tok):
                        s.wait_ge(DV.h, x1tok)
                        r0 = qc * QC + qsub * 128
                        s.dma_start(out=x1_d[r0:r0 + 128, :], in_=x1sb[qsub % 2][:]).then_inc(dsem["x1s"].h, 16)
                        dsem["x1s"].n += 16
                        s.wait_ge(AC.h, AC.n)
                        AC.inc(s.activation(out=sqscr[:], in_=x1sb[qsub % 2][:],
                                            func=AF.Square,
                                            accum_out=ssq3[:, idx:idx + 1]))
                    x1sb_busy[qsub % 2] = dsem["x1s"].n
                    x1sb_ac[qsub % 2] = AC.n

        p2_spills = dsem["x1s"].n
        p2_stats_ac = AC.n

        with contextlib.ExitStack() as p3:
            x1c = p3.enter_context(nc.sbuf_tensor("x1c", [128, 4 * D], F32))
            xn2T = p3.enter_context(nc.sbuf_tensor("xn2T", [128, 16 * 512], BF16))
            Hsb = p3.enter_context(nc.sbuf_tensor("Hsb", [128, NHT * 512], BF16))
            fcb = [p3.enter_context(nc.sbuf_tensor(f"fcb{i}", [128, 2048], BF16))
                   for i in range(2)]
            prb = [p3.enter_context(nc.sbuf_tensor(f"prb{i}", [128, 8192], BF16))
                   for i in range(2)]
            qsb = [p3.enter_context(nc.sbuf_tensor(f"qsb{i}", [128, 512], BF16))
                   for i in range(2)]
            g2b = [p3.enter_context(nc.sbuf_tensor(f"g2b{i}", [128, 512], BF16))
                   for i in range(4)]
            rssb = p3.enter_context(nc.sbuf_tensor("rssb", [128, 512], F32))
            otb = [p3.enter_context(nc.sbuf_tensor(f"otb{i}", [128, 512], BF16))
                   for i in range(2)]
            fnat = p3.enter_context(nc.sbuf_tensor("fnat", [128, 4 * D], BF16))
            xnat3 = p3.enter_context(nc.sbuf_tensor("xnat3", [128, D], BF16))
            ysb = [p3.enter_context(nc.sbuf_tensor(f"ysb{i}", [128, D], F32))
                   for i in range(2)]
            tp3 = p3.enter_context(nc.psum_tensor("tp3", [128, 2048], BF16))
            hp_ps = [p3.enter_context(nc.psum_tensor(f"hp{i}", [128, 512], F32))
                     for i in range(2)]
            ms_ps = p3.enter_context(nc.psum_tensor("ms_ps", [128, 512], F32))
            op_ps = [p3.enter_context(nc.psum_tensor(f"op{i}", [128, 512], F32))
                     for i in range(2)]

            @block.vector
            def _(v):
                v.wait_ge(AC.h, p2_stats_ac)
                DV.inc(v.tensor_scalar(out=m3[:], in0=sv3[:], scalar1=1.0 / D,
                                       scalar2=None, op0=ALU.mult))
                DV.inc(v.tensor_scalar(out=r3[:], in0=ssq3[:], scalar1=1.0 / D,
                                       scalar2=None, op0=ALU.mult))
                v.wait_ge(DV.h, DV.n)
                DV.inc(v.tensor_tensor(out=sv3[:], in0=m3[:], in1=m3[:], op=ALU.mult))
                v.wait_ge(DV.h, DV.n)
                DV.inc(v.tensor_tensor(out=r3[:], in0=r3[:], in1=sv3[:], op=ALU.subtract))
            var3_tok = DV.n

            @block.scalar
            def _(s):
                s.wait_ge(DV.h, var3_tok)
                AC.inc(s.activation(out=r3[:], in_=r3[:], func=AF.Ln, bias=eps_ln[:]))
                s.wait_ge(AC.h, AC.n)
                AC.inc(s.activation(out=r3[:], in_=r3[:], func=AF.Exp, scale=-0.5))
            r3_tok = AC.n

            fcb_busy = [0, 0]
            prb_busy = [0, 0]
            qsb_ac = [0, 0]
            qsb_dv = [0, 0]
            g2b_busy = [0, 0, 0, 0]
            g2_toks = [0] * NHT
            otb_busy = [0, 0]
            ysb_busy = [0, 0]
            ms_read = 0
            xnat3_pe_busy = [0]

            for ch in range(4):
                pe_cs = PEc.n
                ac_cs = AC.n
                dv_cs = DV.n

                @block.sync
                def _(s, ch=ch, dv_cs=dv_cs):
                    s.wait_ge(dsem["x1s"].h, p2_spills)
                    s.wait_ge(DV.h, dv_cs)
                    s.dma_start(out=x1c[:].rearrange("p (a d) -> p a d", a=4),
                                in_=x1_d[ch * 512:(ch + 1) * 512, :]
                                .rearrange("(a p) d -> p a d", p=128)).then_inc(dsem["xc"].h, 16)
                    dsem["xc"].n += 16
                x1tok = dsem["xc"].n

                tp_free3 = DV.n
                for i4 in range(4):
                    idx = ch * 4 + i4

                    @block.vector
                    def _(v, i4=i4, idx=idx, x1tok=x1tok):
                        v.wait_ge(dsem["xc"].h, x1tok)
                        v.wait_ge(AC.h, max(r3_tok, ac_cs))
                        v.wait_ge(PEc.h, pe_cs)
                        DV.inc(v.tensor_scalar(out=fnat[:, i4 * D:(i4 + 1) * D],
                                               in0=x1c[:, i4 * D:(i4 + 1) * D],
                                               scalar1=m3[:, idx:idx + 1],
                                               scalar2=r3[:, idx:idx + 1],
                                               op0=ALU.subtract, op1=ALU.mult))
                    atok = DV.n

                    @block.tensor
                    def _(t, i4=i4, atok=atok, tpf=tp_free3):
                        t.wait_ge(DV.h, atok)
                        t.wait_ge(DV.h, tpf)
                        inst = None
                        for k in range(16):
                            inst = t.transpose(tp3[:, k * 128:(k + 1) * 128],
                                               fnat[:, i4 * D + k * 128:i4 * D + (k + 1) * 128],
                                               ident[:])
                        PEc.inc(inst)
                    ttok = PEc.n
                    xnat3_pe_busy[0] = ttok

                    @block.vector
                    def _(v, i4=i4, ttok=ttok, pe_cs=pe_cs):
                        v.wait_ge(PEc.h, ttok)
                        v.wait_ge(PEc.h, pe_cs)
                        inst = None
                        for k in range(16):
                            inst = v.tensor_copy(
                                out=xn2T[:, k * 512 + i4 * 128:k * 512 + (i4 + 1) * 128],
                                in_=tp3[:, k * 128:(k + 1) * 128])
                        DV.inc(inst)
                    tp_free3 = DV.n
                xn2_tok = DV.n

                AC3 = AC.n
                htoks = [0] * NHT
                qdv_guards = [0] * NHT
                for ht in range(NHT):
                    hb = ht % 2

                    @block.sync
                    def _(s, ht=ht, hb=hb, fb=fcb_busy[hb]):
                        s.wait_ge(CC.h, ccB)
                        s.wait_ge(PEc.h, fb)
                        r0 = _wallB_fc_rows(ht)
                        s.dma_start(out=fcb[hb][:], in_=wallB[r0:r0 + 128, :]).then_inc(dsem[f"f{hb}"].h, 16)
                        dsem[f"f{hb}"].n += 16
                    ftok = dsem[f"f{hb}"].n

                    ab_pre = AC3 + ht - 1 if ht >= 2 else AC3

                    @block.tensor
                    def _(t, hb=hb, ftok=ftok, xn2_tok=xn2_tok, ab=ab_pre, mr=ms_read):
                        t.wait_ge(dsem[f"f{hb}"].h, ftok)
                        t.wait_ge(DV.h, xn2_tok)
                        t.wait_ge(AC.h, ab)
                        t.wait_ge(AC.h, mr)
                        inst = None
                        for k in range(16):
                            inst = t.matmul(hp_ps[hb][:],
                                            fcb[hb][:, k * 128:(k + 1) * 128],
                                            xn2T[:, k * 512:(k + 1) * 512],
                                            start=(k == 0), stop=(k == 15))
                        PEc.inc(inst)
                    fcb_busy[hb] = PEc.n
                    htoks[ht] = PEc.n
                    qtok = AC3 + ht + 1

                    gq = ht % 4

                    @block.vector
                    def _(v, ht=ht, hb=hb, gq=gq, qtok=qtok, gb=g2b_busy[gq], pe_cs=pe_cs):
                        v.wait_ge(AC.h, qtok)
                        v.wait_ge(PEc.h, pe_cs)
                        DV.inc(v.tensor_tensor(out=Hsb[:, ht * 512:(ht + 1) * 512],
                                               in0=qsb[hb][:], in1=qsb[hb][:], op=ALU.mult))
                        qsb_dv[hb] = DV.n
                        v.wait_ge(DV.h, DV.n)
                        v.wait_ge(PEc.h, gb)
                        DV.inc(v.tensor_tensor(out=g2b[gq][:],
                                               in0=Hsb[:, ht * 512:(ht + 1) * 512],
                                               in1=Hsb[:, ht * 512:(ht + 1) * 512],
                                               op=ALU.mult))
                    g2_toks[ht] = DV.n
                    qdv_guards[ht] = qsb_dv[hb]

                    if ht >= 2:
                        hd = ht - 2

                        @block.tensor
                        def _(t, hd=hd):
                            t.wait_ge(DV.h, g2_toks[hd])
                            PEc.inc(t.matmul(ms_ps[:], ones128[:], g2b[hd % 4][:],
                                             start=(hd == 0), stop=False))
                        g2b_busy[hd % 4] = PEc.n
                for hd in (NHT - 2, NHT - 1):
                    @block.tensor
                    def _(t, hd=hd):
                        t.wait_ge(DV.h, g2_toks[hd])
                        PEc.inc(t.matmul(ms_ps[:], ones128[:], g2b[hd % 4][:],
                                         start=False, stop=(hd == NHT - 1)))
                    g2b_busy[hd % 4] = PEc.n
                ms_tok = PEc.n

                @block.scalar
                def _(s, AC3=AC3):
                    assert AC.n == AC3
                    for ht in range(NHT):
                        hb = ht % 2
                        s.wait_ge(PEc.h, htoks[ht])
                        if ht >= 2:
                            s.wait_ge(DV.h, qdv_guards[ht - 2])
                        AC.inc(s.activation(out=qsb[hb][:], in_=hp_ps[hb][:],
                                            func=AF.Relu))

                @block.scalar
                def _(s, ms_tok=ms_tok, dv_cs=dv_cs):
                    s.wait_ge(PEc.h, ms_tok)
                    s.wait_ge(DV.h, dv_cs)
                    AC.inc(s.activation(out=rssb[:], in_=ms_ps[:], func=AF.Ln,
                                        bias=eps_rms[:], scale=1.0 / HID))
                    s.wait_ge(AC.h, AC.n)
                    AC.inc(s.activation(out=rssb[:], in_=rssb[:], func=AF.Exp, scale=-0.5))
                rs_tok = AC.n
                ms_read = AC.n

                h_tok = DV.n
                for do in range(16):
                    db = do % 2

                    @block.sync
                    def _(s, do=do, db=db, pbz=prb_busy[db]):
                        s.wait_ge(CC.h, ccB)
                        s.wait_ge(PEc.h, pbz)
                        r0 = _wallB_pr_rows(do)
                        s.dma_start(out=prb[db][:],
                                    in_=wallB[r0:r0 + 512, :]
                                    .rearrange("(p a) c -> p (a c)", p=128)).then_inc(dsem[f"pr{db}"].h, 16)
                        dsem[f"pr{db}"].n += 16
                    ptok = dsem[f"pr{db}"].n

                    @block.tensor
                    def _(t, db=db, ptok=ptok, h_tok=h_tok, ob=otb_busy[db]):
                        t.wait_ge(dsem[f"pr{db}"].h, ptok)
                        t.wait_ge(DV.h, h_tok)
                        t.wait_ge(DV.h, ob)
                        inst = None
                        for hc in range(NHT):
                            inst = t.matmul(op_ps[db][:],
                                            prb[db][:, hc * 128:(hc + 1) * 128],
                                            Hsb[:, hc * 512:(hc + 1) * 512],
                                            start=(hc == 0), stop=(hc == NHT - 1))
                        PEc.inc(inst)
                    prb_busy[db] = PEc.n
                    optok = PEc.n

                    @block.vector
                    def _(v, db=db, optok=optok, rs_tok=rs_tok, pe_ob=otb_busy[db]):
                        v.wait_ge(PEc.h, optok)
                        v.wait_ge(AC.h, rs_tok)
                        v.wait_ge(PEc.h, pe_ob)
                        DV.inc(v.tensor_tensor(out=otb[db][:], in0=op_ps[db][:],
                                               in1=rssb[:], op=ALU.mult))
                    ottok = DV.n

                    dv_b3 = DV.n

                    @block.tensor
                    def _(t, db=db, ottok=ottok, dv_b3=dv_b3):
                        t.wait_ge(DV.h, ottok)
                        t.wait_ge(DV.h, dv_b3)
                        inst = None
                        for i4 in range(4):
                            inst = t.transpose(tp3[:, i4 * 128:(i4 + 1) * 128],
                                               otb[db][:, i4 * 128:(i4 + 1) * 128],
                                               ident[:])
                        PEc.inc(inst)
                    otb_busy[db] = PEc.n
                    trtok = PEc.n

                    @block.vector
                    def _(v, do=do, trtok=trtok):
                        v.wait_ge(PEc.h, trtok)
                        inst = None
                        for i4 in range(4):
                            inst = v.tensor_copy(
                                out=fnat[:, i4 * D + do * 128:i4 * D + (do + 1) * 128],
                                in_=tp3[:, i4 * 128:(i4 + 1) * 128])
                        DV.inc(inst)
                fn_tok = DV.n

                lnst3 = emit_ln_stats([fnat[:, i * D:(i + 1) * D] for i in range(4)],
                                      [(DV, fn_tok)])
                for i4 in range(4):
                    yb = i4 % 2
                    atok = emit_ln_apply(fnat[:, i4 * D:(i4 + 1) * D], lnst3[i4], xnat3[:])

                    @block.vector
                    def _(v, i4=i4, yb=yb, atok=atok, ybb=ysb_busy[yb]):
                        v.wait_ge(DV.h, atok)
                        v.wait_ge(dsem[f"y{yb}"].h, ybb)
                        DV.inc(v.tensor_tensor(out=ysb[yb][:],
                                               in0=x1c[:, i4 * D:(i4 + 1) * D],
                                               in1=xnat3[:], op=ALU.add))
                    ytok = DV.n

                    @block.scalar
                    def _(s, ch=ch, i4=i4, yb=yb, ytok=ytok):
                        s.wait_ge(DV.h, ytok)
                        r0 = ch * 512 + i4 * 128
                        s.dma_start(out=y[r0:r0 + 128, :], in_=ysb[yb][:]).then_inc(dsem[f"y{yb}"].h, 16)
                        dsem[f"y{yb}"].n += 16
                    ysb_busy[yb] = dsem[f"y{yb}"].n

        y0_done, y1_done = dsem["y0"].n, dsem["y1"].n

        @block.sync
        def _(s):
            s.wait_ge(dsem["y0"].h, y0_done)
            s.wait_ge(dsem["y1"].h, y1_done)

    return nc


_NC_CACHE = {}


def kernel(L0_attn_x, L0_attn_Wq, L0_attn_Wk, L0_attn_Wv, L0_attn_Wo,
           L0_ffn_W_fc, L0_ffn_W_proj):
    in_maps = _host_prep(L0_attn_x, L0_attn_Wq, L0_attn_Wk, L0_attn_Wv,
                         L0_attn_Wo, L0_ffn_W_fc, L0_ffn_W_proj)
    if "nc" not in _NC_CACHE:
        _NC_CACHE["nc"] = build_nc()
    nc = _NC_CACHE["nc"]
    res = run_bass_kernel_spmd(nc, in_maps, core_ids=list(range(NCORES)))
    out = np.empty((4, 4096, 2048), dtype=np.float32)
    for c in range(NCORES):
        b, h = c // 2, c % 2
        out[b, h * S_CORE:(h + 1) * S_CORE, :] = res.results[c]["y"]
    return out


# version 25
# speedup vs baseline: 1.0604x; 1.0604x over previous
import contextlib
import numpy as np
import ml_dtypes
import concourse.bass as bass
import concourse.mybir as mybir
from concourse.bass_utils import run_bass_kernel_spmd

F32 = mybir.dt.float32
BF16 = mybir.dt.bfloat16
AF = mybir.ActivationFunctionType
ALU = mybir.AluOpType
AX = mybir.AxisListType.X

D = 2048
HEADS = 16
S_CORE = 2048
HALO = 512
SK = S_CORE + HALO
NCH_K = 5
QC = 256
NQC = S_CORE // QC
HID = 8192
NHT = 64
MAGIC = 12582912.0
LN_EPS = 1e-5
RMS_EPS = float(np.finfo(np.float32).eps)
SCALE_QK = 1.0 / float(np.sqrt(128.0))
NEG = -1e30
NCORES = 8


def _perm_eo():
    perm = np.zeros(D, dtype=np.int64)
    for g in range(8):
        b = 256 * g
        perm[b:b + 64] = (2 * g) * 128 + 2 * np.arange(64)
        perm[b + 64:b + 128] = (2 * g + 1) * 128 + 2 * np.arange(64)
        perm[b + 128:b + 192] = (2 * g) * 128 + 2 * np.arange(64) + 1
        perm[b + 192:b + 256] = (2 * g + 1) * 128 + 2 * np.arange(64) + 1
    return perm


def _slabs_from_T(WT):
    return np.ascontiguousarray(
        WT.reshape(16, 128, 16, 128).transpose(2, 1, 0, 3).reshape(16, 128, 2048))


def _host_prep(L0_attn_x, L0_attn_Wq, L0_attn_Wk, L0_attn_Wv, L0_attn_Wo,
               L0_ffn_W_fc, L0_ffn_W_proj):
    X = np.asarray(L0_attn_x, dtype=np.float32)
    perm = _perm_eo()
    slab_q = _slabs_from_T(np.ascontiguousarray(np.asarray(L0_attn_Wq, np.float32).T)[:, perm])
    slab_k = _slabs_from_T(np.ascontiguousarray(np.asarray(L0_attn_Wk, np.float32).T)[:, perm])
    slab_v = np.ascontiguousarray(np.asarray(L0_attn_Wv, np.float32).T.reshape(16, 128, 2048))
    slab_o = _slabs_from_T(np.ascontiguousarray(np.asarray(L0_attn_Wo, np.float32).T))
    parts = []
    for c in range(8):
        parts += [slab_k[2 * c:2 * c + 2].reshape(-1, 2048),
                  slab_v[2 * c:2 * c + 2].reshape(-1, 2048),
                  slab_q[2 * c:2 * c + 2].reshape(-1, 2048),
                  slab_o[2 * c:2 * c + 2].reshape(-1, 2048)]
    attn_flat = np.concatenate(parts, axis=0)
    Wfc = np.asarray(L0_ffn_W_fc, np.float32)
    Wpr = np.asarray(L0_ffn_W_proj, np.float32)
    inv = 10000.0 ** (-2.0 * np.arange(64) / 128.0)

    in_maps = []
    for c in range(NCORES):
        b, h = c // 2, c % 2
        start = h * S_CORE
        xm = np.ascontiguousarray(X[b, start:start + S_CORE])
        xhalo = np.ascontiguousarray(X[b, start - HALO:start] if h == 1 else X[b, 0:HALO])
        pos = (start - HALO) + np.arange(SK)
        th = pos[None, :] * inv[:, None]
        c2 = np.concatenate([np.cos(th), np.cos(th)], axis=0).astype(np.float32)
        s2 = np.concatenate([np.sin(th), np.sin(th)], axis=0).astype(np.float32)
        maskcol = np.full((1, 128), 0.0 if h == 1 else NEG, dtype=np.float32)
        in_maps.append({
            "identin": np.eye(128, dtype=np.float32),
            "xm": xm, "xh": xhalo,
            "attnw": np.ascontiguousarray(attn_flat[c * 1024:(c + 1) * 1024]),
            "fcw": np.ascontiguousarray(Wfc[c * 1024:(c + 1) * 1024]),
            "prw": np.ascontiguousarray(Wpr[c * 256:(c + 1) * 256]),
            "c2": c2, "s2": s2, "maskcol": maskcol,
        })
    return in_maps


class Ctr:
    def __init__(self, stack, nc, name):
        self.h = stack.enter_context(nc.semaphore(name))
        self.n = 0

    def inc(self, inst, k=1):
        inst.then_inc(self.h, k)
        self.n += k
        return self.n


def _wallA1_k_rows(i):
    return (i // 2) * 512 + (i % 2) * 128


def _wallA1_v_rows(i):
    return (i // 2) * 512 + 256 + (i % 2) * 128


def _wallA2_q_rows(i):
    return (i // 2) * 512 + (i % 2) * 128


def _wallA2_o_rows(i):
    return (i // 2) * 512 + 256 + (i % 2) * 128


def _wallB_fc_rows(ht):
    return (ht // 8) * 2048 + (ht % 8) * 128


def _wallB_pr_rows(do):
    return (do // 2) * 2048 + 1024 + (do % 2) * 512


def build_nc(debug=()):
    nc = bass.Bass("TRN2", target_bir_lowering=False)

    xm = nc.dram_tensor("xm", [S_CORE, D], F32, kind="ExternalInput").ap()
    xhalo = nc.dram_tensor("xh", [HALO, D], F32, kind="ExternalInput").ap()
    attnw = nc.dram_tensor("attnw", [1024, D], F32, kind="ExternalInput").ap()
    fcw_in = nc.dram_tensor("fcw", [1024, D], F32, kind="ExternalInput").ap()
    prw_in = nc.dram_tensor("prw", [256, HID], F32, kind="ExternalInput").ap()
    c2_in = nc.dram_tensor("c2", [128, SK], F32, kind="ExternalInput").ap()
    s2_in = nc.dram_tensor("s2", [128, SK], F32, kind="ExternalInput").ap()
    maskcol_in = nc.dram_tensor("maskcol", [1, 128], F32, kind="ExternalInput").ap()
    ident_in = nc.dram_tensor("identin", [128, 128], F32, kind="ExternalInput").ap()
    y = nc.dram_tensor("y", [S_CORE, D], F32, kind="ExternalOutput").ap()

    def internal(name, shape, dtype):
        kind = "ExternalOutput" if name in debug else "Internal"
        return nc.dram_tensor(name, shape, dtype, kind=kind).ap()

    wsh = internal("wsh", [3072, D], BF16)
    wallA1 = internal("wallA1", [4096, D], BF16)
    wallA2 = internal("wallA2", [4096, D], BF16)
    wallB = internal("wallB", [16384, D], BF16)
    qT_d = internal("qT_d", [16, 128, S_CORE], BF16)
    kT_d = internal("kT_d", [16, 128, SK], BF16)
    v_d = internal("v_d", [SK, D], BF16)
    x1_d = internal("x1_d", [S_CORE, D], F32)

    st = contextlib.ExitStack()
    with st:
        ident = st.enter_context(nc.sbuf_tensor("ident", [128, 128], BF16))
        ones128 = st.enter_context(nc.sbuf_tensor("ones128", [128, 128], BF16))
        onesrow = st.enter_context(nc.sbuf_tensor("onesrow", [1, QC], BF16))
        maskc_sb = st.enter_context(nc.sbuf_tensor("maskc_sb", [1, 128], BF16))
        eps_ln = st.enter_context(nc.sbuf_tensor("eps_ln", [128, 1], F32))
        eps_rms = st.enter_context(nc.sbuf_tensor("eps_rms", [128, 1], F32))
        sv = st.enter_context(nc.sbuf_tensor("sv", [128, 4], F32))
        ssq = st.enter_context(nc.sbuf_tensor("ssq", [128, 4], F32))
        sv3 = st.enter_context(nc.sbuf_tensor("sv3", [128, 16], F32))
        ssq3 = st.enter_context(nc.sbuf_tensor("ssq3", [128, 16], F32))
        m3 = st.enter_context(nc.sbuf_tensor("m3", [128, 16], F32))
        r3 = st.enter_context(nc.sbuf_tensor("r3", [128, 16], F32))
        mvec = st.enter_context(nc.sbuf_tensor("mvec", [128, 4], F32))
        vvec = st.enter_context(nc.sbuf_tensor("vvec", [128, 4], F32))
        rstd = st.enter_context(nc.sbuf_tensor("rstd", [128, 4], F32))
        sqscr = st.enter_context(nc.sbuf_tensor("sqscr", [128, D], BF16))

        PEc = Ctr(st, nc, "sPE")
        DV = Ctr(st, nc, "sDV")
        AC = Ctr(st, nc, "sAC")
        PO = Ctr(st, nc, "sPO")
        CC = Ctr(st, nc, "sCC")
        dsem = {n: Ctr(st, nc, "d_" + n) for n in (
            "xa", "xb", "wv", "tab", "wq0", "wq1",
            "q0", "q1", "q2", "q3", "vs0", "vs1",
            "cast", "wb", "tev",
            "p2", "wo", "xr", "x1s",
            "xc", "f0", "f1", "pr0", "pr1", "y0", "y1", "p2b", "p2c", "wb2",
            "im", "p2d", "p2e", "p2f", "x1t")}

        block = st.enter_context(nc.Block())

        ln_state = {"slot_busy": [(0, 0)] * 4, "next": 0}

        def emit_ln_stats(xs, pre_waits):
            k = len(xs)
            slots = [(ln_state["next"] + i) % 4 for i in range(k)]
            ln_state["next"] = (ln_state["next"] + k) % 4
            busys = [ln_state["slot_busy"][s] for s in slots]

            @block.vector
            def _(v):
                for (ct, tok) in pre_waits:
                    v.wait_ge(ct.h, tok)
                for (b_dv, b_ac) in busys:
                    v.wait_ge(DV.h, b_dv)
                    v.wait_ge(AC.h, b_ac)
                for x_ap, s_ in zip(xs, slots):
                    DV.inc(v.tensor_reduce(out=sv[:, s_:s_ + 1], in_=x_ap,
                                           axis=AX, op=ALU.add))
            rtok = DV.n

            @block.scalar
            def _(s):
                for (ct, tok) in pre_waits:
                    s.wait_ge(ct.h, tok)
                for (b_dv, b_ac) in busys:
                    s.wait_ge(DV.h, b_dv)
                for x_ap, s_ in zip(xs, slots):
                    s.wait_ge(AC.h, AC.n)
                    AC.inc(s.activation(out=sqscr[:], in_=x_ap, func=AF.Square,
                                        accum_out=ssq[:, s_:s_ + 1]))
            sqtok = AC.n

            @block.vector
            def _(v):
                v.wait_ge(DV.h, rtok)
                v.wait_ge(AC.h, sqtok)
                for s_ in slots:
                    DV.inc(v.tensor_scalar(out=mvec[:, s_:s_ + 1], in0=sv[:, s_:s_ + 1],
                                           scalar1=1.0 / D, scalar2=None, op0=ALU.mult))
                    DV.inc(v.tensor_scalar(out=vvec[:, s_:s_ + 1], in0=ssq[:, s_:s_ + 1],
                                           scalar1=1.0 / D, scalar2=None, op0=ALU.mult))
                v.wait_ge(DV.h, DV.n)
                for s_ in slots:
                    DV.inc(v.tensor_tensor(out=rstd[:, s_:s_ + 1], in0=mvec[:, s_:s_ + 1],
                                           in1=mvec[:, s_:s_ + 1], op=ALU.mult))
                v.wait_ge(DV.h, DV.n)
                for s_ in slots:
                    DV.inc(v.tensor_tensor(out=vvec[:, s_:s_ + 1], in0=vvec[:, s_:s_ + 1],
                                           in1=rstd[:, s_:s_ + 1], op=ALU.subtract))
            vartok = DV.n

            @block.scalar
            def _(s):
                s.wait_ge(DV.h, vartok)
                for s_ in slots:
                    AC.inc(s.activation(out=rstd[:, s_:s_ + 1], in_=vvec[:, s_:s_ + 1],
                                        func=AF.Ln, bias=eps_ln[:]))
                s.wait_ge(AC.h, AC.n)
                for s_ in slots:
                    AC.inc(s.activation(out=rstd[:, s_:s_ + 1], in_=rstd[:, s_:s_ + 1],
                                        func=AF.Exp, scale=-0.5))
            rstok = AC.n
            return [(s_, rstok) for s_ in slots]

        def emit_ln_apply(x_ap, slot_tok, out_ap, apply_waits=()):
            slot, rstok = slot_tok

            @block.vector
            def _(v):
                v.wait_ge(AC.h, rstok)
                for (ct, tok) in apply_waits:
                    v.wait_ge(ct.h, tok)
                DV.inc(v.tensor_scalar(out=out_ap, in0=x_ap,
                                       scalar1=mvec[:, slot:slot + 1],
                                       scalar2=rstd[:, slot:slot + 1],
                                       op0=ALU.subtract, op1=ALU.mult))
            ln_state["slot_busy"][slot] = (DV.n, rstok)
            return DV.n

        def emit_ln(x_ap, pre_waits, out_ap, apply_waits=()):
            st_ = emit_ln_stats([x_ap], pre_waits)[0]
            return emit_ln_apply(x_ap, st_, out_ap, apply_waits)

        @block.gpsimd
        def _(g):
            g.dma_start(out=ident[:], in_=ident_in).then_inc(dsem["im"].h, 16)
            dsem["im"].n += 16
            PO.inc(g.memset(ones128[:], 1.0))
            PO.inc(g.memset(onesrow[:], 1.0))
            PO.inc(g.memset(eps_ln[:], LN_EPS))
            PO.inc(g.memset(eps_rms[:], RMS_EPS))
            g.dma_start(out=wsh[0:1024, :], in_=attnw).then_inc(dsem["cast"].h, 16)
            dsem["cast"].n += 16
            g.dma_start(out=maskc_sb[:], in_=maskcol_in).then_inc(dsem["im"].h, 16)
            dsem["im"].n += 16
        consts_tok = PO.n
        ident_tok = 32
        cast_tok = dsem["cast"].n

        @block.gpsimd
        def _(g):
            g.wait_ge(dsem["cast"].h, cast_tok)
            CC.inc(g.collective_compute(
                "AllGather", ALU.bypass, replica_groups=[list(range(NCORES))],
                ins=[wsh[0:512, :].opt()], outs=[wallA1.opt()]))
        ccA = CC.n

        @block.gpsimd
        def _(g):
            g.wait_ge(dsem["cast"].h, cast_tok)
            CC.inc(g.collective_compute(
                "AllGather", ALU.bypass, replica_groups=[list(range(NCORES))],
                ins=[wsh[512:1024, :].opt()], outs=[wallA2.opt()]))
        ccA2 = CC.n

        with contextlib.ExitStack() as p0:
            wbb = [p0.enter_context(nc.sbuf_tensor(f"q_wb{i}", [128, HID], BF16))
                   for i in range(2)]
            gsum = p0.enter_context(nc.sbuf_tensor("q_gs", [128, 128], F32))
            scb = p0.enter_context(nc.sbuf_tensor("q_sc", [128, 128], BF16))
            rr = p0.enter_context(nc.sbuf_tensor("q_rr", [128, 128], F32))
            qs_sb = p0.enter_context(nc.sbuf_tensor("q_qs", [128, HID], BF16))
            tev = p0.enter_context(nc.sbuf_tensor("q_tev", [128, HID], BF16))
            qp = p0.enter_context(nc.psum_tensor("q_qp", [128, 2048], BF16))

            q_pe_busy = [0]
            tev_busy = [0]
            q_toks = []

            def quant_tile(src_ap, ncols, dst_row0, is_proj):
                ng = ncols // 64
                ti = len(q_toks)
                wb = wbb[ti % 2]
                wsem = dsem["wb"] if ti % 2 == 0 else dsem["wb2"]
                wb_guard = q_toks[ti - 2] if ti >= 2 else 0

                @block.gpsimd
                def _(g):
                    g.wait_ge(DV.h, wb_guard)
                    g.dma_start(out=wb[:, 0:ncols], in_=src_ap).then_inc(wsem.h, 16)
                    wsem.n += 16
                ld = wsem.n

                @block.vector
                def _(v):
                    v.wait_ge(wsem.h, ld)
                    v.wait_ge(PEc.h, q_pe_busy[0])
                    DV.inc(v.tensor_reduce(
                        out=gsum[:, 0:ng],
                        in_=wb[:, 0:ncols].rearrange("p (g k) -> p g k", k=64),
                        axis=AX, op=ALU.add, apply_absolute_value=True))
                    v.wait_ge(DV.h, DV.n)
                    DV.inc(v.tensor_scalar(out=scb[:, 0:ng], in0=gsum[:, 0:ng],
                                           scalar1=1.0 / 64.0, scalar2=1e-8,
                                           op0=ALU.mult, op1=ALU.max))
                    v.wait_ge(DV.h, DV.n)
                    DV.inc(v.reciprocal(out=rr[:, 0:ng], in_=scb[:, 0:ng]))
                    v.wait_ge(DV.h, DV.n)
                    DV.inc(v.tensor_tensor(
                        out=wb[:, 0:ncols].rearrange("p (g k) -> p g k", k=64),
                        in0=wb[:, 0:ncols].rearrange("p (g k) -> p g k", k=64),
                        in1=rr[:, 0:ng].unsqueeze(2).broadcast_to([128, ng, 64]),
                        op=ALU.mult))
                    v.wait_ge(DV.h, DV.n)
                    DV.inc(v.tensor_scalar(out=wb[:, 0:ncols], in0=wb[:, 0:ncols],
                                           scalar1=MAGIC, scalar2=MAGIC,
                                           op0=ALU.add, op1=ALU.subtract))
                    v.wait_ge(DV.h, DV.n)
                    DV.inc(v.tensor_scalar(out=wb[:, 0:ncols], in0=wb[:, 0:ncols],
                                           scalar1=1.0, scalar2=-1.0,
                                           op0=ALU.min, op1=ALU.max))
                    v.wait_ge(DV.h, DV.n)
                    DV.inc(v.tensor_tensor(
                        out=qs_sb[:, 0:ncols].rearrange("p (g k) -> p g k", k=64),
                        in0=wb[:, 0:ncols].rearrange("p (g k) -> p g k", k=64),
                        in1=scb[:, 0:ng].unsqueeze(2).broadcast_to([128, ng, 64]),
                        op=ALU.mult))
                qtok = DV.n
                q_toks.append(qtok)

                for grp in range(ncols // 2048):
                    ac_before = AC.n

                    @block.tensor
                    def _(t, grp=grp, ac_before=ac_before):
                        t.wait_ge(DV.h, qtok)
                        t.wait_ge(AC.h, ac_before)
                        t.wait_ge(PO.h, consts_tok)
                        t.wait_ge(dsem["im"].h, ident_tok)
                        inst = None
                        for kk in range(16):
                            k = grp * 16 + kk
                            inst = t.transpose(qp[:, kk * 128:(kk + 1) * 128],
                                               qs_sb[:, k * 128:(k + 1) * 128], ident[:])
                        PEc.inc(inst)
                    ttok = PEc.n
                    q_pe_busy[0] = ttok

                    @block.scalar
                    def _(s, grp=grp, ttok=ttok):
                        s.wait_ge(PEc.h, ttok)
                        s.wait_ge(dsem["tev"].h, tev_busy[0])
                        AC.inc(s.activation(out=tev[:, grp * 2048:(grp + 1) * 2048],
                                            in_=qp[:], func=AF.Copy))
                evtok = AC.n

                @block.scalar
                def _(s):
                    s.wait_ge(AC.h, evtok)
                    if is_proj:
                        out_ap = wsh[dst_row0:dst_row0 + 512, :].rearrange(
                            "(p a) c -> p (a c)", p=128)
                    else:
                        out_ap = wsh[dst_row0:dst_row0 + 128, :]
                    s.dma_start(out=out_ap, in_=tev[:, 0:ncols]).then_inc(dsem["tev"].h, 16)
                    dsem["tev"].n += 16
                tev_busy[0] = dsem["tev"].n

            for t8 in range(8):
                quant_tile(fcw_in[t8 * 128:(t8 + 1) * 128, :], 2048, 1024 + t8 * 128, False)
            for t2 in range(2):
                quant_tile(prw_in[t2 * 128:(t2 + 1) * 128, :], HID, 2048 + t2 * 512, True)
            spill_tok = dsem["tev"].n

            @block.gpsimd
            def _(g):
                g.wait_ge(dsem["tev"].h, spill_tok)
                CC.inc(g.collective_compute(
                    "AllGather", ALU.bypass, replica_groups=[list(range(NCORES))],
                    ins=[wsh[1024:3072, :].opt()], outs=[wallB.opt()]))
        ccB = CC.n
        p0_ac = [AC.n]
        p0_dv = DV.n
        p0_pe = PEc.n
        p0_tev = dsem["tev"].n
        p0_wb = dsem["wb"].n
        p0_wb2 = dsem["wb2"].n

        if "xh_dump" in debug:
            xh_dump = internal("xh_dump", [HALO, D], F32)

            @block.gpsimd
            def _(g):
                g.dma_start(out=xh_dump, in_=xhalo).then_inc(dsem["cast"].h, 16)
                dsem["cast"].n += 16

        with contextlib.ExitStack() as p1:
            xsb = [p1.enter_context(nc.sbuf_tensor(f"xsb{i}", [128, D], F32)) for i in range(2)]
            xnat = p1.enter_context(nc.sbuf_tensor("xnat", [128, D], BF16))
            xnT = [p1.enter_context(nc.sbuf_tensor(f"xnT{j}", [128, 16 * 512], BF16))
                   for j in range(NCH_K)]
            wvres = p1.enter_context(nc.sbuf_tensor("wvres", [128, 16 * 2048], BF16))
            wqk = [p1.enter_context(nc.sbuf_tensor(f"wqk{i}", [128, 2048], BF16))
                   for i in range(2)]
            c2c = p1.enter_context(nc.sbuf_tensor("c2c", [128, 512], F32))
            s2c = p1.enter_context(nc.sbuf_tensor("s2c", [128, 512], F32))
            rt1 = p1.enter_context(nc.sbuf_tensor("rt1", [128, 512], F32))
            rt2 = p1.enter_context(nc.sbuf_tensor("rt2", [128, 512], F32))
            qkst = [p1.enter_context(nc.sbuf_tensor(f"qkst{i}", [128, 512], BF16))
                    for i in range(4)]
            vst = [p1.enter_context(nc.sbuf_tensor(f"vst{i}", [128, D], BF16)) for i in range(2)]
            tp_stack = contextlib.ExitStack()
            tp = tp_stack.enter_context(nc.psum_tensor("tp", [128, 2048], BF16))

            xsb_busy = [(p0_dv, p0_ac[0]), (p0_dv, p0_ac[0])]
            tp_free = 0
            xnat_pe_busy = 0
            xnT_ready = [0] * NCH_K
            for t20 in range(20):
                j, twi = t20 // 4, t20 % 4
                xb = xsb[t20 % 2]
                dx = dsem["xa"] if t20 % 2 == 0 else dsem["xb"]
                src = xhalo[t20 * 128:(t20 + 1) * 128, :] if t20 < 4 else \
                    xm[(t20 - 4) * 128:(t20 - 3) * 128, :]
                b_dv, b_ac = xsb_busy[t20 % 2]

                @block.sync
                def _(s, b_dv=b_dv, b_ac=b_ac, dx=dx):
                    s.wait_ge(DV.h, b_dv)
                    s.wait_ge(AC.h, b_ac)
                    s.wait_ge(PEc.h, p0_pe)
                    s.wait_ge(dsem["tev"].h, p0_tev)
                    s.wait_ge(dsem["wb"].h, p0_wb)
                    s.wait_ge(dsem["wb2"].h, p0_wb2)
                    s.dma_start(out=xb[:], in_=src).then_inc(dx.h, 16)
                    dx.n += 16
                xtok = dx.n

                atok = emit_ln(xb[:], [(dx, xtok), (PO, consts_tok)], xnat[:],
                               apply_waits=[(PEc, xnat_pe_busy)])
                xsb_busy[t20 % 2] = (atok, AC.n)

                if t20 < 2 and "xnat_d" in debug:
                    if t20 == 0:
                        xnat_d = internal("xnat_d", [256, D], BF16)
                        xsb_d = internal("xsb_d", [256, D], F32)
                        st_d = internal("st_d", [256, 8], F32)

                    @block.scalar
                    def _(s, t20=t20, atok=atok):
                        s.wait_ge(DV.h, atok)
                        stx = nc.allow_non_contiguous_dma(reason="debug")
                        stx.__enter__()
                        s.dma_start(out=xnat_d[t20 * 128:(t20 + 1) * 128, :],
                                    in_=xnat[:]).then_inc(dsem["tev"].h, 16)
                        dsem["tev"].n += 16
                        s.dma_start(out=xsb_d[t20 * 128:(t20 + 1) * 128, :],
                                    in_=xb[:]).then_inc(dsem["tev"].h, 16)
                        dsem["tev"].n += 16
                        slot = t20 % 2
                        s.dma_start(out=st_d[t20 * 128:(t20 + 1) * 128, 0:1],
                                    in_=mvec[:, slot:slot + 1]).then_inc(dsem["tev"].h, 16)
                        dsem["tev"].n += 16
                        s.dma_start(out=st_d[t20 * 128:(t20 + 1) * 128, 1:2],
                                    in_=rstd[:, slot:slot + 1]).then_inc(dsem["tev"].h, 16)
                        dsem["tev"].n += 16
                        s.dma_start(out=st_d[t20 * 128:(t20 + 1) * 128, 2:3],
                                    in_=sv[:, slot:slot + 1]).then_inc(dsem["tev"].h, 16)
                        dsem["tev"].n += 16
                        s.dma_start(out=st_d[t20 * 128:(t20 + 1) * 128, 3:4],
                                    in_=ssq[:, slot:slot + 1]).then_inc(dsem["tev"].h, 16)
                        dsem["tev"].n += 16
                        stx.__exit__(None, None, None)
                    spt = dsem["tev"].n

                    @block.tensor
                    def _(t, spt=spt):
                        t.wait_ge(dsem["tev"].h, spt)

                @block.tensor
                def _(t, atok=atok, tpf=tp_free):
                    t.wait_ge(DV.h, atok)
                    t.wait_ge(DV.h, tpf)
                    t.wait_ge(PO.h, consts_tok)
                    t.wait_ge(dsem["im"].h, ident_tok)
                    t.wait_ge(AC.h, p0_ac[0])
                    inst = None
                    for k in range(16):
                        inst = t.transpose(tp[:, k * 128:(k + 1) * 128],
                                           xnat[:, k * 128:(k + 1) * 128], ident[:])
                    PEc.inc(inst)
                ttok = PEc.n
                xnat_pe_busy = ttok

                @block.vector
                def _(v, j=j, twi=twi, ttok=ttok):
                    v.wait_ge(PEc.h, ttok)
                    inst = None
                    for k in range(16):
                        inst = v.tensor_copy(
                            out=xnT[j][:, k * 512 + twi * 128:k * 512 + (twi + 1) * 128],
                            in_=tp[:, k * 128:(k + 1) * 128])
                    DV.inc(inst)
                tp_free = DV.n
                xnT_ready[j] = DV.n
            passA_done = DV.n
            tp_stack.close()
            qe_ps = [p1.enter_context(nc.psum_tensor(f"qe_ps{i}", [128, 512], F32))
                     for i in range(2)]
            qo_ps = [p1.enter_context(nc.psum_tensor(f"qo_ps{i}", [128, 512], F32))
                     for i in range(2)]
            vp_ps = p1.enter_context(nc.psum_tensor("vp_ps", [128, 2048], F32))

            if "xnT0_d" in debug:
                xnT0_d = internal("xnT0_d", [128, 16 * 512], BF16)

                @block.scalar
                def _(s):
                    s.wait_ge(DV.h, xnT_ready[0])
                    s.dma_start(out=xnT0_d, in_=xnT[0][:]).then_inc(dsem["tev"].h, 16)
                    dsem["tev"].n += 16

            @block.sync
            def _(s):
                s.wait_ge(CC.h, ccA)
                s.wait_ge(DV.h, p0_dv)
                s.wait_ge(AC.h, p0_ac[0])
                s.wait_ge(PEc.h, p0_pe)
                s.wait_ge(dsem["tev"].h, p0_tev)
                s.wait_ge(dsem["wb"].h, p0_wb)
                s.wait_ge(dsem["wb2"].h, p0_wb2)
                for k in range(16):
                    r0 = _wallA1_v_rows(k)
                    s.dma_start(out=wvres[:, k * 2048:(k + 1) * 2048],
                                in_=wallA1[r0:r0 + 128, :]).then_inc(dsem["wv"].h, 16)
                    dsem["wv"].n += 16
            wv_tok = dsem["wv"].n

            wqk_use = [p0_pe, p0_pe]
            qkst_busy = [0, 0, 0, 0]
            vst_busy = [0, 0]
            tab_busy = [p0_dv]
            rope_last = [passA_done]

            for j in range(NCH_K):
                for tt in range(4):
                    ac_before = AC.n

                    @block.tensor
                    def _(t, j=j, tt=tt, ac_before=ac_before):
                        t.wait_ge(dsem["wv"].h, wv_tok)
                        t.wait_ge(DV.h, passA_done)
                        t.wait_ge(AC.h, ac_before)
                        inst = None
                        for k in range(16):
                            for dhc in range(4):
                                inst = t.matmul(
                                    vp_ps[:, dhc * 512:(dhc + 1) * 512],
                                    xnT[j][:, k * 512 + tt * 128:k * 512 + (tt + 1) * 128],
                                    wvres[:, k * 2048 + dhc * 512:k * 2048 + (dhc + 1) * 512],
                                    start=(k == 0), stop=(k == 15))
                        PEc.inc(inst)
                    vtok = PEc.n

                    @block.scalar
                    def _(s, tt=tt, vtok=vtok):
                        s.wait_ge(PEc.h, vtok)
                        s.wait_ge(dsem[f"vs{tt % 2}"].h, vst_busy[tt % 2])
                        AC.inc(s.activation(out=vst[tt % 2][:], in_=vp_ps[:], func=AF.Copy))
                    vev = AC.n

                    @block.scalar
                    def _(s, j=j, tt=tt, vev=vev):
                        s.wait_ge(AC.h, vev)
                        r0 = j * 512 + tt * 128
                        s.dma_start(out=v_d[r0:r0 + 128, :],
                                    in_=vst[tt % 2][:]).then_inc(dsem[f"vs{tt % 2}"].h, 16)
                        dsem[f"vs{tt % 2}"].n += 16
                    vst_busy[tt % 2] = dsem[f"vs{tt % 2}"].n

                @block.sync
                def _(s, j=j, tb=tab_busy[0]):
                    s.wait_ge(DV.h, tb)
                    s.wait_ge(AC.h, p0_ac[0])
                    s.wait_ge(PEc.h, p0_pe)
                    s.wait_ge(dsem["tev"].h, p0_tev)
                    s.dma_start(out=c2c[:], in_=c2_in[:, j * 512:(j + 1) * 512]).then_inc(dsem["tab"].h, 16)
                    dsem["tab"].n += 16
                    s.dma_start(out=s2c[:], in_=s2_in[:, j * 512:(j + 1) * 512]).then_inc(dsem["tab"].h, 16)
                    dsem["tab"].n += 16
                tab_tok = dsem["tab"].n

                mats = [(1, kT_d, j)]
                if j >= 1:
                    mats.append((0, qT_d, j - 1))
                for m, dst, jc in mats:
                    rope_done = [rope_last[0], rope_last[0]]
                    for g8 in range(8):
                        pp = g8 % 2
                        for eo in range(2):
                            ob = 2 * g8 + eo
                            buf = ob % 2
                            ps = qo_ps[pp] if eo else qe_ps[pp]

                            @block.sync
                            def _(s, m=m, ob=ob, buf=buf):
                                s.wait_ge(CC.h, ccA if m == 1 else ccA2)
                                s.wait_ge(PEc.h, wqk_use[buf])
                                if m == 1:
                                    wsrc, r0 = wallA1, _wallA1_k_rows(ob)
                                else:
                                    wsrc, r0 = wallA2, _wallA2_q_rows(ob)
                                s.dma_start(out=wqk[buf][:],
                                            in_=wsrc[r0:r0 + 128, :]).then_inc(dsem[f"wq{buf}"].h, 16)
                                dsem[f"wq{buf}"].n += 16
                            wtok = dsem[f"wq{buf}"].n

                            @block.tensor
                            def _(t, j=j, buf=buf, ps=ps, wtok=wtok, eo=eo,
                                  rb=rope_done[pp]):
                                t.wait_ge(dsem[f"wq{buf}"].h, wtok)
                                t.wait_ge(DV.h, passA_done)
                                if eo == 0:
                                    t.wait_ge(DV.h, rb)
                                inst = None
                                for k in range(16):
                                    inst = t.matmul(ps[:],
                                                    wqk[buf][:, k * 128:(k + 1) * 128],
                                                    xnT[j][:, k * 512:(k + 1) * 512],
                                                    start=(k == 0), stop=(k == 15))
                                PEc.inc(inst)
                            wqk_use[buf] = PEc.n
                        prtok = PEc.n

                        sE, sO = 2 * (g8 % 2), 2 * (g8 % 2) + 1
                        stE, stO = qkst[sE], qkst[sO]

                        @block.vector
                        def _(v, sE=sE, sO=sO, stE=stE, stO=stO, prtok=prtok, tab_tok=tab_tok,
                              bE=qkst_busy[sE], bO=qkst_busy[sO], dv_rope_prev=DV.n):
                            v.wait_ge(PEc.h, prtok)
                            v.wait_ge(dsem["tab"].h, tab_tok)
                            v.wait_ge(dsem[f"q{sE}"].h, bE)
                            v.wait_ge(dsem[f"q{sO}"].h, bO)
                            v.wait_ge(DV.h, dv_rope_prev)
                            DV.inc(v.tensor_tensor(out=rt1[:], in0=qe_ps[pp][:], in1=c2c[:], op=ALU.mult))
                            DV.inc(v.tensor_tensor(out=rt2[:], in0=qo_ps[pp][:], in1=s2c[:], op=ALU.mult))
                            v.wait_ge(DV.h, DV.n)
                            DV.inc(v.tensor_tensor(out=stE[:], in0=rt1[:], in1=rt2[:], op=ALU.subtract))
                            v.wait_ge(DV.h, DV.n)
                            DV.inc(v.tensor_tensor(out=rt1[:], in0=qe_ps[pp][:], in1=s2c[:], op=ALU.mult))
                            DV.inc(v.tensor_tensor(out=rt2[:], in0=qo_ps[pp][:], in1=c2c[:], op=ALU.mult))
                            v.wait_ge(DV.h, DV.n)
                            DV.inc(v.tensor_tensor(out=stO[:], in0=rt1[:], in1=rt2[:], op=ALU.add))
                        rptok = DV.n
                        rope_done[pp] = DV.n
                        rope_last[0] = DV.n
                        tab_busy[0] = DV.n

                        @block.scalar
                        def _(s, g8=g8, dst=dst, jc=jc, sE=sE, sO=sO, stE=stE, stO=stO, rptok=rptok):
                            s.wait_ge(DV.h, rptok)
                            s.dma_start(out=dst[2 * g8, :, jc * 512:(jc + 1) * 512],
                                        in_=stE[:]).then_inc(dsem[f"q{sE}"].h, 16)
                            dsem[f"q{sE}"].n += 16
                            s.dma_start(out=dst[2 * g8 + 1, :, jc * 512:(jc + 1) * 512],
                                        in_=stO[:]).then_inc(dsem[f"q{sO}"].h, 16)
                            dsem[f"q{sO}"].n += 16
                        qkst_busy[sE] = dsem[f"q{sE}"].n
                        qkst_busy[sO] = dsem[f"q{sO}"].n

        p1_spill_toks = [(dsem[f"q{i}"], dsem[f"q{i}"].n) for i in range(4)] + \
            [(dsem[f"vs{i}"], dsem[f"vs{i}"].n) for i in range(2)]

        KSL = 4
        VSL = 9
        with contextlib.ExitStack() as p2:
            wo_sb = [p2.enter_context(nc.sbuf_tensor(f"wo_sb{i}", [128, 2048], BF16))
                     for i in range(2)]
            qq = [p2.enter_context(nc.sbuf_tensor(f"qq{i}", [128, 16 * QC], BF16))
                  for i in range(2)]
            kq = p2.enter_context(nc.sbuf_tensor("kq", [128, 16 * KSL * 256], BF16))
            vq = p2.enter_context(nc.sbuf_tensor("vq", [128, VSL * 2048], BF16))
            asb = [p2.enter_context(nc.sbuf_tensor(f"asb{i}", [128, 6 * QC], BF16))
                   for i in range(2)]
            sums = p2.enter_context(nc.sbuf_tensor("sums", [128, 16 * QC], F32))
            otn = p2.enter_context(nc.sbuf_tensor("otn", [128, 16 * QC], BF16))
            att_ev = [p2.enter_context(nc.sbuf_tensor(f"att_ev{i}", [128, QC], BF16))
                      for i in range(2)]
            att_nat = p2.enter_context(nc.sbuf_tensor("att_nat", [128, 2 * D], BF16))
            x_res = p2.enter_context(nc.sbuf_tensor("x_res", [128, 2 * D], F32))
            lntmp = p2.enter_context(nc.sbuf_tensor("lntmp", [128, D], BF16))
            x1sb = [p2.enter_context(nc.sbuf_tensor(f"x1sb{i}", [128, D], F32))
                    for i in range(2)]
            sc_ps = [p2.enter_context(nc.psum_tensor(f"sc_ps{i}", [128, 6 * QC], F32))
                     for i in range(2)]
            su_ps = p2.enter_context(nc.psum_tensor("su_ps", [128, 512], F32))
            ot_ps = p2.enter_context(nc.psum_tensor("ot_ps", [128, 512], F32))
            wo_ps = [sc_ps[0][:, 0:QC], sc_ps[0][:, 512:512 + QC]]
            nt_ps = sc_ps[1][:, 0:512].bitcast(BF16)

            wo_busy = [0, 0]

            x1sb_busy = [0, 0]
            x1sb_ac = [0, 0]
            scores_done = [0] * NQC

            def emit_loads(qc, pe_guard, first=False):
                sq = dsem["p2"] if qc % 2 == 0 else dsem["p2d"]
                sk = dsem["p2b"] if qc % 2 == 0 else dsem["p2e"]
                sv_ = dsem["p2c"] if qc % 2 == 0 else dsem["p2f"]

                @block.sync
                def _(s, qc=qc, pe_guard=pe_guard, first=first, sq=sq, sk=sk):
                    if first:
                        for ds_, tok_ in p1_spill_toks:
                            s.wait_ge(ds_.h, tok_)
                    s.wait_ge(PEc.h, pe_guard)
                    for half in range(2):
                        for e in range(2):
                            src = qT_d[half::2, 64 * e:64 * e + 64,
                                       qc * QC:(qc + 1) * QC] \
                                .rearrange("g p c -> p g c")
                            dst = qq[qc % 2][64 * half:64 * half + 64, :] \
                                .rearrange("p (g e c) -> p g e c", g=8, e=2)[:, :, e, :]
                            s.dma_start(out=dst, in_=src).then_inc(sq.h, 16)
                            sq.n += 16
                    newc0 = [0, 256, 512] if qc == 0 else [qc * 256 + 512]
                    for c0 in newc0:
                        rc = (c0 // 256) % KSL
                        for half in range(2):
                            for e in range(2):
                                src = kT_d[half::2, 64 * e:64 * e + 64, c0:c0 + 256] \
                                    .rearrange("g p c -> p g c")
                                dst = kq[64 * half:64 * half + 64, :] \
                                    .rearrange("p (g e s c) -> p g e s c",
                                               e=2, s=KSL, c=256)[:, :, e, rc, :]
                                s.dma_start(out=dst, in_=src).then_inc(sk.h, 16)
                                sk.n += 16

                @block.gpsimd
                def _(g, qc=qc, pe_guard=pe_guard, first=first, sv_=sv_):
                    if first:
                        for ds_, tok_ in p1_spill_toks:
                            g.wait_ge(ds_.h, tok_)
                    g.wait_ge(PEc.h, pe_guard)
                    vr0 = [i * 128 for i in range(6)] if qc == 0 else \
                        [qc * 256 + 512, qc * 256 + 640]
                    for r0 in vr0:
                        sl = (r0 // 128) % VSL
                        g.dma_start(out=vq[:, sl * 2048:(sl + 1) * 2048],
                                    in_=v_d[r0:r0 + 128, :]).then_inc(sv_.h, 16)
                        sv_.n += 16
                return ((sq, sq.n), (sk, sk.n), (sv_, sv_.n))

            ld_toks = {0: emit_loads(0, 0, first=True),
                       1: emit_loads(1, 0, first=True)}
            prev_att_done = DV.n
            for qc in range(NQC):
                dv_pre = prev_att_done
                ld_tok = ld_toks[qc]

                AC_base = AC.n
                av_pe = [PEc.n] * HEADS
                av_dv = [0] * HEADS
                sctoks = [0] * HEADS
                aftoks = [0] * HEADS
                pend = None

                def emit_av(ph, paftok):
                    php = ph % 2
                    dv_before = DV.n

                    @block.tensor
                    def _(t, h=ph, hp=php, aftok=paftok, dv_before=dv_before, qc=qc):
                        t.wait_ge(PO.h, aftok)
                        t.wait_ge(PO.h, consts_tok)
                        t.wait_ge(DV.h, dv_before)
                        inst = None
                        for u in range(6):
                            inst = t.matmul(su_ps[:, 0:QC], ones128[:],
                                            asb[hp][:, u * QC:(u + 1) * QC],
                                            start=(u == 0), stop=(u == 5))
                        PEc.inc(inst)
                        for u in range(6):
                            vsl = (qc * 2 + u) % VSL
                            inst = t.matmul(ot_ps[:, 0:QC],
                                            vq[:, vsl * 2048 + h * 128:vsl * 2048 + (h + 1) * 128],
                                            asb[hp][:, u * QC:(u + 1) * QC],
                                            start=(u == 0), stop=(u == 5))
                        PEc.inc(inst)
                    avtok = PEc.n
                    av_pe[ph] = PEc.n

                    @block.vector
                    def _(v, h=ph, avtok=avtok):
                        v.wait_ge(PEc.h, avtok)
                        DV.inc(v.tensor_copy(out=sums[:, h * QC:(h + 1) * QC],
                                             in_=su_ps[:, 0:QC]))
                        DV.inc(v.tensor_copy(out=otn[:, h * QC:(h + 1) * QC],
                                             in_=ot_ps[:, 0:QC]))
                    av_dv[ph] = DV.n

                for h in range(HEADS):
                    g8, eo = h // 2, h % 2
                    hp = h % 2
                    pb = 64 * eo
                    scp = sc_ps[hp]
                    eb_pre = AC_base + h - 1 if h >= 2 else AC_base

                    @block.tensor
                    def _(t, qc=qc, h=h, ld_tok=ld_tok, scp=scp, eb=eb_pre,
                          dv_pre=dv_pre):
                        if h == 0:
                            t.wait_ge(dsem["im"].h, ident_tok)
                            t.wait_ge(ld_tok[0][0].h, ld_tok[0][1])
                            t.wait_ge(ld_tok[1][0].h, ld_tok[1][1])
                            t.wait_ge(ld_tok[2][0].h, ld_tok[2][1])
                        t.wait_ge(AC.h, eb)
                        if h < 2:
                            t.wait_ge(DV.h, dv_pre)
                        inst = None
                        for u in range(6):
                            sl = scp[:, u * QC:(u + 1) * QC]
                            c0 = qc * QC + u * 128
                            kc0 = ((c0 // 256) % KSL) * 256 + (c0 % 256)
                            ops = [(kq[:, h * KSL * 256 + kc0:h * KSL * 256 + kc0 + 128],
                                    qq[qc % 2][:, h * QC:(h + 1) * QC])]
                            if qc + (u // 2) < 2:
                                ops.append((maskc_sb[:], onesrow[:]))
                            for oi, (lhsT, rhs) in enumerate(ops):
                                inst = t.matmul(sl, lhsT, rhs,
                                                start=(u % 2 == 0 and oi == 0),
                                                stop=(u % 2 == 1 and oi == len(ops) - 1))
                        PEc.inc(inst)
                    sctoks[h] = PEc.n
                    extok = AC_base + h + 1
                    if pend is not None:
                        emit_av(*pend)

                    @block.gpsimd
                    def _(g, extok=extok, hp=hp):
                        g.wait_ge(AC.h, extok)
                        for u in (0, 1):
                            PO.inc(g.affine_select(out=asb[hp][:, u * QC:(u + 1) * QC],
                                                   in_=asb[hp][:, u * QC:(u + 1) * QC],
                                                   compare_op=ALU.is_gt, fill=0.0,
                                                   base=128 * u, pattern=[[-1, QC]],
                                                   channel_multiplier=1))
                        for u in (4, 5):
                            PO.inc(g.affine_select(out=asb[hp][:, u * QC:(u + 1) * QC],
                                                   in_=asb[hp][:, u * QC:(u + 1) * QC],
                                                   compare_op=ALU.is_ge, fill=0.0,
                                                   base=-128 * (u - 4), pattern=[[1, QC]],
                                                   channel_multiplier=-1))
                    aftoks[h] = PO.n
                    pend = (h, aftoks[h])

                emit_av(*pend)
                pend = None
                scores_done[qc] = PEc.n

                @block.scalar
                def _(s, AC_base=AC_base):
                    assert AC.n == AC_base
                    for h in range(HEADS):
                        hp = h % 2
                        s.wait_ge(PEc.h, sctoks[h])
                        if h >= 2:
                            s.wait_ge(PEc.h, av_pe[h - 2])
                        AC.inc(s.activation(out=asb[hp][:], in_=sc_ps[hp][:],
                                            func=AF.Exp, scale=SCALE_QK))

                half_toks = []
                for hf in range(2):
                    @block.scalar
                    def _(s, hf=hf, dvtok=av_dv[8 * hf + 7]):
                        s.wait_ge(DV.h, dvtok)
                        sl = sums[:, hf * 8 * QC:(hf + 1) * 8 * QC]
                        AC.inc(s.activation(out=sl, in_=sl, func=AF.Ln))
                        s.wait_ge(AC.h, AC.n)
                        AC.inc(s.activation(out=sl, in_=sl, func=AF.Exp, scale=-1.0))
                    half_toks.append(AC.n)

                for hf in range(2):
                    @block.vector
                    def _(v, hf=hf, actok=half_toks[hf]):
                        v.wait_ge(AC.h, actok)
                        inst = None
                        for h in range(8 * hf, 8 * hf + 8):
                            inst = v.tensor_tensor(out=otn[:, h * QC:(h + 1) * QC],
                                                   in0=otn[:, h * QC:(h + 1) * QC],
                                                   in1=sums[:, h * QC:(h + 1) * QC],
                                                   op=ALU.mult)
                        DV.inc(inst)
                norm_tok = DV.n


                dv_before_x = DV.n

                @block.sync
                def _(s, qc=qc, dv_before_x=dv_before_x):
                    s.wait_ge(DV.h, dv_before_x)
                    s.dma_start(out=x_res[:].rearrange("p (a d) -> p a d", a=2),
                                in_=xm[qc * QC:(qc + 1) * QC, :]
                                .rearrange("(a p) d -> p a d", p=128)).then_inc(dsem["xr"].h, 16)
                    dsem["xr"].n += 16
                xr_tok = dsem["xr"].n

                ev_toks = [0] * 16

                def emit_tr(pdo):
                    dv_b2 = DV.n

                    @block.tensor
                    def _(t, pdo=pdo, dv_b2=dv_b2):
                        t.wait_ge(DV.h, ev_toks[pdo])
                        t.wait_ge(DV.h, dv_b2)
                        PEc.inc(t.transpose(nt_ps[:, 0:128],
                                            att_ev[pdo % 2][:, 0:128], ident[:]))
                        PEc.inc(t.transpose(nt_ps[:, 128:256],
                                            att_ev[pdo % 2][:, 128:256], ident[:]))
                    trtok = PEc.n

                    @block.vector
                    def _(v, pdo=pdo, trtok=trtok):
                        v.wait_ge(PEc.h, trtok)
                        DV.inc(v.tensor_copy(
                            out=att_nat[:, pdo * 128:pdo * 128 + 128],
                            in_=nt_ps[:, 0:128]))
                        DV.inc(v.tensor_copy(
                            out=att_nat[:, D + pdo * 128:D + pdo * 128 + 128],
                            in_=nt_ps[:, 128:256]))

                for do in range(16):
                    wsem = dsem["xa"] if do % 2 == 0 else dsem["xb"]

                    @block.sync
                    def _(s, do=do, wb=wo_busy[do % 2], wsem=wsem):
                        s.wait_ge(CC.h, ccA2)
                        s.wait_ge(PEc.h, wb)
                        r0 = _wallA2_o_rows(do)
                        s.dma_start(out=wo_sb[do % 2][:],
                                    in_=wallA2[r0:r0 + 128, :]).then_inc(wsem.h, 16)
                        wsem.n += 16
                    wo_tok = wsem.n

                    @block.tensor
                    def _(t, do=do, norm_tok=norm_tok, wo_tok=wo_tok, wsem=wsem,
                          evb=ev_toks[do - 2] if do >= 2 else 0):
                        t.wait_ge(DV.h, norm_tok)
                        t.wait_ge(wsem.h, wo_tok)
                        t.wait_ge(DV.h, evb)
                        inst = None
                        for k in range(16):
                            inst = t.matmul(wo_ps[do % 2],
                                            wo_sb[do % 2][:, k * 128:(k + 1) * 128],
                                            otn[:, k * QC:(k + 1) * QC],
                                            start=(k == 0), stop=(k == 15))
                        PEc.inc(inst)
                    wotok = PEc.n
                    wo_busy[do % 2] = PEc.n

                    @block.vector
                    def _(v, do=do, wotok=wotok):
                        v.wait_ge(PEc.h, wotok)
                        DV.inc(v.tensor_copy(out=att_ev[do % 2][:], in_=wo_ps[do % 2]))
                    ev_toks[do] = DV.n

                    if do >= 1:
                        emit_tr(do - 1)
                emit_tr(15)
                att_done = DV.n
                prev_att_done = att_done

                if qc + 2 < NQC:
                    ld_toks[qc + 2] = emit_loads(qc + 2, scores_done[qc])

                lnst = emit_ln_stats([att_nat[:, 0:D], att_nat[:, D:2 * D]],
                                     [(DV, att_done), (dsem["xr"], xr_tok)])
                for qsub in range(2):
                    a_ap = att_nat[:, qsub * D:(qsub + 1) * D]
                    atok = emit_ln_apply(a_ap, lnst[qsub], lntmp[:])
                    idx = qc * 2 + qsub

                    xsem = dsem["x1s"] if qsub % 2 == 0 else dsem["x1t"]

                    @block.vector
                    def _(v, qsub=qsub, idx=idx, atok=atok, xsem=xsem,
                          xb=x1sb_busy[qsub % 2], xa=x1sb_ac[qsub % 2]):
                        v.wait_ge(DV.h, atok)
                        v.wait_ge(xsem.h, xb)
                        v.wait_ge(AC.h, xa)
                        DV.inc(v.tensor_tensor(out=x1sb[qsub % 2][:],
                                               in0=x_res[:, qsub * D:(qsub + 1) * D],
                                               in1=lntmp[:], op=ALU.add))
                        v.wait_ge(DV.h, DV.n)
                        DV.inc(v.tensor_reduce(out=sv3[:, idx:idx + 1],
                                               in_=x1sb[qsub % 2][:],
                                               axis=AX, op=ALU.add))
                    x1tok = DV.n - 1

                    @block.scalar
                    def _(s, qc=qc, qsub=qsub, idx=idx, x1tok=x1tok):
                        s.wait_ge(DV.h, x1tok)
                        r0 = qc * QC + qsub * 128
                        s.dma_start(out=x1_d[r0:r0 + 128, :], in_=x1sb[qsub % 2][:]).then_inc(xsem.h, 16)
                        xsem.n += 16
                        s.wait_ge(AC.h, AC.n)
                        AC.inc(s.activation(out=sqscr[:], in_=x1sb[qsub % 2][:],
                                            func=AF.Square,
                                            accum_out=ssq3[:, idx:idx + 1]))
                    x1sb_busy[qsub % 2] = xsem.n
                    x1sb_ac[qsub % 2] = AC.n

        p2_spills = dsem["x1s"].n
        p2_spills_t = dsem["x1t"].n
        p2_stats_ac = AC.n

        with contextlib.ExitStack() as p3:
            x1c = p3.enter_context(nc.sbuf_tensor("x1c", [128, 4 * D], F32))
            xn2T = p3.enter_context(nc.sbuf_tensor("xn2T", [128, 16 * 512], BF16))
            Hsb = p3.enter_context(nc.sbuf_tensor("Hsb", [128, NHT * 512], BF16))
            fcb = [p3.enter_context(nc.sbuf_tensor(f"fcb{i}", [128, 2048], BF16))
                   for i in range(2)]
            prb = [p3.enter_context(nc.sbuf_tensor(f"prb{i}", [128, 8192], BF16))
                   for i in range(2)]
            qsb = [p3.enter_context(nc.sbuf_tensor(f"qsb{i}", [128, 512], BF16))
                   for i in range(2)]
            g2b = [p3.enter_context(nc.sbuf_tensor(f"g2b{i}", [128, 512], BF16))
                   for i in range(4)]
            rssb = p3.enter_context(nc.sbuf_tensor("rssb", [128, 512], F32))
            otb = [p3.enter_context(nc.sbuf_tensor(f"otb{i}", [128, 512], BF16))
                   for i in range(2)]
            fnat = p3.enter_context(nc.sbuf_tensor("fnat", [128, 4 * D], BF16))
            xnat3 = p3.enter_context(nc.sbuf_tensor("xnat3", [128, D], BF16))
            ysb = [p3.enter_context(nc.sbuf_tensor(f"ysb{i}", [128, D], F32))
                   for i in range(2)]
            tp3 = p3.enter_context(nc.psum_tensor("tp3", [128, 2048], BF16))
            hp_ps = [p3.enter_context(nc.psum_tensor(f"hp{i}", [128, 512], F32))
                     for i in range(2)]
            ms_ps = p3.enter_context(nc.psum_tensor("ms_ps", [128, 512], F32))
            op_ps = [p3.enter_context(nc.psum_tensor(f"op{i}", [128, 512], F32))
                     for i in range(2)]

            @block.vector
            def _(v):
                v.wait_ge(AC.h, p2_stats_ac)
                DV.inc(v.tensor_scalar(out=m3[:], in0=sv3[:], scalar1=1.0 / D,
                                       scalar2=None, op0=ALU.mult))
                DV.inc(v.tensor_scalar(out=r3[:], in0=ssq3[:], scalar1=1.0 / D,
                                       scalar2=None, op0=ALU.mult))
                v.wait_ge(DV.h, DV.n)
                DV.inc(v.tensor_tensor(out=sv3[:], in0=m3[:], in1=m3[:], op=ALU.mult))
                v.wait_ge(DV.h, DV.n)
                DV.inc(v.tensor_tensor(out=r3[:], in0=r3[:], in1=sv3[:], op=ALU.subtract))
            var3_tok = DV.n

            @block.scalar
            def _(s):
                s.wait_ge(DV.h, var3_tok)
                AC.inc(s.activation(out=r3[:], in_=r3[:], func=AF.Ln, bias=eps_ln[:]))
                s.wait_ge(AC.h, AC.n)
                AC.inc(s.activation(out=r3[:], in_=r3[:], func=AF.Exp, scale=-0.5))
            r3_tok = AC.n

            fcb_busy = [0, 0]
            prb_busy = [0, 0]
            qsb_ac = [0, 0]
            qsb_dv = [0, 0]
            g2b_busy = [0, 0, 0, 0]
            g2_toks = [0] * NHT
            otb_busy = [0, 0]
            ysb_busy = [0, 0]
            ms_read = 0
            xnat3_pe_busy = [0]

            for ch in range(4):
                pe_cs = PEc.n
                ac_cs = AC.n
                dv_cs = DV.n

                @block.sync
                def _(s, ch=ch, dv_cs=dv_cs):
                    s.wait_ge(dsem["x1s"].h, p2_spills)
                    s.wait_ge(dsem["x1t"].h, p2_spills_t)
                    s.wait_ge(DV.h, dv_cs)
                    s.dma_start(out=x1c[:].rearrange("p (a d) -> p a d", a=4),
                                in_=x1_d[ch * 512:(ch + 1) * 512, :]
                                .rearrange("(a p) d -> p a d", p=128)).then_inc(dsem["xc"].h, 16)
                    dsem["xc"].n += 16
                x1tok = dsem["xc"].n

                tp_free3 = DV.n
                for i4 in range(4):
                    idx = ch * 4 + i4

                    @block.vector
                    def _(v, i4=i4, idx=idx, x1tok=x1tok):
                        v.wait_ge(dsem["xc"].h, x1tok)
                        v.wait_ge(AC.h, max(r3_tok, ac_cs))
                        v.wait_ge(PEc.h, pe_cs)
                        DV.inc(v.tensor_scalar(out=fnat[:, i4 * D:(i4 + 1) * D],
                                               in0=x1c[:, i4 * D:(i4 + 1) * D],
                                               scalar1=m3[:, idx:idx + 1],
                                               scalar2=r3[:, idx:idx + 1],
                                               op0=ALU.subtract, op1=ALU.mult))
                    atok = DV.n

                    @block.tensor
                    def _(t, i4=i4, atok=atok, tpf=tp_free3):
                        t.wait_ge(DV.h, atok)
                        t.wait_ge(DV.h, tpf)
                        inst = None
                        for k in range(16):
                            inst = t.transpose(tp3[:, k * 128:(k + 1) * 128],
                                               fnat[:, i4 * D + k * 128:i4 * D + (k + 1) * 128],
                                               ident[:])
                        PEc.inc(inst)
                    ttok = PEc.n
                    xnat3_pe_busy[0] = ttok

                    @block.vector
                    def _(v, i4=i4, ttok=ttok, pe_cs=pe_cs):
                        v.wait_ge(PEc.h, ttok)
                        v.wait_ge(PEc.h, pe_cs)
                        inst = None
                        for k in range(16):
                            inst = v.tensor_copy(
                                out=xn2T[:, k * 512 + i4 * 128:k * 512 + (i4 + 1) * 128],
                                in_=tp3[:, k * 128:(k + 1) * 128])
                        DV.inc(inst)
                    tp_free3 = DV.n
                xn2_tok = DV.n

                AC3 = AC.n
                htoks = [0] * NHT
                qdv_guards = [0] * NHT
                for ht in range(NHT):
                    hb = ht % 2

                    @block.sync
                    def _(s, ht=ht, hb=hb, fb=fcb_busy[hb]):
                        s.wait_ge(CC.h, ccB)
                        s.wait_ge(PEc.h, fb)
                        r0 = _wallB_fc_rows(ht)
                        s.dma_start(out=fcb[hb][:], in_=wallB[r0:r0 + 128, :]).then_inc(dsem[f"f{hb}"].h, 16)
                        dsem[f"f{hb}"].n += 16
                    ftok = dsem[f"f{hb}"].n

                    ab_pre = AC3 + ht - 1 if ht >= 2 else AC3

                    @block.tensor
                    def _(t, hb=hb, ftok=ftok, xn2_tok=xn2_tok, ab=ab_pre, mr=ms_read):
                        t.wait_ge(dsem[f"f{hb}"].h, ftok)
                        t.wait_ge(DV.h, xn2_tok)
                        t.wait_ge(AC.h, ab)
                        t.wait_ge(AC.h, mr)
                        inst = None
                        for k in range(16):
                            inst = t.matmul(hp_ps[hb][:],
                                            fcb[hb][:, k * 128:(k + 1) * 128],
                                            xn2T[:, k * 512:(k + 1) * 512],
                                            start=(k == 0), stop=(k == 15))
                        PEc.inc(inst)
                    fcb_busy[hb] = PEc.n
                    htoks[ht] = PEc.n
                    qtok = AC3 + ht + 1

                    gq = ht % 4

                    @block.vector
                    def _(v, ht=ht, hb=hb, gq=gq, qtok=qtok, gb=g2b_busy[gq], pe_cs=pe_cs):
                        v.wait_ge(AC.h, qtok)
                        v.wait_ge(PEc.h, pe_cs)
                        DV.inc(v.tensor_tensor(out=Hsb[:, ht * 512:(ht + 1) * 512],
                                               in0=qsb[hb][:], in1=qsb[hb][:], op=ALU.mult))
                        qsb_dv[hb] = DV.n
                        v.wait_ge(DV.h, DV.n)
                        v.wait_ge(PEc.h, gb)
                        DV.inc(v.tensor_tensor(out=g2b[gq][:],
                                               in0=Hsb[:, ht * 512:(ht + 1) * 512],
                                               in1=Hsb[:, ht * 512:(ht + 1) * 512],
                                               op=ALU.mult))
                    g2_toks[ht] = DV.n
                    qdv_guards[ht] = qsb_dv[hb]

                    if ht >= 2:
                        hd = ht - 2

                        @block.tensor
                        def _(t, hd=hd):
                            t.wait_ge(DV.h, g2_toks[hd])
                            PEc.inc(t.matmul(ms_ps[:], ones128[:], g2b[hd % 4][:],
                                             start=(hd == 0), stop=False))
                        g2b_busy[hd % 4] = PEc.n
                for hd in (NHT - 2, NHT - 1):
                    @block.tensor
                    def _(t, hd=hd):
                        t.wait_ge(DV.h, g2_toks[hd])
                        PEc.inc(t.matmul(ms_ps[:], ones128[:], g2b[hd % 4][:],
                                         start=False, stop=(hd == NHT - 1)))
                    g2b_busy[hd % 4] = PEc.n
                ms_tok = PEc.n

                @block.scalar
                def _(s, AC3=AC3):
                    assert AC.n == AC3
                    for ht in range(NHT):
                        hb = ht % 2
                        s.wait_ge(PEc.h, htoks[ht])
                        if ht >= 2:
                            s.wait_ge(DV.h, qdv_guards[ht - 2])
                        AC.inc(s.activation(out=qsb[hb][:], in_=hp_ps[hb][:],
                                            func=AF.Relu))

                @block.scalar
                def _(s, ms_tok=ms_tok, dv_cs=dv_cs):
                    s.wait_ge(PEc.h, ms_tok)
                    s.wait_ge(DV.h, dv_cs)
                    AC.inc(s.activation(out=rssb[:], in_=ms_ps[:], func=AF.Ln,
                                        bias=eps_rms[:], scale=1.0 / HID))
                    s.wait_ge(AC.h, AC.n)
                    AC.inc(s.activation(out=rssb[:], in_=rssb[:], func=AF.Exp, scale=-0.5))
                rs_tok = AC.n
                ms_read = AC.n

                h_tok = DV.n
                for do in range(16):
                    db = do % 2

                    @block.sync
                    def _(s, do=do, db=db, pbz=prb_busy[db]):
                        s.wait_ge(CC.h, ccB)
                        s.wait_ge(PEc.h, pbz)
                        r0 = _wallB_pr_rows(do)
                        s.dma_start(out=prb[db][:],
                                    in_=wallB[r0:r0 + 512, :]
                                    .rearrange("(p a) c -> p (a c)", p=128)).then_inc(dsem[f"pr{db}"].h, 16)
                        dsem[f"pr{db}"].n += 16
                    ptok = dsem[f"pr{db}"].n

                    @block.tensor
                    def _(t, db=db, ptok=ptok, h_tok=h_tok, ob=otb_busy[db]):
                        t.wait_ge(dsem[f"pr{db}"].h, ptok)
                        t.wait_ge(DV.h, h_tok)
                        t.wait_ge(DV.h, ob)
                        inst = None
                        for hc in range(NHT):
                            inst = t.matmul(op_ps[db][:],
                                            prb[db][:, hc * 128:(hc + 1) * 128],
                                            Hsb[:, hc * 512:(hc + 1) * 512],
                                            start=(hc == 0), stop=(hc == NHT - 1))
                        PEc.inc(inst)
                    prb_busy[db] = PEc.n
                    optok = PEc.n

                    @block.vector
                    def _(v, db=db, optok=optok, rs_tok=rs_tok, pe_ob=otb_busy[db]):
                        v.wait_ge(PEc.h, optok)
                        v.wait_ge(AC.h, rs_tok)
                        v.wait_ge(PEc.h, pe_ob)
                        DV.inc(v.tensor_tensor(out=otb[db][:], in0=op_ps[db][:],
                                               in1=rssb[:], op=ALU.mult))
                    ottok = DV.n

                    dv_b3 = DV.n

                    @block.tensor
                    def _(t, db=db, ottok=ottok, dv_b3=dv_b3):
                        t.wait_ge(DV.h, ottok)
                        t.wait_ge(DV.h, dv_b3)
                        inst = None
                        for i4 in range(4):
                            inst = t.transpose(tp3[:, i4 * 128:(i4 + 1) * 128],
                                               otb[db][:, i4 * 128:(i4 + 1) * 128],
                                               ident[:])
                        PEc.inc(inst)
                    otb_busy[db] = PEc.n
                    trtok = PEc.n

                    @block.vector
                    def _(v, do=do, trtok=trtok):
                        v.wait_ge(PEc.h, trtok)
                        inst = None
                        for i4 in range(4):
                            inst = v.tensor_copy(
                                out=fnat[:, i4 * D + do * 128:i4 * D + (do + 1) * 128],
                                in_=tp3[:, i4 * 128:(i4 + 1) * 128])
                        DV.inc(inst)
                fn_tok = DV.n

                lnst3 = emit_ln_stats([fnat[:, i * D:(i + 1) * D] for i in range(4)],
                                      [(DV, fn_tok)])
                for i4 in range(4):
                    yb = i4 % 2
                    atok = emit_ln_apply(fnat[:, i4 * D:(i4 + 1) * D], lnst3[i4], xnat3[:])

                    @block.vector
                    def _(v, i4=i4, yb=yb, atok=atok, ybb=ysb_busy[yb]):
                        v.wait_ge(DV.h, atok)
                        v.wait_ge(dsem[f"y{yb}"].h, ybb)
                        DV.inc(v.tensor_tensor(out=ysb[yb][:],
                                               in0=x1c[:, i4 * D:(i4 + 1) * D],
                                               in1=xnat3[:], op=ALU.add))
                    ytok = DV.n

                    @block.scalar
                    def _(s, ch=ch, i4=i4, yb=yb, ytok=ytok):
                        s.wait_ge(DV.h, ytok)
                        r0 = ch * 512 + i4 * 128
                        s.dma_start(out=y[r0:r0 + 128, :], in_=ysb[yb][:]).then_inc(dsem[f"y{yb}"].h, 16)
                        dsem[f"y{yb}"].n += 16
                    ysb_busy[yb] = dsem[f"y{yb}"].n

        y0_done, y1_done = dsem["y0"].n, dsem["y1"].n

        @block.sync
        def _(s):
            s.wait_ge(dsem["y0"].h, y0_done)
            s.wait_ge(dsem["y1"].h, y1_done)

    return nc


_NC_CACHE = {}


def kernel(L0_attn_x, L0_attn_Wq, L0_attn_Wk, L0_attn_Wv, L0_attn_Wo,
           L0_ffn_W_fc, L0_ffn_W_proj):
    in_maps = _host_prep(L0_attn_x, L0_attn_Wq, L0_attn_Wk, L0_attn_Wv,
                         L0_attn_Wo, L0_ffn_W_fc, L0_ffn_W_proj)
    if "nc" not in _NC_CACHE:
        _NC_CACHE["nc"] = build_nc()
    nc = _NC_CACHE["nc"]
    res = run_bass_kernel_spmd(nc, in_maps, core_ids=list(range(NCORES)))
    out = np.empty((4, 4096, 2048), dtype=np.float32)
    for c in range(NCORES):
        b, h = c // 2, c % 2
        out[b, h * S_CORE:(h + 1) * S_CORE, :] = res.results[c]["y"]
    return out


# version 26 (final)
# speedup vs baseline: 1.0670x; 1.0062x over previous
import contextlib
import numpy as np
import ml_dtypes
import concourse.bass as bass
import concourse.mybir as mybir
from concourse.bass_utils import run_bass_kernel_spmd

F32 = mybir.dt.float32
BF16 = mybir.dt.bfloat16
AF = mybir.ActivationFunctionType
ALU = mybir.AluOpType
AX = mybir.AxisListType.X

D = 2048
HEADS = 16
S_CORE = 2048
HALO = 512
SK = S_CORE + HALO
NCH_K = 5
QC = 256
NQC = S_CORE // QC
HID = 8192
NHT = 64
MAGIC = 12582912.0
LN_EPS = 1e-5
RMS_EPS = float(np.finfo(np.float32).eps)
SCALE_QK = 1.0 / float(np.sqrt(128.0))
NEG = -1e30
NCORES = 8


def _perm_eo():
    perm = np.zeros(D, dtype=np.int64)
    for g in range(8):
        b = 256 * g
        perm[b:b + 64] = (2 * g) * 128 + 2 * np.arange(64)
        perm[b + 64:b + 128] = (2 * g + 1) * 128 + 2 * np.arange(64)
        perm[b + 128:b + 192] = (2 * g) * 128 + 2 * np.arange(64) + 1
        perm[b + 192:b + 256] = (2 * g + 1) * 128 + 2 * np.arange(64) + 1
    return perm


def _slabs_from_T(WT):
    return np.ascontiguousarray(
        WT.reshape(16, 128, 16, 128).transpose(2, 1, 0, 3).reshape(16, 128, 2048))


def _host_prep(L0_attn_x, L0_attn_Wq, L0_attn_Wk, L0_attn_Wv, L0_attn_Wo,
               L0_ffn_W_fc, L0_ffn_W_proj):
    X = np.asarray(L0_attn_x, dtype=np.float32)
    perm = _perm_eo()
    slab_q = _slabs_from_T(np.ascontiguousarray(np.asarray(L0_attn_Wq, np.float32).T)[:, perm])
    slab_k = _slabs_from_T(np.ascontiguousarray(np.asarray(L0_attn_Wk, np.float32).T)[:, perm])
    slab_v = np.ascontiguousarray(np.asarray(L0_attn_Wv, np.float32).T.reshape(16, 128, 2048))
    slab_o = _slabs_from_T(np.ascontiguousarray(np.asarray(L0_attn_Wo, np.float32).T))
    parts = []
    for c in range(8):
        parts += [slab_k[2 * c:2 * c + 2].reshape(-1, 2048),
                  slab_v[2 * c:2 * c + 2].reshape(-1, 2048),
                  slab_q[2 * c:2 * c + 2].reshape(-1, 2048),
                  slab_o[2 * c:2 * c + 2].reshape(-1, 2048)]
    attn_flat = np.concatenate(parts, axis=0)
    Wfc = np.asarray(L0_ffn_W_fc, np.float32)
    Wpr = np.asarray(L0_ffn_W_proj, np.float32)
    inv = 10000.0 ** (-2.0 * np.arange(64) / 128.0)

    in_maps = []
    for c in range(NCORES):
        b, h = c // 2, c % 2
        start = h * S_CORE
        xm = np.ascontiguousarray(X[b, start:start + S_CORE])
        xhalo = np.ascontiguousarray(X[b, start - HALO:start] if h == 1 else X[b, 0:HALO])
        pos = (start - HALO) + np.arange(SK)
        th = pos[None, :] * inv[:, None]
        c2 = np.concatenate([np.cos(th), np.cos(th)], axis=0).astype(np.float32)
        s2 = np.concatenate([np.sin(th), np.sin(th)], axis=0).astype(np.float32)
        maskcol = np.full((1, 128), 0.0 if h == 1 else NEG, dtype=np.float32)
        in_maps.append({
            "identin": np.eye(128, dtype=np.float32),
            "xm": xm, "xh": xhalo,
            "attnw": np.ascontiguousarray(attn_flat[c * 1024:(c + 1) * 1024]),
            "fcw": np.ascontiguousarray(Wfc[c * 1024:(c + 1) * 1024]),
            "prw": np.ascontiguousarray(Wpr[c * 256:(c + 1) * 256]),
            "c2": c2, "s2": s2, "maskcol": maskcol,
        })
    return in_maps


class Ctr:
    def __init__(self, stack, nc, name):
        self.h = stack.enter_context(nc.semaphore(name))
        self.n = 0

    def inc(self, inst, k=1):
        inst.then_inc(self.h, k)
        self.n += k
        return self.n


def _wallA1_k_rows(i):
    return (i // 2) * 512 + (i % 2) * 128


def _wallA1_v_rows(i):
    return (i // 2) * 512 + 256 + (i % 2) * 128


def _wallA2_q_rows(i):
    return (i // 2) * 512 + (i % 2) * 128


def _wallA2_o_rows(i):
    return (i // 2) * 512 + 256 + (i % 2) * 128


def _wallB_fc_rows(ht):
    return (ht // 8) * 2048 + (ht % 8) * 128


def _wallB_pr_rows(do):
    return (do // 2) * 2048 + 1024 + (do % 2) * 512


def build_nc(debug=()):
    nc = bass.Bass("TRN2", target_bir_lowering=False)

    xm = nc.dram_tensor("xm", [S_CORE, D], F32, kind="ExternalInput").ap()
    xhalo = nc.dram_tensor("xh", [HALO, D], F32, kind="ExternalInput").ap()
    attnw = nc.dram_tensor("attnw", [1024, D], F32, kind="ExternalInput").ap()
    fcw_in = nc.dram_tensor("fcw", [1024, D], F32, kind="ExternalInput").ap()
    prw_in = nc.dram_tensor("prw", [256, HID], F32, kind="ExternalInput").ap()
    c2_in = nc.dram_tensor("c2", [128, SK], F32, kind="ExternalInput").ap()
    s2_in = nc.dram_tensor("s2", [128, SK], F32, kind="ExternalInput").ap()
    maskcol_in = nc.dram_tensor("maskcol", [1, 128], F32, kind="ExternalInput").ap()
    ident_in = nc.dram_tensor("identin", [128, 128], F32, kind="ExternalInput").ap()
    y = nc.dram_tensor("y", [S_CORE, D], F32, kind="ExternalOutput").ap()

    def internal(name, shape, dtype):
        kind = "ExternalOutput" if name in debug else "Internal"
        return nc.dram_tensor(name, shape, dtype, kind=kind).ap()

    wsh = internal("wsh", [3072, D], BF16)
    wallA1 = internal("wallA1", [4096, D], BF16)
    wallA2 = internal("wallA2", [4096, D], BF16)
    wallB = internal("wallB", [16384, D], BF16)
    qT_d = internal("qT_d", [16, 128, S_CORE], BF16)
    kT_d = internal("kT_d", [16, 128, SK], BF16)
    v_d = internal("v_d", [SK, D], BF16)
    x1_d = internal("x1_d", [S_CORE, D], F32)

    st = contextlib.ExitStack()
    with st:
        ident = st.enter_context(nc.sbuf_tensor("ident", [128, 128], BF16))
        ones128 = st.enter_context(nc.sbuf_tensor("ones128", [128, 128], BF16))
        onesrow = st.enter_context(nc.sbuf_tensor("onesrow", [1, QC], BF16))
        maskc_sb = st.enter_context(nc.sbuf_tensor("maskc_sb", [1, 128], BF16))
        eps_ln = st.enter_context(nc.sbuf_tensor("eps_ln", [128, 1], F32))
        eps_rms = st.enter_context(nc.sbuf_tensor("eps_rms", [128, 1], F32))
        sv = st.enter_context(nc.sbuf_tensor("sv", [128, 4], F32))
        ssq = st.enter_context(nc.sbuf_tensor("ssq", [128, 4], F32))
        sv3 = st.enter_context(nc.sbuf_tensor("sv3", [128, 16], F32))
        ssq3 = st.enter_context(nc.sbuf_tensor("ssq3", [128, 16], F32))
        m3 = st.enter_context(nc.sbuf_tensor("m3", [128, 16], F32))
        r3 = st.enter_context(nc.sbuf_tensor("r3", [128, 16], F32))
        mvec = st.enter_context(nc.sbuf_tensor("mvec", [128, 4], F32))
        vvec = st.enter_context(nc.sbuf_tensor("vvec", [128, 4], F32))
        rstd = st.enter_context(nc.sbuf_tensor("rstd", [128, 4], F32))
        sqscr = st.enter_context(nc.sbuf_tensor("sqscr", [128, D], BF16))

        PEc = Ctr(st, nc, "sPE")
        DV = Ctr(st, nc, "sDV")
        AC = Ctr(st, nc, "sAC")
        PO = Ctr(st, nc, "sPO")
        CC = Ctr(st, nc, "sCC")
        dsem = {n: Ctr(st, nc, "d_" + n) for n in (
            "xa", "xb", "wv", "tab", "wq0", "wq1",
            "q0", "q1", "q2", "q3", "vs0", "vs1",
            "cast", "wb", "tev",
            "p2", "wo", "xr", "x1s",
            "xc", "f0", "f1", "pr0", "pr1", "y0", "y1", "p2b", "p2c", "wb2",
            "im", "p2d", "p2e", "p2f", "x1t")}

        block = st.enter_context(nc.Block())

        ln_state = {"slot_busy": [(0, 0)] * 4, "next": 0}

        def emit_ln_stats(xs, pre_waits):
            k = len(xs)
            slots = [(ln_state["next"] + i) % 4 for i in range(k)]
            ln_state["next"] = (ln_state["next"] + k) % 4
            busys = [ln_state["slot_busy"][s] for s in slots]

            @block.vector
            def _(v):
                for (ct, tok) in pre_waits:
                    v.wait_ge(ct.h, tok)
                for (b_dv, b_ac) in busys:
                    v.wait_ge(DV.h, b_dv)
                    v.wait_ge(AC.h, b_ac)
                for x_ap, s_ in zip(xs, slots):
                    DV.inc(v.tensor_reduce(out=sv[:, s_:s_ + 1], in_=x_ap,
                                           axis=AX, op=ALU.add))
            rtok = DV.n

            @block.scalar
            def _(s):
                for (ct, tok) in pre_waits:
                    s.wait_ge(ct.h, tok)
                for (b_dv, b_ac) in busys:
                    s.wait_ge(DV.h, b_dv)
                for x_ap, s_ in zip(xs, slots):
                    s.wait_ge(AC.h, AC.n)
                    AC.inc(s.activation(out=sqscr[:], in_=x_ap, func=AF.Square,
                                        accum_out=ssq[:, s_:s_ + 1]))
            sqtok = AC.n

            @block.vector
            def _(v):
                v.wait_ge(DV.h, rtok)
                v.wait_ge(AC.h, sqtok)
                for s_ in slots:
                    DV.inc(v.tensor_scalar(out=mvec[:, s_:s_ + 1], in0=sv[:, s_:s_ + 1],
                                           scalar1=1.0 / D, scalar2=None, op0=ALU.mult))
                    DV.inc(v.tensor_scalar(out=vvec[:, s_:s_ + 1], in0=ssq[:, s_:s_ + 1],
                                           scalar1=1.0 / D, scalar2=None, op0=ALU.mult))
                v.wait_ge(DV.h, DV.n)
                for s_ in slots:
                    DV.inc(v.tensor_tensor(out=rstd[:, s_:s_ + 1], in0=mvec[:, s_:s_ + 1],
                                           in1=mvec[:, s_:s_ + 1], op=ALU.mult))
                v.wait_ge(DV.h, DV.n)
                for s_ in slots:
                    DV.inc(v.tensor_tensor(out=vvec[:, s_:s_ + 1], in0=vvec[:, s_:s_ + 1],
                                           in1=rstd[:, s_:s_ + 1], op=ALU.subtract))
            vartok = DV.n

            @block.scalar
            def _(s):
                s.wait_ge(DV.h, vartok)
                for s_ in slots:
                    AC.inc(s.activation(out=rstd[:, s_:s_ + 1], in_=vvec[:, s_:s_ + 1],
                                        func=AF.Ln, bias=eps_ln[:]))
                s.wait_ge(AC.h, AC.n)
                for s_ in slots:
                    AC.inc(s.activation(out=rstd[:, s_:s_ + 1], in_=rstd[:, s_:s_ + 1],
                                        func=AF.Exp, scale=-0.5))
            rstok = AC.n
            return [(s_, rstok) for s_ in slots]

        def emit_ln_apply(x_ap, slot_tok, out_ap, apply_waits=()):
            slot, rstok = slot_tok

            @block.vector
            def _(v):
                v.wait_ge(AC.h, rstok)
                for (ct, tok) in apply_waits:
                    v.wait_ge(ct.h, tok)
                DV.inc(v.tensor_scalar(out=out_ap, in0=x_ap,
                                       scalar1=mvec[:, slot:slot + 1],
                                       scalar2=rstd[:, slot:slot + 1],
                                       op0=ALU.subtract, op1=ALU.mult))
            ln_state["slot_busy"][slot] = (DV.n, rstok)
            return DV.n

        def emit_ln(x_ap, pre_waits, out_ap, apply_waits=()):
            st_ = emit_ln_stats([x_ap], pre_waits)[0]
            return emit_ln_apply(x_ap, st_, out_ap, apply_waits)

        @block.gpsimd
        def _(g):
            g.dma_start(out=ident[:], in_=ident_in).then_inc(dsem["im"].h, 16)
            dsem["im"].n += 16
            PO.inc(g.memset(ones128[:], 1.0))
            PO.inc(g.memset(onesrow[:], 1.0))
            PO.inc(g.memset(eps_ln[:], LN_EPS))
            PO.inc(g.memset(eps_rms[:], RMS_EPS))
            g.dma_start(out=wsh[0:1024, :], in_=attnw).then_inc(dsem["cast"].h, 16)
            dsem["cast"].n += 16
            g.dma_start(out=maskc_sb[:], in_=maskcol_in).then_inc(dsem["im"].h, 16)
            dsem["im"].n += 16
        consts_tok = PO.n
        ident_tok = 32
        cast_tok = dsem["cast"].n

        @block.gpsimd
        def _(g):
            g.wait_ge(dsem["cast"].h, cast_tok)
            CC.inc(g.collective_compute(
                "AllGather", ALU.bypass, replica_groups=[list(range(NCORES))],
                ins=[wsh[0:512, :].opt()], outs=[wallA1.opt()]))
        ccA = CC.n

        @block.gpsimd
        def _(g):
            g.wait_ge(dsem["cast"].h, cast_tok)
            CC.inc(g.collective_compute(
                "AllGather", ALU.bypass, replica_groups=[list(range(NCORES))],
                ins=[wsh[512:1024, :].opt()], outs=[wallA2.opt()]))
        ccA2 = CC.n

        with contextlib.ExitStack() as p0:
            wbb = [p0.enter_context(nc.sbuf_tensor(f"q_wb{i}", [128, HID], BF16))
                   for i in range(2)]
            gsum = p0.enter_context(nc.sbuf_tensor("q_gs", [128, 128], F32))
            scb = p0.enter_context(nc.sbuf_tensor("q_sc", [128, 128], BF16))
            rr = p0.enter_context(nc.sbuf_tensor("q_rr", [128, 128], F32))
            qs_sb = p0.enter_context(nc.sbuf_tensor("q_qs", [128, HID], BF16))
            tev = p0.enter_context(nc.sbuf_tensor("q_tev", [128, HID], BF16))
            qp = p0.enter_context(nc.psum_tensor("q_qp", [128, 2048], BF16))

            q_pe_busy = [0]
            tev_busy = [0]
            q_toks = []

            def quant_tile(src_ap, ncols, dst_row0, is_proj):
                ng = ncols // 64
                ti = len(q_toks)
                wb = wbb[ti % 2]
                wsem = dsem["wb"] if ti % 2 == 0 else dsem["wb2"]
                wb_guard = q_toks[ti - 2] if ti >= 2 else 0

                @block.gpsimd
                def _(g):
                    g.wait_ge(DV.h, wb_guard)
                    g.dma_start(out=wb[:, 0:ncols], in_=src_ap).then_inc(wsem.h, 16)
                    wsem.n += 16
                ld = wsem.n

                @block.vector
                def _(v):
                    v.wait_ge(wsem.h, ld)
                    v.wait_ge(PEc.h, q_pe_busy[0])
                    DV.inc(v.tensor_reduce(
                        out=gsum[:, 0:ng],
                        in_=wb[:, 0:ncols].rearrange("p (g k) -> p g k", k=64),
                        axis=AX, op=ALU.add, apply_absolute_value=True))
                    v.wait_ge(DV.h, DV.n)
                    DV.inc(v.tensor_scalar(out=scb[:, 0:ng], in0=gsum[:, 0:ng],
                                           scalar1=1.0 / 64.0, scalar2=1e-8,
                                           op0=ALU.mult, op1=ALU.max))
                    v.wait_ge(DV.h, DV.n)
                    DV.inc(v.reciprocal(out=rr[:, 0:ng], in_=scb[:, 0:ng]))
                    v.wait_ge(DV.h, DV.n)
                    DV.inc(v.tensor_tensor(
                        out=wb[:, 0:ncols].rearrange("p (g k) -> p g k", k=64),
                        in0=wb[:, 0:ncols].rearrange("p (g k) -> p g k", k=64),
                        in1=rr[:, 0:ng].unsqueeze(2).broadcast_to([128, ng, 64]),
                        op=ALU.mult))
                    v.wait_ge(DV.h, DV.n)
                    DV.inc(v.tensor_scalar(out=wb[:, 0:ncols], in0=wb[:, 0:ncols],
                                           scalar1=MAGIC, scalar2=MAGIC,
                                           op0=ALU.add, op1=ALU.subtract))
                    v.wait_ge(DV.h, DV.n)
                    DV.inc(v.tensor_scalar(out=wb[:, 0:ncols], in0=wb[:, 0:ncols],
                                           scalar1=1.0, scalar2=-1.0,
                                           op0=ALU.min, op1=ALU.max))
                    v.wait_ge(DV.h, DV.n)
                    DV.inc(v.tensor_tensor(
                        out=qs_sb[:, 0:ncols].rearrange("p (g k) -> p g k", k=64),
                        in0=wb[:, 0:ncols].rearrange("p (g k) -> p g k", k=64),
                        in1=scb[:, 0:ng].unsqueeze(2).broadcast_to([128, ng, 64]),
                        op=ALU.mult))
                qtok = DV.n
                q_toks.append(qtok)

                for grp in range(ncols // 2048):
                    ac_before = AC.n

                    @block.tensor
                    def _(t, grp=grp, ac_before=ac_before):
                        t.wait_ge(DV.h, qtok)
                        t.wait_ge(AC.h, ac_before)
                        t.wait_ge(PO.h, consts_tok)
                        t.wait_ge(dsem["im"].h, ident_tok)
                        inst = None
                        for kk in range(16):
                            k = grp * 16 + kk
                            inst = t.transpose(qp[:, kk * 128:(kk + 1) * 128],
                                               qs_sb[:, k * 128:(k + 1) * 128], ident[:])
                        PEc.inc(inst)
                    ttok = PEc.n
                    q_pe_busy[0] = ttok

                    @block.scalar
                    def _(s, grp=grp, ttok=ttok):
                        s.wait_ge(PEc.h, ttok)
                        s.wait_ge(dsem["tev"].h, tev_busy[0])
                        AC.inc(s.activation(out=tev[:, grp * 2048:(grp + 1) * 2048],
                                            in_=qp[:], func=AF.Copy))
                evtok = AC.n

                @block.scalar
                def _(s):
                    s.wait_ge(AC.h, evtok)
                    if is_proj:
                        out_ap = wsh[dst_row0:dst_row0 + 512, :].rearrange(
                            "(p a) c -> p (a c)", p=128)
                    else:
                        out_ap = wsh[dst_row0:dst_row0 + 128, :]
                    s.dma_start(out=out_ap, in_=tev[:, 0:ncols]).then_inc(dsem["tev"].h, 16)
                    dsem["tev"].n += 16
                tev_busy[0] = dsem["tev"].n

            for t8 in range(8):
                quant_tile(fcw_in[t8 * 128:(t8 + 1) * 128, :], 2048, 1024 + t8 * 128, False)
            for t2 in range(2):
                quant_tile(prw_in[t2 * 128:(t2 + 1) * 128, :], HID, 2048 + t2 * 512, True)
            spill_tok = dsem["tev"].n

            @block.gpsimd
            def _(g):
                g.wait_ge(dsem["tev"].h, spill_tok)
                CC.inc(g.collective_compute(
                    "AllGather", ALU.bypass, replica_groups=[list(range(NCORES))],
                    ins=[wsh[1024:3072, :].opt()], outs=[wallB.opt()]))
        ccB = CC.n
        p0_ac = [AC.n]
        p0_dv = DV.n
        p0_pe = PEc.n
        p0_tev = dsem["tev"].n
        p0_wb = dsem["wb"].n
        p0_wb2 = dsem["wb2"].n

        if "xh_dump" in debug:
            xh_dump = internal("xh_dump", [HALO, D], F32)

            @block.gpsimd
            def _(g):
                g.dma_start(out=xh_dump, in_=xhalo).then_inc(dsem["cast"].h, 16)
                dsem["cast"].n += 16

        with contextlib.ExitStack() as p1:
            xsb = [p1.enter_context(nc.sbuf_tensor(f"xsb{i}", [128, D], F32)) for i in range(2)]
            xnat = p1.enter_context(nc.sbuf_tensor("xnat", [128, D], BF16))
            xnT = [p1.enter_context(nc.sbuf_tensor(f"xnT{j}", [128, 16 * 512], BF16))
                   for j in range(NCH_K)]
            wvres = p1.enter_context(nc.sbuf_tensor("wvres", [128, 16 * 2048], BF16))
            wqk = [p1.enter_context(nc.sbuf_tensor(f"wqk{i}", [128, 2048], BF16))
                   for i in range(2)]
            c2c = p1.enter_context(nc.sbuf_tensor("c2c", [128, 512], F32))
            s2c = p1.enter_context(nc.sbuf_tensor("s2c", [128, 512], F32))
            rt1 = p1.enter_context(nc.sbuf_tensor("rt1", [128, 512], F32))
            rt2 = p1.enter_context(nc.sbuf_tensor("rt2", [128, 512], F32))
            qkst = [p1.enter_context(nc.sbuf_tensor(f"qkst{i}", [128, 512], BF16))
                    for i in range(4)]
            vst = [p1.enter_context(nc.sbuf_tensor(f"vst{i}", [128, D], BF16)) for i in range(2)]
            tp_stack = contextlib.ExitStack()
            tp = tp_stack.enter_context(nc.psum_tensor("tp", [128, 2048], BF16))

            xsb_busy = [(p0_dv, p0_ac[0]), (p0_dv, p0_ac[0])]
            tp_free = 0
            xnat_pe_busy = 0
            xnT_ready = [0] * NCH_K
            for t20 in range(20):
                j, twi = t20 // 4, t20 % 4
                xb = xsb[t20 % 2]
                dx = dsem["xa"] if t20 % 2 == 0 else dsem["xb"]
                src = xhalo[t20 * 128:(t20 + 1) * 128, :] if t20 < 4 else \
                    xm[(t20 - 4) * 128:(t20 - 3) * 128, :]
                b_dv, b_ac = xsb_busy[t20 % 2]

                @block.sync
                def _(s, b_dv=b_dv, b_ac=b_ac, dx=dx):
                    s.wait_ge(DV.h, b_dv)
                    s.wait_ge(AC.h, b_ac)
                    s.wait_ge(PEc.h, p0_pe)
                    s.wait_ge(dsem["tev"].h, p0_tev)
                    s.wait_ge(dsem["wb"].h, p0_wb)
                    s.wait_ge(dsem["wb2"].h, p0_wb2)
                    s.dma_start(out=xb[:], in_=src).then_inc(dx.h, 16)
                    dx.n += 16
                xtok = dx.n

                atok = emit_ln(xb[:], [(dx, xtok), (PO, consts_tok)], xnat[:],
                               apply_waits=[(PEc, xnat_pe_busy)])
                xsb_busy[t20 % 2] = (atok, AC.n)

                if t20 < 2 and "xnat_d" in debug:
                    if t20 == 0:
                        xnat_d = internal("xnat_d", [256, D], BF16)
                        xsb_d = internal("xsb_d", [256, D], F32)
                        st_d = internal("st_d", [256, 8], F32)

                    @block.scalar
                    def _(s, t20=t20, atok=atok):
                        s.wait_ge(DV.h, atok)
                        stx = nc.allow_non_contiguous_dma(reason="debug")
                        stx.__enter__()
                        s.dma_start(out=xnat_d[t20 * 128:(t20 + 1) * 128, :],
                                    in_=xnat[:]).then_inc(dsem["tev"].h, 16)
                        dsem["tev"].n += 16
                        s.dma_start(out=xsb_d[t20 * 128:(t20 + 1) * 128, :],
                                    in_=xb[:]).then_inc(dsem["tev"].h, 16)
                        dsem["tev"].n += 16
                        slot = t20 % 2
                        s.dma_start(out=st_d[t20 * 128:(t20 + 1) * 128, 0:1],
                                    in_=mvec[:, slot:slot + 1]).then_inc(dsem["tev"].h, 16)
                        dsem["tev"].n += 16
                        s.dma_start(out=st_d[t20 * 128:(t20 + 1) * 128, 1:2],
                                    in_=rstd[:, slot:slot + 1]).then_inc(dsem["tev"].h, 16)
                        dsem["tev"].n += 16
                        s.dma_start(out=st_d[t20 * 128:(t20 + 1) * 128, 2:3],
                                    in_=sv[:, slot:slot + 1]).then_inc(dsem["tev"].h, 16)
                        dsem["tev"].n += 16
                        s.dma_start(out=st_d[t20 * 128:(t20 + 1) * 128, 3:4],
                                    in_=ssq[:, slot:slot + 1]).then_inc(dsem["tev"].h, 16)
                        dsem["tev"].n += 16
                        stx.__exit__(None, None, None)
                    spt = dsem["tev"].n

                    @block.tensor
                    def _(t, spt=spt):
                        t.wait_ge(dsem["tev"].h, spt)

                @block.tensor
                def _(t, atok=atok, tpf=tp_free):
                    t.wait_ge(DV.h, atok)
                    t.wait_ge(DV.h, tpf)
                    t.wait_ge(PO.h, consts_tok)
                    t.wait_ge(dsem["im"].h, ident_tok)
                    t.wait_ge(AC.h, p0_ac[0])
                    inst = None
                    for k in range(16):
                        inst = t.transpose(tp[:, k * 128:(k + 1) * 128],
                                           xnat[:, k * 128:(k + 1) * 128], ident[:])
                    PEc.inc(inst)
                ttok = PEc.n
                xnat_pe_busy = ttok

                @block.vector
                def _(v, j=j, twi=twi, ttok=ttok):
                    v.wait_ge(PEc.h, ttok)
                    inst = None
                    for k in range(16):
                        inst = v.tensor_copy(
                            out=xnT[j][:, k * 512 + twi * 128:k * 512 + (twi + 1) * 128],
                            in_=tp[:, k * 128:(k + 1) * 128])
                    DV.inc(inst)
                tp_free = DV.n
                xnT_ready[j] = DV.n
            passA_done = DV.n
            tp_stack.close()
            qe_ps = [p1.enter_context(nc.psum_tensor(f"qe_ps{i}", [128, 512], F32))
                     for i in range(2)]
            qo_ps = [p1.enter_context(nc.psum_tensor(f"qo_ps{i}", [128, 512], F32))
                     for i in range(2)]
            vp_ps = p1.enter_context(nc.psum_tensor("vp_ps", [128, 2048], F32))

            if "xnT0_d" in debug:
                xnT0_d = internal("xnT0_d", [128, 16 * 512], BF16)

                @block.scalar
                def _(s):
                    s.wait_ge(DV.h, xnT_ready[0])
                    s.dma_start(out=xnT0_d, in_=xnT[0][:]).then_inc(dsem["tev"].h, 16)
                    dsem["tev"].n += 16

            @block.sync
            def _(s):
                s.wait_ge(CC.h, ccA)
                s.wait_ge(DV.h, p0_dv)
                s.wait_ge(AC.h, p0_ac[0])
                s.wait_ge(PEc.h, p0_pe)
                s.wait_ge(dsem["tev"].h, p0_tev)
                s.wait_ge(dsem["wb"].h, p0_wb)
                s.wait_ge(dsem["wb2"].h, p0_wb2)
                for k in range(16):
                    r0 = _wallA1_v_rows(k)
                    s.dma_start(out=wvres[:, k * 2048:(k + 1) * 2048],
                                in_=wallA1[r0:r0 + 128, :]).then_inc(dsem["wv"].h, 16)
                    dsem["wv"].n += 16
            wv_tok = dsem["wv"].n

            wqk_use = [p0_pe, p0_pe]
            qkst_busy = [0, 0, 0, 0]
            vst_busy = [0, 0]
            tab_busy = [p0_dv]
            rope_last = [passA_done]

            for j in range(NCH_K):
                for tt in range(4):
                    ac_before = AC.n

                    @block.tensor
                    def _(t, j=j, tt=tt, ac_before=ac_before):
                        t.wait_ge(dsem["wv"].h, wv_tok)
                        t.wait_ge(DV.h, passA_done)
                        t.wait_ge(AC.h, ac_before)
                        inst = None
                        for k in range(16):
                            for dhc in range(4):
                                inst = t.matmul(
                                    vp_ps[:, dhc * 512:(dhc + 1) * 512],
                                    xnT[j][:, k * 512 + tt * 128:k * 512 + (tt + 1) * 128],
                                    wvres[:, k * 2048 + dhc * 512:k * 2048 + (dhc + 1) * 512],
                                    start=(k == 0), stop=(k == 15))
                        PEc.inc(inst)
                    vtok = PEc.n

                    @block.scalar
                    def _(s, tt=tt, vtok=vtok):
                        s.wait_ge(PEc.h, vtok)
                        s.wait_ge(dsem[f"vs{tt % 2}"].h, vst_busy[tt % 2])
                        AC.inc(s.activation(out=vst[tt % 2][:], in_=vp_ps[:], func=AF.Copy))
                    vev = AC.n

                    @block.scalar
                    def _(s, j=j, tt=tt, vev=vev):
                        s.wait_ge(AC.h, vev)
                        r0 = j * 512 + tt * 128
                        s.dma_start(out=v_d[r0:r0 + 128, :],
                                    in_=vst[tt % 2][:]).then_inc(dsem[f"vs{tt % 2}"].h, 16)
                        dsem[f"vs{tt % 2}"].n += 16
                    vst_busy[tt % 2] = dsem[f"vs{tt % 2}"].n

                @block.sync
                def _(s, j=j, tb=tab_busy[0]):
                    s.wait_ge(DV.h, tb)
                    s.wait_ge(AC.h, p0_ac[0])
                    s.wait_ge(PEc.h, p0_pe)
                    s.wait_ge(dsem["tev"].h, p0_tev)
                    s.dma_start(out=c2c[:], in_=c2_in[:, j * 512:(j + 1) * 512]).then_inc(dsem["tab"].h, 16)
                    dsem["tab"].n += 16
                    s.dma_start(out=s2c[:], in_=s2_in[:, j * 512:(j + 1) * 512]).then_inc(dsem["tab"].h, 16)
                    dsem["tab"].n += 16
                tab_tok = dsem["tab"].n

                mats = [(1, kT_d, j)]
                if j >= 1:
                    mats.append((0, qT_d, j - 1))
                for m, dst, jc in mats:
                    rope_done = [rope_last[0], rope_last[0]]
                    for g8 in range(8):
                        pp = g8 % 2
                        for eo in range(2):
                            ob = 2 * g8 + eo
                            buf = ob % 2
                            ps = qo_ps[pp] if eo else qe_ps[pp]

                            @block.sync
                            def _(s, m=m, ob=ob, buf=buf):
                                s.wait_ge(CC.h, ccA if m == 1 else ccA2)
                                s.wait_ge(PEc.h, wqk_use[buf])
                                if m == 1:
                                    wsrc, r0 = wallA1, _wallA1_k_rows(ob)
                                else:
                                    wsrc, r0 = wallA2, _wallA2_q_rows(ob)
                                s.dma_start(out=wqk[buf][:],
                                            in_=wsrc[r0:r0 + 128, :]).then_inc(dsem[f"wq{buf}"].h, 16)
                                dsem[f"wq{buf}"].n += 16
                            wtok = dsem[f"wq{buf}"].n

                            @block.tensor
                            def _(t, j=j, buf=buf, ps=ps, wtok=wtok, eo=eo,
                                  rb=rope_done[pp]):
                                t.wait_ge(dsem[f"wq{buf}"].h, wtok)
                                t.wait_ge(DV.h, passA_done)
                                if eo == 0:
                                    t.wait_ge(DV.h, rb)
                                inst = None
                                for k in range(16):
                                    inst = t.matmul(ps[:],
                                                    wqk[buf][:, k * 128:(k + 1) * 128],
                                                    xnT[j][:, k * 512:(k + 1) * 512],
                                                    start=(k == 0), stop=(k == 15))
                                PEc.inc(inst)
                            wqk_use[buf] = PEc.n
                        prtok = PEc.n

                        sE, sO = 2 * (g8 % 2), 2 * (g8 % 2) + 1
                        stE, stO = qkst[sE], qkst[sO]

                        @block.vector
                        def _(v, sE=sE, sO=sO, stE=stE, stO=stO, prtok=prtok, tab_tok=tab_tok,
                              bE=qkst_busy[sE], bO=qkst_busy[sO], dv_rope_prev=DV.n):
                            v.wait_ge(PEc.h, prtok)
                            v.wait_ge(dsem["tab"].h, tab_tok)
                            v.wait_ge(dsem[f"q{sE}"].h, bE)
                            v.wait_ge(dsem[f"q{sO}"].h, bO)
                            v.wait_ge(DV.h, dv_rope_prev)
                            DV.inc(v.tensor_tensor(out=rt1[:], in0=qe_ps[pp][:], in1=c2c[:], op=ALU.mult))
                            DV.inc(v.tensor_tensor(out=rt2[:], in0=qo_ps[pp][:], in1=s2c[:], op=ALU.mult))
                            v.wait_ge(DV.h, DV.n)
                            DV.inc(v.tensor_tensor(out=stE[:], in0=rt1[:], in1=rt2[:], op=ALU.subtract))
                            v.wait_ge(DV.h, DV.n)
                            DV.inc(v.tensor_tensor(out=rt1[:], in0=qe_ps[pp][:], in1=s2c[:], op=ALU.mult))
                            DV.inc(v.tensor_tensor(out=rt2[:], in0=qo_ps[pp][:], in1=c2c[:], op=ALU.mult))
                            v.wait_ge(DV.h, DV.n)
                            DV.inc(v.tensor_tensor(out=stO[:], in0=rt1[:], in1=rt2[:], op=ALU.add))
                        rptok = DV.n
                        rope_done[pp] = DV.n
                        rope_last[0] = DV.n
                        tab_busy[0] = DV.n

                        @block.scalar
                        def _(s, g8=g8, dst=dst, jc=jc, sE=sE, sO=sO, stE=stE, stO=stO, rptok=rptok):
                            s.wait_ge(DV.h, rptok)
                            s.dma_start(out=dst[2 * g8, :, jc * 512:(jc + 1) * 512],
                                        in_=stE[:]).then_inc(dsem[f"q{sE}"].h, 16)
                            dsem[f"q{sE}"].n += 16
                            s.dma_start(out=dst[2 * g8 + 1, :, jc * 512:(jc + 1) * 512],
                                        in_=stO[:]).then_inc(dsem[f"q{sO}"].h, 16)
                            dsem[f"q{sO}"].n += 16
                        qkst_busy[sE] = dsem[f"q{sE}"].n
                        qkst_busy[sO] = dsem[f"q{sO}"].n

        p1_spill_toks = [(dsem[f"q{i}"], dsem[f"q{i}"].n) for i in range(4)] + \
            [(dsem[f"vs{i}"], dsem[f"vs{i}"].n) for i in range(2)]

        KSL = 4
        VSL = 9
        with contextlib.ExitStack() as p2:
            wo_sb = [p2.enter_context(nc.sbuf_tensor(f"wo_sb{i}", [128, 2048], BF16))
                     for i in range(2)]
            qq = [p2.enter_context(nc.sbuf_tensor(f"qq{i}", [128, 16 * QC], BF16))
                  for i in range(2)]
            kq = p2.enter_context(nc.sbuf_tensor("kq", [128, 16 * KSL * 256], BF16))
            vq = p2.enter_context(nc.sbuf_tensor("vq", [128, VSL * 2048], BF16))
            asb = [p2.enter_context(nc.sbuf_tensor(f"asb{i}", [128, 6 * QC], BF16))
                   for i in range(2)]
            sums = p2.enter_context(nc.sbuf_tensor("sums", [128, 16 * QC], F32))
            otn = p2.enter_context(nc.sbuf_tensor("otn", [128, 16 * QC], BF16))
            att_ev = [p2.enter_context(nc.sbuf_tensor(f"att_ev{i}", [128, QC], BF16))
                      for i in range(2)]
            att_nat = p2.enter_context(nc.sbuf_tensor("att_nat", [128, 2 * D], BF16))
            x_res = p2.enter_context(nc.sbuf_tensor("x_res", [128, 2 * D], F32))
            lntmp = p2.enter_context(nc.sbuf_tensor("lntmp", [128, D], BF16))
            x1sb = [p2.enter_context(nc.sbuf_tensor(f"x1sb{i}", [128, D], F32))
                    for i in range(2)]
            sc_ps = [p2.enter_context(nc.psum_tensor(f"sc_ps{i}", [128, 6 * QC], F32))
                     for i in range(2)]
            su_ps = p2.enter_context(nc.psum_tensor("su_ps", [128, 512], F32))
            ot_ps = p2.enter_context(nc.psum_tensor("ot_ps", [128, 512], F32))
            wo_ps = [sc_ps[0][:, 0:QC], sc_ps[0][:, 512:512 + QC]]
            nt_ps = sc_ps[1][:, 0:512].bitcast(BF16)

            wo_busy = [0, 0]

            x1sb_busy = [0, 0]
            x1sb_ac = [0, 0]
            scores_done = [0] * NQC

            def emit_loads(qc, pe_guard, first=False):
                sq = dsem["p2"] if qc % 2 == 0 else dsem["p2d"]
                sk = dsem["p2b"] if qc % 2 == 0 else dsem["p2e"]
                sv_ = dsem["p2c"] if qc % 2 == 0 else dsem["p2f"]

                @block.sync
                def _(s, qc=qc, pe_guard=pe_guard, first=first, sq=sq, sk=sk):
                    if first:
                        for ds_, tok_ in p1_spill_toks:
                            s.wait_ge(ds_.h, tok_)
                    s.wait_ge(PEc.h, pe_guard)
                    for half in range(2):
                        for e in range(2):
                            src = qT_d[half::2, 64 * e:64 * e + 64,
                                       qc * QC:(qc + 1) * QC] \
                                .rearrange("g p c -> p g c")
                            dst = qq[qc % 2][64 * half:64 * half + 64, :] \
                                .rearrange("p (g e c) -> p g e c", g=8, e=2)[:, :, e, :]
                            s.dma_start(out=dst, in_=src).then_inc(sq.h, 16)
                            sq.n += 16
                    newc0 = [0, 256, 512] if qc == 0 else [qc * 256 + 512]
                    for c0 in newc0:
                        rc = (c0 // 256) % KSL
                        for half in range(2):
                            for e in range(2):
                                src = kT_d[half::2, 64 * e:64 * e + 64, c0:c0 + 256] \
                                    .rearrange("g p c -> p g c")
                                dst = kq[64 * half:64 * half + 64, :] \
                                    .rearrange("p (g e s c) -> p g e s c",
                                               e=2, s=KSL, c=256)[:, :, e, rc, :]
                                s.dma_start(out=dst, in_=src).then_inc(sk.h, 16)
                                sk.n += 16

                @block.gpsimd
                def _(g, qc=qc, pe_guard=pe_guard, first=first, sv_=sv_):
                    if first:
                        for ds_, tok_ in p1_spill_toks:
                            g.wait_ge(ds_.h, tok_)
                    g.wait_ge(PEc.h, pe_guard)
                    vr0 = [i * 128 for i in range(6)] if qc == 0 else \
                        [qc * 256 + 512, qc * 256 + 640]
                    for r0 in vr0:
                        sl = (r0 // 128) % VSL
                        g.dma_start(out=vq[:, sl * 2048:(sl + 1) * 2048],
                                    in_=v_d[r0:r0 + 128, :]).then_inc(sv_.h, 16)
                        sv_.n += 16
                return ((sq, sq.n), (sk, sk.n), (sv_, sv_.n))

            ld_toks = {0: emit_loads(0, 0, first=True),
                       1: emit_loads(1, 0, first=True)}
            prev_att_done = DV.n
            for qc in range(NQC):
                dv_pre = prev_att_done
                ld_tok = ld_toks[qc]

                AC_base = AC.n
                av_pe = [PEc.n] * HEADS
                av_dv = [0] * HEADS
                sctoks = [0] * HEADS
                aftoks = [0] * HEADS
                pend = None

                def emit_av(ph, paftok):
                    php = ph % 2
                    dv_before = DV.n

                    @block.tensor
                    def _(t, h=ph, hp=php, aftok=paftok, dv_before=dv_before, qc=qc):
                        t.wait_ge(PO.h, aftok)
                        t.wait_ge(PO.h, consts_tok)
                        t.wait_ge(DV.h, dv_before)
                        inst = None
                        for u in range(6):
                            inst = t.matmul(su_ps[:, 0:QC], ones128[:],
                                            asb[hp][:, u * QC:(u + 1) * QC],
                                            start=(u == 0), stop=(u == 5))
                        PEc.inc(inst)
                        for u in range(6):
                            vsl = (qc * 2 + u) % VSL
                            inst = t.matmul(ot_ps[:, 0:QC],
                                            vq[:, vsl * 2048 + h * 128:vsl * 2048 + (h + 1) * 128],
                                            asb[hp][:, u * QC:(u + 1) * QC],
                                            start=(u == 0), stop=(u == 5))
                        PEc.inc(inst)
                    avtok = PEc.n
                    av_pe[ph] = PEc.n

                    @block.vector
                    def _(v, h=ph, avtok=avtok):
                        v.wait_ge(PEc.h, avtok)
                        DV.inc(v.tensor_copy(out=sums[:, h * QC:(h + 1) * QC],
                                             in_=su_ps[:, 0:QC]))
                        DV.inc(v.tensor_copy(out=otn[:, h * QC:(h + 1) * QC],
                                             in_=ot_ps[:, 0:QC]))
                    av_dv[ph] = DV.n

                for h in range(HEADS):
                    g8, eo = h // 2, h % 2
                    hp = h % 2
                    pb = 64 * eo
                    scp = sc_ps[hp]
                    eb_pre = AC_base + h - 1 if h >= 2 else AC_base

                    @block.tensor
                    def _(t, qc=qc, h=h, ld_tok=ld_tok, scp=scp, eb=eb_pre,
                          dv_pre=dv_pre):
                        if h == 0:
                            t.wait_ge(dsem["im"].h, ident_tok)
                            t.wait_ge(ld_tok[0][0].h, ld_tok[0][1])
                            t.wait_ge(ld_tok[1][0].h, ld_tok[1][1])
                            t.wait_ge(ld_tok[2][0].h, ld_tok[2][1])
                        t.wait_ge(AC.h, eb)
                        if h < 2:
                            t.wait_ge(DV.h, dv_pre)
                        inst = None
                        for u in range(6):
                            sl = scp[:, u * QC:(u + 1) * QC]
                            c0 = qc * QC + u * 128
                            kc0 = ((c0 // 256) % KSL) * 256 + (c0 % 256)
                            ops = [(kq[:, h * KSL * 256 + kc0:h * KSL * 256 + kc0 + 128],
                                    qq[qc % 2][:, h * QC:(h + 1) * QC])]
                            if qc + (u // 2) < 2:
                                ops.append((maskc_sb[:], onesrow[:]))
                            for oi, (lhsT, rhs) in enumerate(ops):
                                inst = t.matmul(sl, lhsT, rhs,
                                                start=(u % 2 == 0 and oi == 0),
                                                stop=(u % 2 == 1 and oi == len(ops) - 1))
                        PEc.inc(inst)
                    sctoks[h] = PEc.n
                    extok = AC_base + h + 1
                    if pend is not None:
                        emit_av(*pend)

                    @block.gpsimd
                    def _(g, extok=extok, hp=hp):
                        g.wait_ge(AC.h, extok)
                        for u in (0, 1):
                            PO.inc(g.affine_select(out=asb[hp][:, u * QC:(u + 1) * QC],
                                                   in_=asb[hp][:, u * QC:(u + 1) * QC],
                                                   compare_op=ALU.is_gt, fill=0.0,
                                                   base=128 * u, pattern=[[-1, QC]],
                                                   channel_multiplier=1))
                        for u in (4, 5):
                            PO.inc(g.affine_select(out=asb[hp][:, u * QC:(u + 1) * QC],
                                                   in_=asb[hp][:, u * QC:(u + 1) * QC],
                                                   compare_op=ALU.is_ge, fill=0.0,
                                                   base=-128 * (u - 4), pattern=[[1, QC]],
                                                   channel_multiplier=-1))
                    aftoks[h] = PO.n
                    pend = (h, aftoks[h])

                emit_av(*pend)
                pend = None
                scores_done[qc] = PEc.n

                @block.scalar
                def _(s, AC_base=AC_base):
                    assert AC.n == AC_base
                    for h in range(HEADS):
                        hp = h % 2
                        s.wait_ge(PEc.h, sctoks[h])
                        if h >= 2:
                            s.wait_ge(PEc.h, av_pe[h - 2])
                        AC.inc(s.activation(out=asb[hp][:], in_=sc_ps[hp][:],
                                            func=AF.Exp, scale=SCALE_QK))

                half_toks = []
                for hf in range(2):
                    @block.scalar
                    def _(s, hf=hf, dvtok=av_dv[8 * hf + 7]):
                        s.wait_ge(DV.h, dvtok)
                        sl = sums[:, hf * 8 * QC:(hf + 1) * 8 * QC]
                        AC.inc(s.activation(out=sl, in_=sl, func=AF.Ln))
                        s.wait_ge(AC.h, AC.n)
                        AC.inc(s.activation(out=sl, in_=sl, func=AF.Exp, scale=-1.0))
                    half_toks.append(AC.n)

                for hf in range(2):
                    @block.vector
                    def _(v, hf=hf, actok=half_toks[hf]):
                        v.wait_ge(AC.h, actok)
                        inst = None
                        for h in range(8 * hf, 8 * hf + 8):
                            inst = v.tensor_tensor(out=otn[:, h * QC:(h + 1) * QC],
                                                   in0=otn[:, h * QC:(h + 1) * QC],
                                                   in1=sums[:, h * QC:(h + 1) * QC],
                                                   op=ALU.mult)
                        DV.inc(inst)
                norm_tok = DV.n


                dv_before_x = DV.n

                @block.sync
                def _(s, qc=qc, dv_before_x=dv_before_x):
                    s.wait_ge(DV.h, dv_before_x)
                    s.dma_start(out=x_res[:].rearrange("p (a d) -> p a d", a=2),
                                in_=xm[qc * QC:(qc + 1) * QC, :]
                                .rearrange("(a p) d -> p a d", p=128)).then_inc(dsem["xr"].h, 16)
                    dsem["xr"].n += 16
                xr_tok = dsem["xr"].n

                ev_toks = [0] * 16

                def emit_tr(pdo):
                    dv_b2 = DV.n

                    @block.tensor
                    def _(t, pdo=pdo, dv_b2=dv_b2):
                        t.wait_ge(DV.h, ev_toks[pdo])
                        t.wait_ge(DV.h, dv_b2)
                        PEc.inc(t.transpose(nt_ps[:, 0:128],
                                            att_ev[pdo % 2][:, 0:128], ident[:]))
                        PEc.inc(t.transpose(nt_ps[:, 128:256],
                                            att_ev[pdo % 2][:, 128:256], ident[:]))
                    trtok = PEc.n

                    @block.vector
                    def _(v, pdo=pdo, trtok=trtok):
                        v.wait_ge(PEc.h, trtok)
                        DV.inc(v.tensor_copy(
                            out=att_nat[:, pdo * 128:pdo * 128 + 128],
                            in_=nt_ps[:, 0:128]))
                        DV.inc(v.tensor_copy(
                            out=att_nat[:, D + pdo * 128:D + pdo * 128 + 128],
                            in_=nt_ps[:, 128:256]))

                for do in range(16):
                    wsem = dsem["xa"] if do % 2 == 0 else dsem["xb"]

                    @block.sync
                    def _(s, do=do, wb=wo_busy[do % 2], wsem=wsem):
                        s.wait_ge(CC.h, ccA2)
                        s.wait_ge(PEc.h, wb)
                        r0 = _wallA2_o_rows(do)
                        s.dma_start(out=wo_sb[do % 2][:],
                                    in_=wallA2[r0:r0 + 128, :]).then_inc(wsem.h, 16)
                        wsem.n += 16
                    wo_tok = wsem.n

                    @block.tensor
                    def _(t, do=do, norm_tok=norm_tok, wo_tok=wo_tok, wsem=wsem,
                          evb=ev_toks[do - 2] if do >= 2 else 0):
                        t.wait_ge(DV.h, norm_tok)
                        t.wait_ge(wsem.h, wo_tok)
                        t.wait_ge(DV.h, evb)
                        inst = None
                        for k in range(16):
                            inst = t.matmul(wo_ps[do % 2],
                                            wo_sb[do % 2][:, k * 128:(k + 1) * 128],
                                            otn[:, k * QC:(k + 1) * QC],
                                            start=(k == 0), stop=(k == 15))
                        PEc.inc(inst)
                    wotok = PEc.n
                    wo_busy[do % 2] = PEc.n

                    @block.vector
                    def _(v, do=do, wotok=wotok):
                        v.wait_ge(PEc.h, wotok)
                        DV.inc(v.tensor_copy(out=att_ev[do % 2][:], in_=wo_ps[do % 2]))
                    ev_toks[do] = DV.n

                    if do >= 1:
                        emit_tr(do - 1)
                emit_tr(15)
                att_done = DV.n
                prev_att_done = att_done

                if qc + 2 < NQC:
                    ld_toks[qc + 2] = emit_loads(qc + 2, scores_done[qc])

                lnst = emit_ln_stats([att_nat[:, 0:D], att_nat[:, D:2 * D]],
                                     [(DV, att_done), (dsem["xr"], xr_tok)])
                for qsub in range(2):
                    a_ap = att_nat[:, qsub * D:(qsub + 1) * D]
                    atok = emit_ln_apply(a_ap, lnst[qsub], lntmp[:])
                    idx = qc * 2 + qsub

                    xsem = dsem["x1s"] if qsub % 2 == 0 else dsem["x1t"]

                    @block.vector
                    def _(v, qsub=qsub, idx=idx, atok=atok, xsem=xsem,
                          xb=x1sb_busy[qsub % 2], xa=x1sb_ac[qsub % 2]):
                        v.wait_ge(DV.h, atok)
                        v.wait_ge(xsem.h, xb)
                        v.wait_ge(AC.h, xa)
                        DV.inc(v.tensor_tensor(out=x1sb[qsub % 2][:],
                                               in0=x_res[:, qsub * D:(qsub + 1) * D],
                                               in1=lntmp[:], op=ALU.add))
                        v.wait_ge(DV.h, DV.n)
                        DV.inc(v.tensor_reduce(out=sv3[:, idx:idx + 1],
                                               in_=x1sb[qsub % 2][:],
                                               axis=AX, op=ALU.add))
                    x1tok = DV.n - 1

                    @block.scalar
                    def _(s, qc=qc, qsub=qsub, idx=idx, x1tok=x1tok):
                        s.wait_ge(DV.h, x1tok)
                        r0 = qc * QC + qsub * 128
                        s.dma_start(out=x1_d[r0:r0 + 128, :], in_=x1sb[qsub % 2][:]).then_inc(xsem.h, 16)
                        xsem.n += 16
                        s.wait_ge(AC.h, AC.n)
                        AC.inc(s.activation(out=sqscr[:], in_=x1sb[qsub % 2][:],
                                            func=AF.Square,
                                            accum_out=ssq3[:, idx:idx + 1]))
                    x1sb_busy[qsub % 2] = xsem.n
                    x1sb_ac[qsub % 2] = AC.n

        p2_spills = dsem["x1s"].n
        p2_spills_t = dsem["x1t"].n
        p2_stats_ac = AC.n

        with contextlib.ExitStack() as p3:
            x1c = p3.enter_context(nc.sbuf_tensor("x1c", [128, 4 * D], F32))
            xn2T = p3.enter_context(nc.sbuf_tensor("xn2T", [128, 16 * 512], BF16))
            Hsb = p3.enter_context(nc.sbuf_tensor("Hsb", [128, NHT * 512], BF16))
            fcb = [p3.enter_context(nc.sbuf_tensor(f"fcb{i}", [128, 2048], BF16))
                   for i in range(2)]
            prb = [p3.enter_context(nc.sbuf_tensor(f"prb{i}", [128, 8192], BF16))
                   for i in range(2)]
            qsb = [p3.enter_context(nc.sbuf_tensor(f"qsb{i}", [128, 512], BF16))
                   for i in range(2)]
            g2b = [p3.enter_context(nc.sbuf_tensor(f"g2b{i}", [128, 512], BF16))
                   for i in range(4)]
            rssb = p3.enter_context(nc.sbuf_tensor("rssb", [128, 512], F32))
            otb = [p3.enter_context(nc.sbuf_tensor(f"otb{i}", [128, 512], BF16))
                   for i in range(2)]
            fnat = p3.enter_context(nc.sbuf_tensor("fnat", [128, 4 * D], BF16))
            xnat3 = p3.enter_context(nc.sbuf_tensor("xnat3", [128, D], BF16))
            ysb = [p3.enter_context(nc.sbuf_tensor(f"ysb{i}", [128, D], F32))
                   for i in range(2)]
            tp3 = p3.enter_context(nc.psum_tensor("tp3", [128, 2048], BF16))
            hp_ps = [p3.enter_context(nc.psum_tensor(f"hp{i}", [128, 512], F32))
                     for i in range(2)]
            ms_ps = p3.enter_context(nc.psum_tensor("ms_ps", [128, 512], F32))
            op_ps = [p3.enter_context(nc.psum_tensor(f"op{i}", [128, 512], F32))
                     for i in range(2)]

            @block.vector
            def _(v, dv_stats=DV.n):
                v.wait_ge(AC.h, p2_stats_ac)
                v.wait_ge(DV.h, dv_stats)
                DV.inc(v.tensor_scalar(out=m3[:], in0=sv3[:], scalar1=1.0 / D,
                                       scalar2=None, op0=ALU.mult))
                DV.inc(v.tensor_scalar(out=r3[:], in0=ssq3[:], scalar1=1.0 / D,
                                       scalar2=None, op0=ALU.mult))
                v.wait_ge(DV.h, DV.n)
                DV.inc(v.tensor_tensor(out=sv3[:], in0=m3[:], in1=m3[:], op=ALU.mult))
                v.wait_ge(DV.h, DV.n)
                DV.inc(v.tensor_tensor(out=r3[:], in0=r3[:], in1=sv3[:], op=ALU.subtract))
            var3_tok = DV.n

            @block.scalar
            def _(s):
                s.wait_ge(DV.h, var3_tok)
                AC.inc(s.activation(out=r3[:], in_=r3[:], func=AF.Ln, bias=eps_ln[:]))
                s.wait_ge(AC.h, AC.n)
                AC.inc(s.activation(out=r3[:], in_=r3[:], func=AF.Exp, scale=-0.5))
            r3_tok = AC.n

            fcb_busy = [0, 0]
            prb_busy = [0, 0]
            qsb_ac = [0, 0]
            qsb_dv = [0, 0]
            g2b_busy = [0, 0, 0, 0]
            g2_toks = [0] * NHT
            otb_busy = [0, 0]
            ysb_busy = [0, 0]
            ms_read = 0
            xnat3_pe_busy = [0]

            for ch in range(4):
                pe_cs = PEc.n
                ac_cs = AC.n
                dv_cs = DV.n

                @block.sync
                def _(s, ch=ch, dv_cs=dv_cs):
                    s.wait_ge(dsem["x1s"].h, p2_spills)
                    s.wait_ge(dsem["x1t"].h, p2_spills_t)
                    s.wait_ge(DV.h, dv_cs)
                    s.dma_start(out=x1c[:].rearrange("p (a d) -> p a d", a=4),
                                in_=x1_d[ch * 512:(ch + 1) * 512, :]
                                .rearrange("(a p) d -> p a d", p=128)).then_inc(dsem["xc"].h, 16)
                    dsem["xc"].n += 16
                x1tok = dsem["xc"].n

                tp_free3 = DV.n
                for i4 in range(4):
                    idx = ch * 4 + i4

                    @block.vector
                    def _(v, i4=i4, idx=idx, x1tok=x1tok):
                        v.wait_ge(dsem["xc"].h, x1tok)
                        v.wait_ge(AC.h, max(r3_tok, ac_cs))
                        v.wait_ge(PEc.h, pe_cs)
                        DV.inc(v.tensor_scalar(out=fnat[:, i4 * D:(i4 + 1) * D],
                                               in0=x1c[:, i4 * D:(i4 + 1) * D],
                                               scalar1=m3[:, idx:idx + 1],
                                               scalar2=r3[:, idx:idx + 1],
                                               op0=ALU.subtract, op1=ALU.mult))
                    atok = DV.n

                    @block.tensor
                    def _(t, i4=i4, atok=atok, tpf=tp_free3):
                        t.wait_ge(DV.h, atok)
                        t.wait_ge(DV.h, tpf)
                        inst = None
                        for k in range(16):
                            inst = t.transpose(tp3[:, k * 128:(k + 1) * 128],
                                               fnat[:, i4 * D + k * 128:i4 * D + (k + 1) * 128],
                                               ident[:])
                        PEc.inc(inst)
                    ttok = PEc.n
                    xnat3_pe_busy[0] = ttok

                    @block.vector
                    def _(v, i4=i4, ttok=ttok, pe_cs=pe_cs):
                        v.wait_ge(PEc.h, ttok)
                        v.wait_ge(PEc.h, pe_cs)
                        inst = None
                        for k in range(16):
                            inst = v.tensor_copy(
                                out=xn2T[:, k * 512 + i4 * 128:k * 512 + (i4 + 1) * 128],
                                in_=tp3[:, k * 128:(k + 1) * 128])
                        DV.inc(inst)
                    tp_free3 = DV.n
                xn2_tok = DV.n

                AC3 = AC.n
                htoks = [0] * NHT
                qdv_guards = [0] * NHT
                for ht in range(NHT):
                    hb = ht % 2

                    @block.sync
                    def _(s, ht=ht, hb=hb, fb=fcb_busy[hb]):
                        s.wait_ge(CC.h, ccB)
                        s.wait_ge(PEc.h, fb)
                        r0 = _wallB_fc_rows(ht)
                        s.dma_start(out=fcb[hb][:], in_=wallB[r0:r0 + 128, :]).then_inc(dsem[f"f{hb}"].h, 16)
                        dsem[f"f{hb}"].n += 16
                    ftok = dsem[f"f{hb}"].n

                    ab_pre = AC3 + ht - 1 if ht >= 2 else AC3

                    @block.tensor
                    def _(t, hb=hb, ftok=ftok, xn2_tok=xn2_tok, ab=ab_pre, mr=ms_read):
                        t.wait_ge(dsem[f"f{hb}"].h, ftok)
                        t.wait_ge(DV.h, xn2_tok)
                        t.wait_ge(AC.h, ab)
                        t.wait_ge(AC.h, mr)
                        inst = None
                        for k in range(16):
                            inst = t.matmul(hp_ps[hb][:],
                                            fcb[hb][:, k * 128:(k + 1) * 128],
                                            xn2T[:, k * 512:(k + 1) * 512],
                                            start=(k == 0), stop=(k == 15))
                        PEc.inc(inst)
                    fcb_busy[hb] = PEc.n
                    htoks[ht] = PEc.n
                    qtok = AC3 + ht + 1

                    gq = ht % 4

                    @block.vector
                    def _(v, ht=ht, hb=hb, gq=gq, qtok=qtok, gb=g2b_busy[gq], pe_cs=pe_cs):
                        v.wait_ge(AC.h, qtok)
                        v.wait_ge(PEc.h, pe_cs)
                        DV.inc(v.tensor_tensor(out=Hsb[:, ht * 512:(ht + 1) * 512],
                                               in0=qsb[hb][:], in1=qsb[hb][:], op=ALU.mult))
                        qsb_dv[hb] = DV.n
                        v.wait_ge(DV.h, DV.n)
                        v.wait_ge(PEc.h, gb)
                        DV.inc(v.tensor_tensor(out=g2b[gq][:],
                                               in0=Hsb[:, ht * 512:(ht + 1) * 512],
                                               in1=Hsb[:, ht * 512:(ht + 1) * 512],
                                               op=ALU.mult))
                    g2_toks[ht] = DV.n
                    qdv_guards[ht] = qsb_dv[hb]

                    if ht >= 2:
                        hd = ht - 2

                        @block.tensor
                        def _(t, hd=hd):
                            t.wait_ge(DV.h, g2_toks[hd])
                            PEc.inc(t.matmul(ms_ps[:], ones128[:], g2b[hd % 4][:],
                                             start=(hd == 0), stop=False))
                        g2b_busy[hd % 4] = PEc.n
                for hd in (NHT - 2, NHT - 1):
                    @block.tensor
                    def _(t, hd=hd):
                        t.wait_ge(DV.h, g2_toks[hd])
                        PEc.inc(t.matmul(ms_ps[:], ones128[:], g2b[hd % 4][:],
                                         start=False, stop=(hd == NHT - 1)))
                    g2b_busy[hd % 4] = PEc.n
                ms_tok = PEc.n

                @block.scalar
                def _(s, AC3=AC3):
                    assert AC.n == AC3
                    for ht in range(NHT):
                        hb = ht % 2
                        s.wait_ge(PEc.h, htoks[ht])
                        if ht >= 2:
                            s.wait_ge(DV.h, qdv_guards[ht - 2])
                        AC.inc(s.activation(out=qsb[hb][:], in_=hp_ps[hb][:],
                                            func=AF.Relu))

                @block.scalar
                def _(s, ms_tok=ms_tok, dv_cs=dv_cs):
                    s.wait_ge(PEc.h, ms_tok)
                    s.wait_ge(DV.h, dv_cs)
                    AC.inc(s.activation(out=rssb[:], in_=ms_ps[:], func=AF.Ln,
                                        bias=eps_rms[:], scale=1.0 / HID))
                    s.wait_ge(AC.h, AC.n)
                    AC.inc(s.activation(out=rssb[:], in_=rssb[:], func=AF.Exp, scale=-0.5))
                rs_tok = AC.n
                ms_read = AC.n

                h_tok = DV.n
                for do in range(16):
                    db = do % 2

                    @block.sync
                    def _(s, do=do, db=db, pbz=prb_busy[db]):
                        s.wait_ge(CC.h, ccB)
                        s.wait_ge(PEc.h, pbz)
                        r0 = _wallB_pr_rows(do)
                        s.dma_start(out=prb[db][:],
                                    in_=wallB[r0:r0 + 512, :]
                                    .rearrange("(p a) c -> p (a c)", p=128)).then_inc(dsem[f"pr{db}"].h, 16)
                        dsem[f"pr{db}"].n += 16
                    ptok = dsem[f"pr{db}"].n

                    @block.tensor
                    def _(t, db=db, ptok=ptok, h_tok=h_tok, ob=otb_busy[db]):
                        t.wait_ge(dsem[f"pr{db}"].h, ptok)
                        t.wait_ge(DV.h, h_tok)
                        t.wait_ge(DV.h, ob)
                        inst = None
                        for hc in range(NHT):
                            inst = t.matmul(op_ps[db][:],
                                            prb[db][:, hc * 128:(hc + 1) * 128],
                                            Hsb[:, hc * 512:(hc + 1) * 512],
                                            start=(hc == 0), stop=(hc == NHT - 1))
                        PEc.inc(inst)
                    prb_busy[db] = PEc.n
                    optok = PEc.n

                    @block.vector
                    def _(v, db=db, optok=optok, rs_tok=rs_tok, pe_ob=otb_busy[db]):
                        v.wait_ge(PEc.h, optok)
                        v.wait_ge(AC.h, rs_tok)
                        v.wait_ge(PEc.h, pe_ob)
                        DV.inc(v.tensor_tensor(out=otb[db][:], in0=op_ps[db][:],
                                               in1=rssb[:], op=ALU.mult))
                    ottok = DV.n

                    dv_b3 = DV.n

                    @block.tensor
                    def _(t, db=db, ottok=ottok, dv_b3=dv_b3):
                        t.wait_ge(DV.h, ottok)
                        t.wait_ge(DV.h, dv_b3)
                        inst = None
                        for i4 in range(4):
                            inst = t.transpose(tp3[:, i4 * 128:(i4 + 1) * 128],
                                               otb[db][:, i4 * 128:(i4 + 1) * 128],
                                               ident[:])
                        PEc.inc(inst)
                    otb_busy[db] = PEc.n
                    trtok = PEc.n

                    @block.vector
                    def _(v, do=do, trtok=trtok):
                        v.wait_ge(PEc.h, trtok)
                        inst = None
                        for i4 in range(4):
                            inst = v.tensor_copy(
                                out=fnat[:, i4 * D + do * 128:i4 * D + (do + 1) * 128],
                                in_=tp3[:, i4 * 128:(i4 + 1) * 128])
                        DV.inc(inst)
                fn_tok = DV.n

                lnst3 = emit_ln_stats([fnat[:, i * D:(i + 1) * D] for i in range(4)],
                                      [(DV, fn_tok)])
                for i4 in range(4):
                    yb = i4 % 2
                    atok = emit_ln_apply(fnat[:, i4 * D:(i4 + 1) * D], lnst3[i4], xnat3[:])

                    @block.vector
                    def _(v, i4=i4, yb=yb, atok=atok, ybb=ysb_busy[yb]):
                        v.wait_ge(DV.h, atok)
                        v.wait_ge(dsem[f"y{yb}"].h, ybb)
                        DV.inc(v.tensor_tensor(out=ysb[yb][:],
                                               in0=x1c[:, i4 * D:(i4 + 1) * D],
                                               in1=xnat3[:], op=ALU.add))
                    ytok = DV.n

                    @block.scalar
                    def _(s, ch=ch, i4=i4, yb=yb, ytok=ytok):
                        s.wait_ge(DV.h, ytok)
                        r0 = ch * 512 + i4 * 128
                        s.dma_start(out=y[r0:r0 + 128, :], in_=ysb[yb][:]).then_inc(dsem[f"y{yb}"].h, 16)
                        dsem[f"y{yb}"].n += 16
                    ysb_busy[yb] = dsem[f"y{yb}"].n

        y0_done, y1_done = dsem["y0"].n, dsem["y1"].n

        @block.sync
        def _(s):
            s.wait_ge(dsem["y0"].h, y0_done)
            s.wait_ge(dsem["y1"].h, y1_done)

    return nc


_NC_CACHE = {}


def kernel(L0_attn_x, L0_attn_Wq, L0_attn_Wk, L0_attn_Wv, L0_attn_Wo,
           L0_ffn_W_fc, L0_ffn_W_proj):
    in_maps = _host_prep(L0_attn_x, L0_attn_Wq, L0_attn_Wk, L0_attn_Wv,
                         L0_attn_Wo, L0_ffn_W_fc, L0_ffn_W_proj)
    if "nc" not in _NC_CACHE:
        _NC_CACHE["nc"] = build_nc()
    nc = _NC_CACHE["nc"]
    res = run_bass_kernel_spmd(nc, in_maps, core_ids=list(range(NCORES)))
    out = np.empty((4, 4096, 2048), dtype=np.float32)
    for c in range(NCORES):
        b, h = c // 2, c % 2
        out[b, h * S_CORE:(h + 1) * S_CORE, :] = res.results[c]["y"]
    return out
